# version 09
# baseline (speedup 1.0000x reference)
import numpy as np
import concourse.bass as bass
import concourse.mybir as mybir
from concourse.bass_utils import run_bass_kernel_spmd

F32 = mybir.dt.float32
BF16 = mybir.dt.bfloat16
AF = mybir.ActivationFunctionType
ALU = mybir.AluOpType

D = 1024
CH = 8
H = 16
PAIRS = 8
S = 4096
OWN = 2049
FF = 4096
FFT = 32
EPS_LN = 1e-5
EPS_Z = 1e-6

AG = [(1 + 256 * g, 256) for g in range(16)]
CG = [(0, 385), (385, 416), (801, 416), (1217, 416), (1633, 416)]
C2G = [(0, 257)] + [(257 + 256 * i, 256) for i in range(7)]
DG = [(1, 410), (411, 410), (821, 410), (1231, 410), (1641, 408)]

ENGS = ("PE", "ACT", "DVE", "POOL", "SP")


class Res:
    __slots__ = ("w", "r")

    def __init__(self):
        self.w = None
        self.r = []

    def rdep(self):
        return [self.w] if self.w else []

    def wdep(self):
        d = list(self.r)
        if self.w:
            d.append(self.w)
        return d


class Builder:
    def __init__(self):
        self.ops = []
        self.cnt = {}
        self.gate = []

    def st(self, eng, emits, reads=(), writes=(), extra=(), dma=False, tag="",
           dsem=None):
        if not isinstance(emits, (list, tuple)):
            emits = [emits]
        deps = [d for d in extra if d is not None] + list(self.gate)
        for r in reads:
            deps += r.rdep()
        for r in writes:
            deps += r.wdep()
        if dma:
            sem = dsem or {"SP": "qSP", "ACT": "qACT", "POOL": "qGP"}[eng]
            amt = 16 * len(emits)
        else:
            sem = "s" + eng
            amt = 1
        own = None if dma else sem
        waits = {}
        for s, v in deps:
            if s == own:
                continue
            if waits.get(s, -1) < v:
                waits[s] = v
        cur = self.cnt.get(sem, 0) + amt
        self.cnt[sem] = cur
        self.ops.append((eng, waits, emits, sem, dma, tag))
        h = (sem, cur)
        for r in reads:
            r.r.append(h)
        for r in writes:
            r.w = h
            r.r = []
        return h


def build_program(taps=0, upto=None):
    nc = bass.Bass("TRN2", target_bir_lowering=False, debug=False)
    B = Builder()
    st = B.st

    xT = nc.dram_tensor("xT", [D, S + 1], F32, kind="ExternalInput").ap()
    xTo = nc.dram_tensor("xTo", [D, OWN], F32, kind="ExternalInput").ap()
    wdr = {}
    for nm, shp in (("wq", [D, D]), ("wk", [D, D]), ("wv", [D, D]), ("wo", [D, D]),
                    ("w1", [D, FF]), ("w2", [FF, D]),
                    ("wr", [D, D]), ("wk2", [D, D]), ("wv2", [D, D])):
        wdr[nm] = nc.dram_tensor(nm, shp, F32, kind="ExternalInput").ap()
    vnames = ("bq", "bo", "b2", "g1", "be1", "g2", "be2", "mur", "muk", "cmask")
    vd = {nm: nc.dram_tensor(nm, [128, CH], F32, kind="ExternalInput").ap() for nm in vnames}
    b1c_d = nc.dram_tensor("b1c", [128, FFT], F32, kind="ExternalInput").ap()
    bk_row_d = nc.dram_tensor("bkrow", [1, D], F32, kind="ExternalInput").ap()
    bv_row_d = nc.dram_tensor("bvrow", [1, D], F32, kind="ExternalInput").ap()
    zpat_d = nc.dram_tensor("zpat", [32, PAIRS, 128], F32, kind="ExternalInput").ap()
    ident_d = nc.dram_tensor("ident", [128, 128], F32, kind="ExternalInput").ap()
    x1f_d = nc.dram_tensor("x1f_d", [D, OWN], F32, kind="Internal").ap()
    x1h_d = nc.dram_tensor("x1h_d", [D, OWN], BF16, kind="Internal").ap()
    a_d = nc.dram_tensor("a_d", [D, OWN], F32, kind="Internal").ap()
    h1_d = nc.dram_tensor("h1_d", [FF, OWN], BF16, kind="Internal").ap()
    outT = nc.dram_tensor("outT", [D, S // 2], F32, kind="ExternalOutput").ap()
    tap_ap = {}
    if taps >= 1:
        tap_ap["rl"] = nc.dram_tensor("t_rl", [4, 512], F32, kind="ExternalOutput").ap()
        tap_ap["rsb"] = nc.dram_tensor("t_rsb", [2, 128, 512], F32, kind="ExternalOutput").ap()
        tap_ap["ln0h"] = nc.dram_tensor("t_ln0h", [128, CH, 512], F32, kind="ExternalOutput").ap()
        tap_ap["K"] = nc.dram_tensor("t_K", [128, D], F32, kind="ExternalOutput").ap()
        tap_ap["V"] = nc.dram_tensor("t_V", [128, H * 65], F32, kind="ExternalOutput").ap()
        tap_ap["kf"] = nc.dram_tensor("t_kf", [128, D], F32, kind="ExternalOutput").ap()
        tap_ap["Q"] = nc.dram_tensor("t_Q", [128, CH, 448], F32, kind="ExternalOutput").ap()
        tap_ap["z"] = nc.dram_tensor("t_z", [128, 4, 16], F32, kind="ExternalOutput").ap()
        tap_ap["o"] = nc.dram_tensor("t_o", [128, CH, 448], F32, kind="ExternalOutput").ap()
        tap_ap["u"] = nc.dram_tensor("t_u", [128, CH, 448], F32, kind="ExternalOutput").ap()
        tap_ap["zb"] = nc.dram_tensor("t_zb", [2, 128, 448], F32, kind="ExternalOutput").ap()
        tap_ap["zf"] = nc.dram_tensor("t_zf", [32, 448], F32, kind="ExternalOutput").ap()
        tap_ap["oun"] = nc.dram_tensor("t_oun", [128, 448], F32, kind="ExternalOutput").ap()
        tap_ap["u2"] = nc.dram_tensor("t_u2", [128, CH, 448], F32, kind="ExternalOutput").ap()
        tap_ap["l0"] = nc.dram_tensor("t_l0", [128, CH, 448], F32, kind="ExternalOutput").ap()
        tap_ap["zb2"] = nc.dram_tensor("t_zb2", [128, PAIRS, 448], F32, kind="ExternalOutput").ap()
        tap_ap["kvbz"] = nc.dram_tensor("t_kvbz", [128, PAIRS * 2 * 2 * 64], F32,
                                        kind="ExternalOutput").ap()
        tap_ap["rows"] = nc.dram_tensor("t_rows", [4, S + 1], F32, kind="ExternalOutput").ap()
        tap_ap["kvb"] = nc.dram_tensor("t_kvb", [128, PAIRS * 65], F32, kind="ExternalOutput").ap()
        tap_ap["x1"] = nc.dram_tensor("t_x1", [D, OWN], F32, kind="ExternalOutput").ap()
        tap_ap["a"] = nc.dram_tensor("t_a", [D, OWN], F32, kind="ExternalOutput").ap()

    guards = []

    def SB(name, shape, dt):
        g = nc.sbuf_tensor(name, shape, dt)
        t = g.__enter__()
        guards.append(g)
        return t, Res()

    vec = {nm: SB("v_" + nm, [128, CH], F32) for nm in vnames + ("bq1", "bqn")}
    b1_sb = SB("v_b1", [128, FFT], F32)
    onescol = SB("onescol", [128, 1], BF16)
    zercol = SB("zercol", [128, 64], BF16)
    ones1 = SB("ones1", [1, 128], BF16)
    rw = [{k: SB("rw%d_%s" % (s, k), [128, 512], F32) for k in ("mu", "q", "t")}
          for s in range(2)]
    rw.append(rw[1])
    ones128 = SB("ones128", [128, 128], BF16)
    ones128f = SB("ones128f", [128, 128], F32)
    constvals = (1e-5, 1e-6, -0.5, -1.0)
    ccell = SB("ccell", [128, len(constvals)], F32)

    wqg = []

    def WQSB(name, shape, dt):
        g = nc.sbuf_tensor(name, shape, dt)
        t = g.__enter__()
        wqg.append(g)
        return t, Res()

    wq_sb = WQSB("wq_sb", [128, CH, D], BF16)
    wo_sb = WQSB("wo_sb", [128, CH, D], BF16)
    wqr = [Res() for _ in range(4)]
    wor = [Res() for _ in range(4)]

    kvg = []

    def KVSB(name, shape, dt):
        g = nc.sbuf_tensor(name, shape, dt)
        t = g.__enter__()
        kvg.append(g)
        return t, Res()

    zpat_sb = KVSB("zpat_s", [32, PAIRS, 128], BF16)
    ident_sb = KVSB("ident_s", [128, 128], BF16)
    kvbz = KVSB("kvbz", [128, PAIRS, 2, 2, 64], BF16)
    zrhs = KVSB("zrhs", [128, PAIRS, 4], BF16)

    PS = []
    for i in range(7):
        g = nc.psum_tensor("ps%d" % i, [128, 512], F32)
        PS.append((g.__enter__(), Res()))
        guards.append(g)
    g = nc.psum_tensor("psb", [128, 1024], BF16)
    PSB = (g.__enter__(), Res())
    guards.append(g)

    def bcast2(bi, hi_res, hi_ap, lo_ap, n, extra=(), tag="bc"):
        pb, rb = PS[bi]
        return st("PE", [
            lambda e: e.matmul(pb[:, 0:n], ones1[0][:, :], hi_ap, start=True, stop=False,
                               skip_group_check=True),
            lambda e: e.matmul(pb[:, 0:n], ones1[0][:, :], lo_ap, start=False, stop=True,
                               skip_group_check=True),
        ], reads=[ones1[1], hi_res], writes=[rb], extra=extra, tag=tag)

    def stat_mms(bi_s, bi_q, val_ap, sq_ap, val_res, sq_res, n, extra=(), tag="st",
                 off_s=0, off_q=0, ones_val=None):
        pbs, rbs = PS[bi_s]
        pbq, rbq = PS[bi_q]
        packed = bi_s == bi_q
        ov = ones_val if ones_val is not None else ones128
        ems = []
        for c in range(CH):
            ems.append(lambda e, c=c: e.matmul(
                pbs[:, off_s:off_s + n], ov[0][:, :], val_ap(c), start=(c == 0),
                stop=(c == CH - 1), skip_group_check=True))
        for c in range(CH):
            ems.append(lambda e, c=c: e.matmul(
                pbq[:, off_q:off_q + n], ones128[0][:, :], sq_ap(c),
                start=(c == 0 and not packed),
                stop=(c == CH - 1), skip_group_check=True))
        return st("PE", ems, reads=[ones128[1], val_res, sq_res], writes=[rbs, rbq],
                  extra=extra, tag=tag)

    def row_math(bi_s, bi_q, n, rsb, msb, si=0, eps=EPS_LN, tag="rm",
                 off_s=0, off_q=0):
        rws = rw[si]
        pbs, rbs = PS[bi_s]
        pbq, rbq = PS[bi_q]
        st("DVE", [
            lambda e: e.tensor_scalar_mul(rws["mu"][0][:, 0:n],
                                          pbs[:, off_s:off_s + n], 1.0 / D),
            lambda e: e.tensor_scalar_mul(rws["q"][0][:, 0:n],
                                          pbq[:, off_q:off_q + n], 1.0 / D),
            lambda e: e.tensor_mul(rws["t"][0][:, 0:n], rws["mu"][0][:, 0:n],
                                   rws["mu"][0][:, 0:n]),
            lambda e: e.tensor_sub(rws["q"][0][:, 0:n], rws["q"][0][:, 0:n],
                                   rws["t"][0][:, 0:n]),
        ], reads=[rbs, rbq], writes=[rws["mu"][1], rws["q"][1], rws["t"][1]],
            tag=tag + "1")
        st("ACT", [
            lambda e: e.activation(rws["q"][0][:, 0:n], rws["q"][0][:, 0:n], AF.Ln,
                                   bias=float(eps)),
            lambda e: e.activation(rsb[0][:, 0:n], rws["q"][0][:, 0:n], AF.Exp,
                                   scale=-0.5),
        ], reads=[rws["q"][1]], writes=[rws["q"][1], rsb[1]], tag=tag + "2")
        h3 = st("DVE", [
            lambda e: e.tensor_mul(msb[0][:, 0:n], rws["mu"][0][:, 0:n],
                                   rsb[0][:, 0:n]),
        ], reads=[rws["mu"][1], rsb[1]], writes=[msb[1]], tag=tag + "3")
        return h3

    def ln0_group(src_dram, w0, n, xhb, xqb, bs, bq, rsb, msb, tag, si=0,
                  off_s=0, off_q=0):
        st("POOL", lambda e, w0=w0, n=n: e.dma_start(
            out=xhb[0][:, :, 0:n],
            in_=src_dram[:, w0:w0 + n].rearrange("(c p) n -> p c n", p=128)),
            writes=[xhb[1]], dma=True, tag=tag + "lh")
        st("ACT", lambda e, n=n: e.activation(
            xqb[0][:, :, 0:n], xhb[0][:, :, 0:n], AF.Square),
            reads=[xhb[1]], writes=[xqb[1]], tag=tag + "sq")
        stat_mms(bs, bq, lambda c, n=n: xhb[0][:, c, 0:n],
                 lambda c, n=n: xqb[0][:, c, 0:n], xhb[1], xqb[1], n, tag=tag + "sm",
                 off_s=off_s, off_q=off_q)
        row_math(bs, bq, n, rsb, msb, si=si, tag=tag + "rm", off_s=off_s, off_q=off_q)

    h_const = st("DVE", [
        (lambda e, i=i, v=v: e.memset(ccell[0][:, i:i + 1], float(v)))
        for i, v in enumerate(constvals)
    ], writes=[ccell[1]], tag="consts")
    for i, v in enumerate(constvals):
        nc.const_aps.register(F32, float(v), ccell[0][:, i:i + 1])
    B.gate = [h_const]

    p0g = []

    def P0SB(name, shape, dt):
        g = nc.sbuf_tensor(name, shape, dt)
        t = g.__enter__()
        p0g.append(g)
        return t, Res()

    bkrow = P0SB("bkrow_s", [1, D], F32)
    bvrow = P0SB("bvrow_s", [1, D], F32)
    rowh = P0SB("rowh", [1, D], BF16)
    rowl = P0SB("rowl", [1, D], BF16)
    bkb = P0SB("bkb", [128, D], F32)
    bvb = P0SB("bvb", [128, D], F32)

    st("SP", [
        (lambda e, nm=nm: e.dma_start(out=vec[nm][0][:, :], in_=vd[nm]))
        for nm in vnames
    ] + [
        lambda e: e.dma_start(out=b1_sb[0][:, :], in_=b1c_d),
        lambda e: e.dma_start(out=bkrow[0][:, :], in_=bk_row_d),
        lambda e: e.dma_start(out=bvrow[0][:, :], in_=bv_row_d),
    ], writes=[vec[nm][1] for nm in vnames] + [b1_sb[1], bkrow[1], bvrow[1]],
        dma=True, tag="p0vec")
    st("POOL", [
        lambda e: e.dma_start(out=zpat_sb[0][:, :, :], in_=zpat_d),
        lambda e: e.dma_start(out=ident_sb[0][:, :], in_=ident_d),
    ], writes=[zpat_sb[1], ident_sb[1]], dma=True, tag="p0zp")
    st("DVE", [
        lambda e: e.memset(onescol[0][:, :], 1.0),
        lambda e: e.memset(zercol[0][:, :], 0.0),
        lambda e: e.memset(ones1[0][:, :], 1.0),
        lambda e: e.memset(ones128[0][:, :], 1.0),
        lambda e: e.memset(ones128f[0][:, :], 1.0),
    ], writes=[onescol[1], zercol[1], ones1[1], ones128[1], ones128f[1]],
        tag="p0ones")
    st("DVE", [
        lambda e: e.tensor_scalar_add(vec["bq1"][0][:, :], vec["bq"][0][:, :], 1.0),
        lambda e: e.tensor_scalar_mul(vec["bqn"][0][:, :], vec["bq"][0][:, :], -1.0),
    ], reads=[vec["bq"][1]], writes=[vec["bq1"][1], vec["bqn"][1]], tag="p0bq1")
    for src_row, dstt in ((bkrow, bkb), (bvrow, bvb)):
        st("DVE", [
            lambda e, src_row=src_row: e.tensor_copy(rowh[0][:, :], src_row[0][:, :]),
            lambda e, src_row=src_row: e.tensor_sub(rowl[0][:, :], src_row[0][:, :],
                                                    rowh[0][:, :]),
        ], reads=[src_row[1]], writes=[rowh[1], rowl[1]], tag="p0r")
        for half in range(2):
            bcast2(6, rowh[1], rowh[0][:, 512 * half:512 * (half + 1)],
                   rowl[0][:, 512 * half:512 * (half + 1)], 512,
                   extra=rowl[1].rdep(), tag="p0bc")
            st("DVE", lambda e, dstt=dstt, half=half: e.tensor_copy(
                dstt[0][:, 512 * half:512 * (half + 1)], PS[6][0][:, 0:512]),
                reads=[PS[6][1]], writes=[dstt[1]], tag="p0bcc")

    a1g = []

    def A1SB(name, shape, dt):
        g = nc.sbuf_tensor(name, shape, dt)
        t = g.__enter__()
        a1g.append(g)
        return t, Res()

    wk_sb = A1SB("wk_sb", [128, CH, D], BF16)
    wv_sb = A1SB("wv_sb", [128, CH, D], BF16)
    xh1 = [A1SB("a1xh%d" % i, [128, CH, 256], BF16) for i in range(2)]
    xq1 = [A1SB("a1xq%d" % i, [128, CH, 256], BF16) for i in range(2)]
    ln0h1 = [A1SB("a1ln%d" % i, [128, CH, 256], BF16) for i in range(2)]
    ln0h1b = [Res() for _ in range(2)]
    scr_a = A1SB("a1scr", [128, CH, 256], F32)
    scr_ab = Res()
    rsb1 = [A1SB("a1rsb%d" % i, [128, 256], F32) for i in range(2)]
    msb1 = [A1SB("a1msb%d" % i, [128, 256], F32) for i in range(2)]
    kf1 = A1SB("a1kf", [128, D], F32)
    km1 = A1SB("a1km", [128, D], F32)
    ke1 = A1SB("a1ke", [128, D], F32)
    Kt1 = [A1SB("a1K%d" % i, [128, D], BF16) for i in range(2)]
    Vt1 = [A1SB("a1V%d" % i, [128, H, 65], BF16) for i in range(2)]
    kvf = A1SB("kvf", [128, PAIRS, 65], F32)
    kvb = A1SB("kvb", [128, PAIRS, 2, 65], BF16)

    st("POOL", [lambda e: e.dma_start(out=wk_sb[0][:, :, :],
                 in_=wdr["wk"].rearrange("(c p) d -> p c d", p=128))],
       writes=[wk_sb[1]], dma=True, dsem="qWA", tag="a1wk")
    st("POOL", [lambda e: e.dma_start(out=wv_sb[0][:, :, :],
                 in_=wdr["wv"].rearrange("(c p) d -> p c d", p=128))],
       writes=[wv_sb[1]], dma=True, dsem="qWA", tag="a1wv")
    st("DVE", [lambda e, i=i: e.memset(Vt1[i][0][:, :, 64:65], 1.0) for i in range(2)],
       writes=[Vt1[0][1], Vt1[1][1]], tag="a1vone")

    ems = []
    for b in (4, 5):
        for cg in (0, 64):
            ems.append(lambda e, b=b, cg=cg: e.matmul(
                PS[b][0][cg:cg + 64, 0:260], zercol[0][:, :], wk_sb[0][:, 0, 0:260],
                start=True, stop=True, skip_group_check=True, tile_position=(0, cg)))
    st("PE", ems, reads=[zercol[1], wk_sb[1]], writes=[PS[4][1], PS[5][1]], tag="a1kvz")

    def a1_head(gi):
        w0, n = AG[gi]
        bi = gi % 2
        ln0_group(xT, w0, n, xh1[bi], xq1[bi], 6, 6, rsb1[bi], msb1[bi],
                  tag="a1s%d" % gi, si=0, off_s=0, off_q=256)
        for eng, c0, c1, wrs in (("DVE", 0, 5, (scr_a[1], ln0h1[bi][1])),
                                 ("POOL", 5, CH, (scr_ab, ln0h1b[bi]))):
            ems = []
            for c in range(c0, c1):
                ems.append(lambda e, bi=bi, c=c, n=n: e.tensor_mul(
                    scr_a[0][:, c, 0:n], xh1[bi][0][:, c, 0:n], rsb1[bi][0][:, 0:n]))
                ems.append(lambda e, bi=bi, c=c, n=n: e.tensor_sub(
                    ln0h1[bi][0][:, c, 0:n], scr_a[0][:, c, 0:n], msb1[bi][0][:, 0:n]))
            st(eng, ems, reads=[xh1[bi][1], rsb1[bi][1], msb1[bi][1]],
               writes=list(wrs), tag="a1ln%s%d" % (eng, gi))

    def a1_main(gi):
        w0, n = AG[gi]
        bi = gi % 2
        for tt in range(2):
            t0 = tt * 128
            ti = tt % 2
            ems = []
            for c in range(CH):
                for dh in range(2):
                    ems.append(lambda e, c=c, dh=dh, bi=bi, t0=t0: e.matmul(
                        PS[dh][0][:, 0:512], ln0h1[bi][0][:, c, t0:t0 + 128],
                        wk_sb[0][:, c, dh * 512:(dh + 1) * 512],
                        start=(c == 0), stop=(c == CH - 1), skip_group_check=True))
            st("PE", ems, reads=[ln0h1[bi][1], ln0h1b[bi], wk_sb[1], wv_sb[1]],
               writes=[PS[0][1], PS[1][1]], tag="a1km%d_%d" % (gi, tt))
            ems = []
            for c in range(CH):
                for dh in range(2):
                    ems.append(lambda e, c=c, dh=dh, bi=bi, t0=t0: e.matmul(
                        PS[2 + dh][0][:, 0:512], ln0h1[bi][0][:, c, t0:t0 + 128],
                        wv_sb[0][:, c, dh * 512:(dh + 1) * 512],
                        start=(c == 0), stop=(c == CH - 1), skip_group_check=True))
            st("PE", ems, reads=[ln0h1[bi][1], ln0h1b[bi], wv_sb[1], wk_sb[1]],
               writes=[PS[2][1], PS[3][1]], tag="a1vm%d_%d" % (gi, tt))
            st("DVE", [
                lambda e, dh=dh: e.tensor_add(kf1[0][:, dh * 512:(dh + 1) * 512],
                                              PS[dh][0][:, 0:512],
                                              bkb[0][:, dh * 512:(dh + 1) * 512])
                for dh in range(2)
            ], reads=[PS[0][1], PS[1][1], bkb[1]], writes=[kf1[1]], tag="a1kf%d_%d" % (gi, tt))
            st("DVE", [lambda e: e.tensor_scalar_min(km1[0][:, :], kf1[0][:, :], 0.0)],
               reads=[kf1[1]], writes=[km1[1]], tag="a1km2%d_%d" % (gi, tt))
            st("ACT", [lambda e: e.activation(ke1[0][:, :], km1[0][:, :], AF.Exp)],
               reads=[km1[1]], writes=[ke1[1]], tag="a1ke%d_%d" % (gi, tt))
            st("DVE", [lambda e, ti=ti: e.scalar_tensor_tensor(
                Kt1[ti][0][:, :], kf1[0][:, :], 1.0, ke1[0][:, :], ALU.add, ALU.max)],
                reads=[kf1[1], ke1[1]], writes=[Kt1[ti][1]], tag="a1K%d_%d" % (gi, tt))
            if taps == 2 and gi == 0 and tt == 0:
                st("POOL", [
                    lambda e: e.dma_start(out=tap_ap["K"], in_=Kt1[0][0][:, :]),
                    lambda e: e.dma_start(out=tap_ap["kf"], in_=kf1[0][:, :]),
                ], reads=[Kt1[0][1], kf1[1]], dma=True, tag="tapa1K")
            st("DVE", [
                lambda e, dh=dh, ti=ti: e.tensor_add(
                    Vt1[ti][0][:, dh * 8:(dh + 1) * 8, 0:64],
                    PS[2 + dh][0][:, 0:512].rearrange("p (h m) -> p h m", h=8),
                    bvb[0][:, dh * 512:(dh + 1) * 512].rearrange("p (h m) -> p h m", h=8))
                for dh in range(2)
            ], reads=[PS[2][1], PS[3][1], bvb[1]], writes=[Vt1[ti][1]],
                tag="a1V%d_%d" % (gi, tt))
            ems = []
            for p in range(PAIRS):
                bidx = 4 + p // 4
                off = (p % 4) * 65
                last_kv = (gi == len(AG) - 1 and tt == 1)
                for hh in range(2):
                    hd = 2 * p + hh
                    ems.append(lambda e, bidx=bidx, off=off, hd=hd, hh=hh, ti=ti,
                               last_kv=last_kv:
                               e.matmul(PS[bidx][0][64 * hh:64 * (hh + 1), off:off + 65],
                                        Kt1[ti][0][:, 64 * hd:64 * (hd + 1)],
                                        Vt1[ti][0][:, hd, :],
                                        start=False, stop=last_kv, skip_group_check=True,
                                        tile_position=(0, 64 * hh)))
            st("PE", ems, reads=[Kt1[ti][1], Vt1[ti][1]], writes=[PS[4][1], PS[5][1]],
               tag="a1kv%d_%d" % (gi, tt))

    a1_head(0)
    a1_head(1)
    for j in range(4):
        st("POOL", [lambda e, j=j: e.dma_start(
            out=wq_sb[0][:, :, j * 256:(j + 1) * 256],
            in_=wdr["wq"][:, j * 256:(j + 1) * 256].rearrange(
                "(c p) d -> p c d", p=128))],
            writes=[wqr[j]], dma=True, dsem="qWB", tag="pfwq%d" % j)
    for j in range(4):
        st("POOL", [lambda e, j=j: e.dma_start(
            out=wo_sb[0][:, :, j * 256:(j + 1) * 256],
            in_=wdr["wo"][:, j * 256:(j + 1) * 256].rearrange(
                "(c p) d -> p c d", p=128))],
            writes=[wor[j]], dma=True, dsem="qWB", tag="pfwo%d" % j)
    for gi in range(len(AG)):
        a1_main(gi)
        if gi + 2 < len(AG):
            a1_head(gi + 2)
        if taps == 2 and gi == 0:
            st("POOL", [
                lambda e: e.dma_start(out=tap_ap["rsb"][0], in_=rsb1[0][:, :]),
                lambda e: e.dma_start(out=tap_ap["ln0h"], in_=ln0h1[0][0][:, :, :]),
            ], reads=[rsb1[1], ln0h1[0][1]], dma=True, tag="tapa1")

    st("DVE", [
        lambda e: e.tensor_copy(kvf[0][:, 0:4, :].rearrange("p a b -> p (a b)"),
                                PS[4][0][:, 0:260]),
        lambda e: e.tensor_copy(kvf[0][:, 4:8, :].rearrange("p a b -> p (a b)"),
                                PS[5][0][:, 0:260]),
    ], reads=[PS[4][1], PS[5][1]], writes=[kvf[1]], tag="a2ev")
    st("DVE", [
        lambda e: e.tensor_copy(kvb[0][:, :, 0, :], kvf[0][:, :, :]),
        lambda e: e.tensor_sub(kvb[0][:, :, 1, :], kvf[0][:, :, :], kvb[0][:, :, 0, :]),
    ], reads=[kvf[1]], writes=[kvb[1]], tag="a2hl")
    st("DVE", [
        lambda e: e.memset(kvbz[0][:, :, :, :, :].rearrange('p a b c d -> p (a b c d)'), 0.0),
        lambda e: e.tensor_copy(kvbz[0][0:64, :, :, 0, :], kvb[0][0:64, :, :, 0:64]),
        lambda e: e.tensor_copy(kvbz[0][64:128, :, :, 1, :], kvb[0][64:128, :, :, 0:64]),
    ], reads=[kvb[1]], writes=[kvbz[1]], tag="a2kvz")
    ems = [lambda e: e.memset(zrhs[0][:, :, :], 0.0)]
    for p in range(PAIRS):
        for hl in range(2):
            ems.append(lambda e, p=p, hl=hl: e.tensor_copy(
                zrhs[0][0:64, p, hl:hl + 1], kvb[0][0:64, p, hl, 64:65]))
            ems.append(lambda e, p=p, hl=hl: e.tensor_copy(
                zrhs[0][64:128, p, 2 + hl:3 + hl], kvb[0][64:128, p, hl, 64:65]))
    st("DVE", ems, reads=[kvb[1]], writes=[zrhs[1]], tag="a2zr")
    if taps >= 1:
        st("POOL", [lambda e: e.dma_start(
            out=tap_ap["kvbz"],
            in_=kvbz[0][:, :, :, :, :].rearrange("p a b c d -> p (a b c d)"))],
            reads=[kvbz[1]], dma=True, tag="tapkvz")
    if taps >= 1:
        st("SP", [lambda e: e.dma_start(
            out=tap_ap["kvb"][:, :], in_=kvf[0][:, :, :].rearrange("p a b -> p (a b)"))],
            reads=[kvf[1]], dma=True, tag="tapkv")

    def finish():
        st("SP", [lambda e: e.nop()],
           extra=[(s, B.cnt.get(s, 0)) for s in
                  ("qSP", "qACT", "qGP", "qWA", "qWB", "qW1", "qW2", "qWr",
                   "qWk2", "qWv2", "sPE", "sACT", "sDVE", "sPOOL")], tag="finX")

    def barrier(name):
        hs = [st(eng, lambda e: e.nop(),
                 extra=[(s, B.cnt.get(s, 0)) for s in
                        ("sDVE", "sPE", "sACT", "sPOOL", "sSP", "qSP", "qACT",
                         "qGP", "qWA", "qWB", "qW1", "qW2", "qWr",
                         "qWk2", "qWv2")],
                 tag=name + eng) for eng in ENGS]
        B.gate = hs
        return hs

    if upto == "A":
        finish()
        return _replay(nc, B)
    barh = barrier("barC1")
    for g in reversed(p0g + a1g):
        g.__exit__(None, None, None)

    c1g = []

    def C1SB(name, shape, dt):
        g = nc.sbuf_tensor(name, shape, dt)
        t = g.__enter__()
        c1g.append(g)
        return t, Res()

    bsc = [C1SB("c1bs%d" % i, [128, CH, 416], BF16) for i in range(2)]
    sqc = [C1SB("c1sq%d" % i, [128, CH, 416], BF16) for i in range(2)]
    ln0f = [C1SB("c1l0f%d" % i, [128, CH, 416], F32) for i in range(2)]
    ln0fb = [Res() for _ in range(2)]
    ln0h = [C1SB("c1l0h%d" % i, [128, CH, 416], BF16) for i in range(2)]
    rsbc = [C1SB("c1rsb%d" % i, [128, 416], F32) for i in range(2)]
    msbc = [C1SB("c1msb%d" % i, [128, 416], F32) for i in range(2)]
    eqm = C1SB("c1eqm", [128, 416], F32)
    eqc = eqm
    Qt = [C1SB("c1Q%d" % i, [128, PAIRS, 416], BF16) for i in range(2)]
    ztc = C1SB("c1zt", [128, 4, 16], F32)
    zthl = C1SB("c1zthl", [128, 4, 2, 16], BF16)
    zfc = [C1SB("c1zf%d" % i, [32, 416], BF16) for i in range(2)]
    zbc8 = C1SB("c1zb8", [128, PAIRS, 416], F32)
    otc = C1SB("c1ot", [128, CH, 416], BF16)
    of_ = [C1SB("c1of%d" % i, [128, 416], F32) for i in range(2)]
    uf = [C1SB("c1uf%d" % i, [128, 416], F32) for i in range(2)]
    x1c = C1SB("c1x1", [128, CH, 416], F32)
    x1cb = Res()

    rsbc1 = [C1SB("c1rsb1%d" % i, [128, 416], F32) for i in range(2)]
    msbc1 = [C1SB("c1msb1%d" % i, [128, 416], F32) for i in range(2)]

    def c1_head(gi):
        w0, n = CG[gi]
        bi = gi % 2
        ln0_group(xTo, w0, n, bsc[bi], sqc[bi], 4, 5, rsbc[bi], msbc[bi],
                  tag="c1s%d" % gi, si=0)
        for eng, c0, c1, wrs in (("DVE", 0, 5, (ln0f[bi][1],)),
                                 ("POOL", 5, CH, (ln0fb[bi],))):
            ems = []
            for c in range(c0, c1):
                ems.append(lambda e, bi=bi, c=c, n=n: e.tensor_mul(
                    ln0f[bi][0][:, c, 0:n], bsc[bi][0][:, c, 0:n], rsbc[bi][0][:, 0:n]))
                ems.append(lambda e, bi=bi, c=c, n=n: e.tensor_sub(
                    ln0f[bi][0][:, c, 0:n], ln0f[bi][0][:, c, 0:n], msbc[bi][0][:, 0:n]))
            st(eng, ems, reads=[bsc[bi][1], rsbc[bi][1], msbc[bi][1]],
               writes=list(wrs), tag="c1l0f%s%d" % (eng, gi))
        st("ACT", [lambda e, bi=bi, n=n: e.activation(
            ln0h[bi][0][:, :, 0:n], ln0f[bi][0][:, :, 0:n], AF.Copy)],
            reads=[ln0f[bi][1], ln0fb[bi]], writes=[ln0h[bi][1]], tag="c1l0h%d" % gi)

    def c1_main(gi):
        w0, n = CG[gi]
        bi = gi % 2
        nch = (n + 127) // 128
        for dt in range(PAIRS):
            bidx = dt % 2
            ems = [lambda e, c=c, dt=dt, bidx=bidx, n=n: e.matmul(
                PS[bidx][0][:, 0:n], wq_sb[0][:, c, dt * 128:(dt + 1) * 128],
                ln0h[bi][0][:, c, 0:n], start=(c == 0), stop=(c == CH - 1),
                skip_group_check=True) for c in range(CH)]
            st("PE", ems, reads=[wqr[3], wor[3], ln0h[bi][1]], writes=[PS[bidx][1]],
               tag="c1q%d_%d" % (gi, dt))
            st("DVE", [lambda e, bidx=bidx, dt=dt, n=n: e.tensor_scalar(
                eqm[0][:, 0:n], PS[bidx][0][:, 0:n], vec["bq"][0][:, dt:dt + 1], 0.0,
                ALU.add, ALU.min)],
                reads=[PS[bidx][1], vec["bq"][1]], writes=[eqm[1]],
                tag="c1qm%d_%d" % (gi, dt))
            st("ACT", [lambda e, n=n: e.activation(
                eqc[0][:, 0:n], eqm[0][:, 0:n], AF.Exp)],
                reads=[eqm[1]], writes=[eqc[1]],
                tag="c1eq%d_%d" % (gi, dt))
            st("DVE", [lambda e, bidx=bidx, dt=dt, bi=bi, n=n: e.scalar_tensor_tensor(
                Qt[bi][0][:, dt, 0:n], PS[bidx][0][:, 0:n], vec["bq1"][0][:, dt:dt + 1],
                eqc[0][:, 0:n], ALU.add, ALU.max)],
                reads=[PS[bidx][1], eqc[1], vec["bq1"][1]], writes=[Qt[bi][1]],
                tag="c1Q%d_%d" % (gi, dt))
        ems = []
        for ch in range(nch):
            t0 = ch * 128
            m = min(128, n - t0)
            for p in range(PAIRS):
                for hl in range(2):
                    ems.append(lambda e, ch=ch, t0=t0, m=m, p=p, hl=hl, bi=bi: e.matmul(
                        PS[6][0][0:m, ch * 16 + 2 * p:ch * 16 + 2 * p + 2],
                        Qt[bi][0][:, p, t0:t0 + m], zrhs[0][:, p, hl::2],
                        start=(hl == 0), stop=(hl == 1), skip_group_check=True))
        st("PE", ems, reads=[Qt[bi][1], zrhs[1]], writes=[PS[6][1]], tag="c1zd%d" % gi)
        st("ACT", [
            lambda e, nch=nch: e.activation(
                ztc[0][:, 0:nch, :].rearrange("p a b -> p (a b)"),
                PS[6][0][:, 0:nch * 16], AF.Ln, bias=float(EPS_Z)),
            lambda e, nch=nch: e.activation(
                ztc[0][:, 0:nch, :].rearrange("p a b -> p (a b)"),
                ztc[0][:, 0:nch, :].rearrange("p a b -> p (a b)"), AF.Exp, scale=-1.0),
        ], reads=[PS[6][1]], writes=[ztc[1]], tag="c1zl%d" % gi)
        st("DVE", [
            lambda e, nch=nch: e.tensor_copy(zthl[0][:, 0:nch, 0, :], ztc[0][:, 0:nch, :]),
            lambda e, nch=nch: e.tensor_sub(zthl[0][:, 0:nch, 1, :], ztc[0][:, 0:nch, :],
                                            zthl[0][:, 0:nch, 0, :]),
        ], reads=[ztc[1]], writes=[zthl[1]], tag="c1zh%d" % gi)
        for ch in range(nch):
            t0 = ch * 128
            m = min(128, n - t0)
            st("PE", [lambda e, ch=ch: e.transpose(
                PSB[0][0:32, 0:128],
                zthl[0][:, ch, :, :].rearrange("p a b -> p (a b)"),
                ident_sb[0][:, :])],
                reads=[zthl[1], ident_sb[1]], writes=[PSB[1]], tag="c1zt%d_%d" % (gi, ch))
            st("ACT", [lambda e, t0=t0, m=m, bi=bi: e.activation(
                zfc[bi][0][:, t0:t0 + m], PSB[0][0:32, 0:m], AF.Copy)],
                reads=[PSB[1]], writes=[zfc[bi][1]], tag="c1ze%d_%d" % (gi, ch))
        for p in range(PAIRS):
            zi = p % 2
            st("PE", [lambda e, p=p, zi=zi, n=n, bi=bi: e.matmul(
                PS[zi][0][:, 0:n], zpat_sb[0][:, p, :], zfc[bi][0][:, 0:n],
                start=True, stop=True, skip_group_check=True)],
                reads=[zpat_sb[1], zfc[bi][1]], writes=[PS[zi][1]], tag="c1zb%d_%d" % (gi, p))
            st("ACT", [lambda e, p=p, zi=zi, n=n: e.activation(
                zbc8[0][:, p, 0:n], PS[zi][0][:, 0:n], AF.Copy)],
                reads=[PS[zi][1]], writes=[zbc8[1]], tag="c1zc%d_%d" % (gi, p))
        for p in range(PAIRS):
            oi = 2 + p % 2
            ems = []
            for hh in range(2):
                for hl in range(2):
                    ems.append(lambda e, oi=oi, hh=hh, hl=hl, p=p, n=n, bi=bi: e.matmul(
                        PS[oi][0][64 * hh:64 * (hh + 1), 0:n],
                        kvbz[0][:, p, hl, hh, :],
                        Qt[bi][0][:, p, 0:n],
                        start=(hl == 0), stop=(hl == 1),
                        skip_group_check=True, tile_position=(0, 64 * hh)))
            st("PE", ems, reads=[kvbz[1], Qt[bi][1]], writes=[PS[oi][1]],
               tag="c1o%d_%d" % (gi, p))
            st("ACT", [lambda e, oi=oi, p=p, n=n: e.activation(
                of_[p % 2][0][:, 0:n], PS[oi][0][:, 0:n], AF.Copy)],
                reads=[PS[oi][1]], writes=[of_[p % 2][1]], tag="c1of%d_%d" % (gi, p))
            st("DVE", [lambda e, p=p, n=n: e.tensor_mul(
                otc[0][:, p, 0:n], of_[p % 2][0][:, 0:n], zbc8[0][:, p, 0:n])],
                reads=[of_[p % 2][1], zbc8[1]], writes=[otc[1]], tag="c1oe%d_%d" % (gi, p))
        for dt in range(PAIRS):
            bidx = dt % 2
            ems = [lambda e, c=c, dt=dt, bidx=bidx, n=n: e.matmul(
                PS[bidx][0][:, 0:n], wo_sb[0][:, c, dt * 128:(dt + 1) * 128],
                otc[0][:, c, 0:n], start=(c == 0), stop=(c == CH - 1),
                skip_group_check=True) for c in range(CH)]
            st("PE", ems, reads=[wqr[3], wor[3], otc[1]], writes=[PS[bidx][1]],
               tag="c1wo%d_%d" % (gi, dt))
            st("ACT", [lambda e, bidx=bidx, dt=dt, n=n: e.activation(
                uf[dt % 2][0][:, 0:n], PS[bidx][0][:, 0:n], AF.Identity,
                bias=vec["bo"][0][:, dt:dt + 1])],
                reads=[PS[bidx][1], vec["bo"][1]], writes=[uf[dt % 2][1]],
                tag="c1uf%d_%d" % (gi, dt))
            st("DVE", [lambda e, dt=dt, bi=bi, n=n: e.tensor_add(
                ln0f[bi][0][:, dt, 0:n], ln0f[bi][0][:, dt, 0:n],
                uf[dt % 2][0][:, 0:n])],
                reads=[uf[dt % 2][1], ln0fb[bi]], writes=[ln0f[bi][1]],
                tag="c1u%d_%d" % (gi, dt))
        st("ACT", [lambda e, bi=bi, n=n: e.activation(
            bsc[bi][0][:, :, 0:n], ln0f[bi][0][:, :, 0:n], AF.Copy)],
            reads=[ln0f[bi][1], ln0fb[bi]], writes=[bsc[bi][1]], tag="c1ub%d" % gi)
        st("ACT", [lambda e, bi=bi, n=n: e.activation(
            sqc[bi][0][:, :, 0:n], bsc[bi][0][:, :, 0:n], AF.Square)],
            reads=[bsc[bi][1]], writes=[sqc[bi][1]], tag="c1us%d" % gi)
        stat_mms(2, 3, lambda c, bi=bi, n=n: bsc[bi][0][:, c, 0:n],
                 lambda c, bi=bi, n=n: sqc[bi][0][:, c, 0:n],
                 bsc[bi][1], sqc[bi][1], n, tag="c1sm%d" % gi)

    def c1_tail(gi):
        w0, n = CG[gi]
        bi = gi % 2
        row_math(2, 3, n, rsbc1[bi], msbc1[bi], si=1, tag="c1rm%d" % gi)
        for eng, c0, c1, wrs in (("DVE", 0, 5, (x1c[1],)), ("POOL", 5, CH, (x1cb,))):
            ems = []
            for c in range(c0, c1):
                ems.append(lambda e, bi=bi, c=c, n=n: e.tensor_mul(
                    x1c[0][:, c, 0:n], ln0f[bi][0][:, c, 0:n], rsbc1[bi][0][:, 0:n]))
                ems.append(lambda e, bi=bi, c=c, n=n: e.tensor_sub(
                    x1c[0][:, c, 0:n], x1c[0][:, c, 0:n], msbc1[bi][0][:, 0:n]))
            st(eng, ems, reads=[ln0f[bi][1], ln0fb[bi], rsbc1[bi][1], msbc1[bi][1]],
               writes=list(wrs), tag="c1xa%s%d" % (eng, gi))
        st("ACT", [lambda e, c=c, n=n: e.activation(
            x1c[0][:, c, 0:n], x1c[0][:, c, 0:n], AF.Identity,
            bias=vec["be1"][0][:, c:c + 1], scale=vec["g1"][0][:, c:c + 1])
            for c in range(CH)],
            reads=[vec["be1"][1], vec["g1"][1], x1c[1], x1cb],
            writes=[x1c[1], x1cb], tag="c1xg%d" % gi)
        st("ACT", [lambda e, w0=w0, n=n: e.dma_start(
            out=x1f_d[:, w0:w0 + n].rearrange("(c p) n -> p c n", p=128),
            in_=x1c[0][:, :, 0:n])], reads=[x1c[1]], dma=True, tag="c1d1%d" % gi)
        st("POOL", [lambda e, w0=w0, n=n: e.dma_start(
            out=x1h_d[:, w0:w0 + n].rearrange("(c p) n -> p c n", p=128),
            in_=x1c[0][:, :, 0:n])], reads=[x1c[1]], dma=True, tag="c1d2%d" % gi)
        if taps >= 1:
            st("SP", [lambda e, w0=w0, n=n: e.dma_start(
                out=tap_ap["x1"][:, w0:w0 + n].rearrange("(c p) n -> p c n", p=128),
                in_=x1c[0][:, :, 0:n])], reads=[x1c[1]], dma=True, tag="tapx1%d" % gi)

    c1_head(0)
    for gi in range(len(CG)):
        c1_main(gi)
        if gi + 1 < len(CG):
            c1_head(gi + 1)
        c1_tail(gi)

    if upto == "C1":
        finish()
        return _replay(nc, B)
    barh2 = barrier("barC2")
    for g in reversed(c1g):
        g.__exit__(None, None, None)
    for g in reversed(kvg):
        g.__exit__(None, None, None)
    for g in reversed(wqg):
        g.__exit__(None, None, None)

    c2ag = []

    def C2ASB(name, shape, dt):
        g = nc.sbuf_tensor(name, shape, dt)
        t = g.__enter__()
        c2ag.append(g)
        return t, Res()

    c2wg = []

    def C2WSB(name, shape, dt):
        g = nc.sbuf_tensor(name, shape, dt)
        t = g.__enter__()
        c2wg.append(g)
        return t, Res()

    w2_sb = C2WSB("w2_sb", [128, FFT, D], BF16)
    w2q = [Res() for _ in range(4)]

    w1_sb = C2ASB("w1_sb", [128, CH, FF], BF16)
    w1q = [Res() for _ in range(8)]
    x1hc = [C2ASB("c2x1h%d" % i, [128, CH, 257], BF16) for i in range(2)]
    h1c = [C2ASB("c2h1%d" % i, [128, FFT, 257], BF16) for i in range(2)]

    for q in range(8):
        st("POOL", [lambda e, q=q: e.dma_start(
            out=w1_sb[0][:, :, q * 512:(q + 1) * 512],
            in_=wdr["w1"][:, q * 512:(q + 1) * 512].rearrange("(c p) f -> p c f", p=128))],
            writes=[w1q[q]], dma=True, dsem="qW1", tag="c2aw%d" % q)
    for q in range(4):
        st("POOL", [lambda e, q=q: e.dma_start(
            out=w2_sb[0][:, :, q * 256:(q + 1) * 256],
            in_=wdr["w2"][:, q * 256:(q + 1) * 256].rearrange("(f p) d -> p f d", p=128))],
            writes=[w2q[q]], dma=True, dsem="qW2", tag="c2bw%d" % q)

    for gi, (w0, n) in enumerate(C2G):
        bi = gi % 2
        st("SP", [lambda e, bi=bi, w0=w0, n=n: e.dma_start(
            out=x1hc[bi][0][:, :, 0:n],
            in_=x1h_d[:, w0:w0 + n].rearrange("(c p) n -> p c n", p=128))],
            writes=[x1hc[bi][1]], dma=True, tag="c2alh%d" % gi)
        for f in range(FFT):
            bidx = f % 4
            ems = [lambda e, c=c, f=f, bidx=bidx, bi=bi, n=n: e.matmul(
                PS[bidx][0][:, 0:n], w1_sb[0][:, c, f * 128:(f + 1) * 128],
                x1hc[bi][0][:, c, 0:n], start=(c == 0), stop=(c == CH - 1),
                skip_group_check=True) for c in range(CH)]
            st("PE", ems, reads=[w1q[7], x1hc[bi][1]], writes=[PS[bidx][1]],
               tag="c2w1_%d_%d" % (gi, f))
            st("ACT", [lambda e, f=f, bidx=bidx, bi=bi, n=n: e.activation(
                h1c[bi][0][:, f, 0:n], PS[bidx][0][:, 0:n], AF.Gelu,
                bias=b1_sb[0][:, f:f + 1])],
                reads=[PS[bidx][1], b1_sb[1]], writes=[h1c[bi][1]], tag="c2g%d_%d" % (gi, f))
        st("ACT", [lambda e, bi=bi, w0=w0, n=n: e.dma_start(
            out=h1_d[:, w0:w0 + n].rearrange("(f p) n -> p f n", p=128),
            in_=h1c[bi][0][:, :, 0:n])], reads=[h1c[bi][1]], dma=True, tag="c2ast%d" % gi)

    barh2b = barrier("barC2b")
    for g in reversed(c2ag):
        g.__exit__(None, None, None)

    c2g = []

    def C2SB(name, shape, dt):
        g = nc.sbuf_tensor(name, shape, dt)
        t = g.__enter__()
        c2g.append(g)
        return t, Res()

    h1b = [C2SB("c2bh%d" % i, [128, FFT, 257], BF16) for i in range(2)]
    x1fc = [C2SB("c2x1f%d" % i, [128, CH, 257], F32) for i in range(2)]
    vc = C2SB("c2v", [128, CH, 257], F32)
    tc = C2SB("c2t", [128, CH, 257], F32)
    ac = tc
    vf = [C2SB("c2vf%d" % i, [128, 257], F32) for i in range(2)]
    vbc = C2SB("c2vb", [128, CH, 257], BF16)
    vsqc = C2SB("c2vsq", [128, CH, 257], BF16)
    rsb2 = C2SB("c2rsb", [128, 257], F32)
    msb2 = C2SB("c2msb", [128, 257], F32)

    def c2b_head(gi):
        w0, n = C2G[gi]
        bi = gi % 2
        st("SP", [
            lambda e, bi=bi, w0=w0, n=n: e.dma_start(
                out=h1b[bi][0][:, :, 0:n],
                in_=h1_d[:, w0:w0 + n].rearrange("(f p) n -> p f n", p=128)),
            lambda e, bi=bi, w0=w0, n=n: e.dma_start(
                out=x1fc[bi][0][:, :, 0:n],
                in_=x1f_d[:, w0:w0 + n].rearrange("(c p) n -> p c n", p=128)),
        ], writes=[h1b[bi][1], x1fc[bi][1]], dma=True, tag="c2bl%d" % gi)

    def c2b_main(gi):
        w0, n = C2G[gi]
        bi = gi % 2
        for dt in range(CH):
            bidx = 4 + dt % 2
            ems = [lambda e, f=f, dt=dt, bidx=bidx, bi=bi, n=n: e.matmul(
                PS[bidx][0][:, 0:n], w2_sb[0][:, f, dt * 128:(dt + 1) * 128],
                h1b[bi][0][:, f, 0:n], start=(f == 0), stop=(f == FFT - 1),
                skip_group_check=True) for f in range(FFT)]
            st("PE", ems, reads=[w2q[3], h1b[bi][1]], writes=[PS[bidx][1]],
               tag="c2w2_%d_%d" % (gi, dt))
            st("ACT", [lambda e, bidx=bidx, dt=dt, n=n: e.activation(
                vf[dt % 2][0][:, 0:n], PS[bidx][0][:, 0:n], AF.Identity,
                bias=vec["b2"][0][:, dt:dt + 1])],
                reads=[PS[bidx][1], vec["b2"][1]], writes=[vf[dt % 2][1]],
                tag="c2ve%d_%d" % (gi, dt))
            st("DVE", [lambda e, dt=dt, bi=bi, n=n: e.tensor_add(
                vc[0][:, dt, 0:n], vf[dt % 2][0][:, 0:n], x1fc[bi][0][:, dt, 0:n])],
                reads=[vf[dt % 2][1], x1fc[bi][1]], writes=[vc[1]],
                tag="c2v%d_%d" % (gi, dt))
        st("ACT", [lambda e, n=n: e.activation(vbc[0][:, :, 0:n], vc[0][:, :, 0:n],
                                               AF.Copy)],
           reads=[vc[1]], writes=[vbc[1]], tag="c2vb%d" % gi)
        st("ACT", [lambda e, n=n: e.activation(vsqc[0][:, :, 0:n], vbc[0][:, :, 0:n],
                                               AF.Square)],
           reads=[vbc[1]], writes=[vsqc[1]], tag="c2vs%d" % gi)
        stat_mms(2, 3, lambda c, n=n: vbc[0][:, c, 0:n], lambda c, n=n: vsqc[0][:, c, 0:n],
                 vbc[1], vsqc[1], n, tag="c2sm%d" % gi)

    def c2b_tail(gi):
        w0, n = C2G[gi]
        row_math(2, 3, n, rsb2, msb2, si=1, tag="c2rm%d" % gi)
        ems = []
        for c in range(CH):
            ems.append(lambda e, c=c, n=n: e.tensor_mul(
                tc[0][:, c, 0:n], vc[0][:, c, 0:n], rsb2[0][:, 0:n]))
            ems.append(lambda e, c=c, n=n: e.tensor_sub(
                tc[0][:, c, 0:n], tc[0][:, c, 0:n], msb2[0][:, 0:n]))
        st("DVE", ems, reads=[vc[1], rsb2[1], msb2[1]], writes=[tc[1]], tag="c2t%d" % gi)
        st("ACT", [lambda e, c=c, n=n: e.activation(
            ac[0][:, c, 0:n], tc[0][:, c, 0:n], AF.Identity,
            bias=vec["be2"][0][:, c:c + 1], scale=vec["g2"][0][:, c:c + 1])
            for c in range(CH)],
            reads=[tc[1], vec["be2"][1], vec["g2"][1]], writes=[ac[1]], tag="c2a%d" % gi)
        st("ACT", [lambda e, w0=w0, n=n: e.dma_start(
            out=a_d[:, w0:w0 + n].rearrange("(c p) n -> p c n", p=128),
            in_=ac[0][:, :, 0:n])], reads=[ac[1]], dma=True, tag="c2d%d" % gi)
        if taps >= 1:
            st("SP", [lambda e, w0=w0, n=n: e.dma_start(
                out=tap_ap["a"][:, w0:w0 + n].rearrange("(c p) n -> p c n", p=128),
                in_=ac[0][:, :, 0:n])], reads=[ac[1]], dma=True, tag="tapa%d" % gi)

    c2b_head(0)
    for gi in range(len(C2G)):
        c2b_main(gi)
        if gi + 1 < len(C2G):
            c2b_head(gi + 1)
        c2b_tail(gi)

    barh3 = barrier("barD")
    for g in reversed(c2g):
        g.__exit__(None, None, None)
    for g in reversed(c2wg):
        g.__exit__(None, None, None)

    dg = []

    def DSB(name, shape, dt):
        g = nc.sbuf_tensor(name, shape, dt)
        t = g.__enter__()
        dg.append(g)
        return t, Res()

    DG2 = [(1 + 256 * i, 256) for i in range(8)]
    NN = 257
    wr_sb = DSB("wr_sb", [128, CH, D], BF16)
    wk2_sb = DSB("wk2_sb", [128, CH, D], BF16)
    wv2_sb = DSB("wv2_sb", [128, CH, D], BF16)
    wrr = [Res() for _ in range(4)]
    wkr = [Res() for _ in range(4)]
    wvr = [Res() for _ in range(4)]
    agb = [DSB("d_a%d" % i, [128, CH, NN], F32) for i in range(2)]
    abf = [DSB("d_ab%d" % i, [128, CH, NN], BF16) for i in range(2)]
    xgb = DSB("d_xg", [128, CH, NN], F32)
    xmid = [DSB("d_xm%d" % i, [128, CH, NN], F32) for i in range(2)]
    xmidb = [Res() for _ in range(2)]
    ln2 = [DSB("d_ln2%d" % i, [128, CH, NN], F32) for i in range(2)]
    ln2b = [Res() for _ in range(2)]
    asq = DSB("d_asq", [128, CH, NN], BF16)
    bb1 = DSB("d_bb1", [128, CH, NN], BF16)
    bb2 = DSB("d_bb2", [128, CH, NN], BF16)
    xrb = [DSB("d_xr%d" % i, [128, CH, 256], BF16) for i in range(2)]
    xkb = [DSB("d_xk%d" % i, [128, CH, 256], BF16) for i in range(2)]
    scr2 = DSB("d_s2", [128, CH, 256], F32)
    rb_ = DSB("d_rb", [128, CH, 256], F32)
    ff_ = [DSB("d_ff%d" % i, [128, 256], F32) for i in range(2)]
    dif = abf
    kab = xrb
    rsbd = [DSB("d_rsb%d" % i, [128, NN], F32) for i in range(2)]
    msbd = [DSB("d_msb%d" % i, [128, NN], F32) for i in range(2)]
    rsbdF = DSB("d_rsbF", [128, 256], F32)
    msbdF = DSB("d_msbF", [128, 256], F32)
    bbf1 = xkb
    bbf2 = xrb

    for nm, tsb, rr, ds in (("wr", wr_sb, wrr, "qWr"), ("wk2", wk2_sb, wkr, "qWk2"),
                            ("wv2", wv2_sb, wvr, "qWv2")):
        for j in range(4):
            st("POOL", [lambda e, nm=nm, tsb=tsb, j=j: e.dma_start(
                out=tsb[0][:, :, j * 256:(j + 1) * 256],
                in_=wdr[nm][:, j * 256:(j + 1) * 256].rearrange(
                    "(c p) d -> p c d", p=128))],
                writes=[rr[j]], dma=True, dsem=ds, tag="dw%s%d" % (nm, j))

    def d_head(gi):
        w0, n = DG2[gi]
        cw0 = w0 - 1
        nn = n + 1
        bi = gi % 2
        st("SP", [
            lambda e, bi=bi, cw0=cw0, nn=nn: e.dma_start(
                out=agb[bi][0][:, :, 0:nn],
                in_=a_d[:, cw0:cw0 + nn].rearrange("(c p) n -> p c n", p=128)),
            lambda e, bi=bi, cw0=cw0, nn=nn: e.dma_start(
                out=xgb[0][:, :, 0:nn],
                in_=xTo[:, cw0:cw0 + nn].rearrange("(c p) n -> p c n", p=128)),
        ], writes=[agb[bi][1], xgb[1]], dma=True, tag="dl%d" % gi)
        st("POOL", [lambda e, bi=bi, cw0=cw0, nn=nn: e.dma_start(
            out=abf[bi][0][:, :, 0:nn],
            in_=a_d[:, cw0:cw0 + nn].rearrange("(c p) n -> p c n", p=128))],
            writes=[abf[bi][1]], dma=True, tag="dlb%d" % gi)
        st("ACT", [lambda e, bi=bi, nn=nn: e.activation(
            asq[0][:, :, 0:nn], abf[bi][0][:, :, 0:nn], AF.Square)],
            reads=[abf[bi][1]], writes=[asq[1]], tag="das%d" % gi)
        stat_mms(2, 3, lambda c, bi=bi, nn=nn: abf[bi][0][:, c, 0:nn],
                 lambda c, nn=nn: asq[0][:, c, 0:nn],
                 abf[bi][1], asq[1], nn, tag="dsm%d" % gi)

    def d_chain(gi):
        w0, n = DG2[gi]
        nn = n + 1
        bi = gi % 2
        row_math(2, 3, nn, rsbd[bi], msbd[bi], si=0, tag="drm%d" % gi)
        for eng, c0, c1, wrs in (("DVE", 0, 6, (xmid[bi][1],)),
                                 ("POOL", 6, CH, (xmidb[bi],))):
            ems = []
            for c in range(c0, c1):
                ems.append(lambda e, bi=bi, c=c, nn=nn: e.tensor_mul(
                    xmid[bi][0][:, c, 0:nn], agb[bi][0][:, c, 0:nn], rsbd[bi][0][:, 0:nn]))
                ems.append(lambda e, bi=bi, c=c, nn=nn: e.tensor_sub(
                    xmid[bi][0][:, c, 0:nn], xmid[bi][0][:, c, 0:nn], msbd[bi][0][:, 0:nn]))
                ems.append(lambda e, bi=bi, c=c, nn=nn: e.tensor_add(
                    xmid[bi][0][:, c, 0:nn], xmid[bi][0][:, c, 0:nn], xgb[0][:, c, 0:nn]))
            st(eng, ems, reads=[agb[bi][1], rsbd[bi][1], msbd[bi][1], xgb[1]],
               writes=list(wrs), tag="dxm%s%d" % (eng, gi))
        st("ACT", [lambda e, bi=bi, nn=nn: e.activation(
            bb1[0][:, :, 0:nn], xmid[bi][0][:, :, 0:nn], AF.Copy)],
            reads=[xmid[bi][1], xmidb[bi]], writes=[bb1[1]], tag="dxb%d" % gi)
        st("ACT", [lambda e, nn=nn: e.activation(bb2[0][:, :, 0:nn], bb1[0][:, :, 0:nn],
                                                 AF.Square)],
           reads=[bb1[1]], writes=[bb2[1]], tag="dxs%d" % gi)
        stat_mms(2, 3, lambda c, nn=nn: bb1[0][:, c, 0:nn],
                 lambda c, nn=nn: bb2[0][:, c, 0:nn],
                 bb1[1], bb2[1], nn, tag="dsm2%d" % gi)
        row_math(2, 3, nn, rsbd[bi], msbd[bi], si=0, tag="drm2%d" % gi)
        if gi == 0:
            st("DVE", [
                lambda e: e.tensor_mul(rsbd[0][0][:, 0:1], rsbd[0][0][:, 0:1],
                                       vec["cmask"][0][:, 0:1]),
                lambda e: e.tensor_mul(msbd[0][0][:, 0:1], msbd[0][0][:, 0:1],
                                       vec["cmask"][0][:, 0:1]),
            ], reads=[vec["cmask"][1]], writes=[rsbd[0][1], msbd[0][1]], tag="dmask")
        for eng, c0, c1, wrs in (("DVE", 0, 6, (ln2[bi][1],)),
                                 ("POOL", 6, CH, (ln2b[bi],))):
            ems = []
            for c in range(c0, c1):
                ems.append(lambda e, bi=bi, c=c, nn=nn: e.tensor_mul(
                    ln2[bi][0][:, c, 0:nn], xmid[bi][0][:, c, 0:nn], rsbd[bi][0][:, 0:nn]))
                ems.append(lambda e, bi=bi, c=c, nn=nn: e.tensor_sub(
                    ln2[bi][0][:, c, 0:nn], ln2[bi][0][:, c, 0:nn], msbd[bi][0][:, 0:nn]))
            st(eng, ems, reads=[xmid[bi][1], xmidb[bi], rsbd[bi][1], msbd[bi][1]],
               writes=list(wrs), tag="dln2%s%d" % (eng, gi))
        st("POOL", [lambda e, bi=bi, c=c, n=n: e.tensor_sub(
            dif[bi][0][:, c, 0:n], ln2[bi][0][:, c, 1:n + 1], ln2[bi][0][:, c, 0:n])
            for c in range(CH)],
            reads=[ln2[bi][1], ln2b[bi]], writes=[dif[bi][1]], tag="ddif%d" % gi)
        ems = []
        for c in range(CH):
            ems.append(lambda e, bi=bi, c=c, n=n: e.scalar_tensor_tensor(
                xrb[bi][0][:, c, 0:n], dif[bi][0][:, c, 0:n], vec["mur"][0][:, c:c + 1],
                ln2[bi][0][:, c, 0:n], ALU.mult, ALU.add))
            ems.append(lambda e, bi=bi, c=c, n=n: e.scalar_tensor_tensor(
                xkb[bi][0][:, c, 0:n], dif[bi][0][:, c, 0:n], vec["muk"][0][:, c:c + 1],
                ln2[bi][0][:, c, 0:n], ALU.mult, ALU.add))
        st("DVE", ems, reads=[ln2[bi][1], ln2b[bi], dif[bi][1],
                              vec["mur"][1], vec["muk"][1]],
           writes=[xrb[bi][1], xkb[bi][1]], tag="dxrk%d" % gi)

    def d_ffn_a(gi):
        w0, n = DG2[gi]
        bi = gi % 2
        for dt in range(CH):
            bidx = dt % 2
            ems = [lambda e, c=c, dt=dt, bidx=bidx, bi=bi, n=n: e.matmul(
                PS[bidx][0][:, 0:n], wr_sb[0][:, c, dt * 128:(dt + 1) * 128],
                xrb[bi][0][:, c, 0:n], start=(c == 0), stop=(c == CH - 1),
                skip_group_check=True) for c in range(CH)]
            st("PE", ems, reads=[wrr[3], xrb[bi][1]], writes=[PS[bidx][1]],
               tag="dwr%d_%d" % (gi, dt))
            st("ACT", [lambda e, bidx=bidx, dt=dt, n=n: e.activation(
                rb_[0][:, dt, 0:n], PS[bidx][0][:, 0:n], AF.Sigmoid)],
                reads=[PS[bidx][1]], writes=[rb_[1]], tag="drs%d_%d" % (gi, dt))
        for dt in range(CH):
            bidx = dt % 2
            ems = [lambda e, c=c, dt=dt, bidx=bidx, bi=bi, n=n: e.matmul(
                PS[bidx][0][:, 0:n], wk2_sb[0][:, c, dt * 128:(dt + 1) * 128],
                xkb[bi][0][:, c, 0:n], start=(c == 0), stop=(c == CH - 1),
                skip_group_check=True) for c in range(CH)]
            st("PE", ems, reads=[wkr[3], xkb[bi][1]], writes=[PS[bidx][1]],
               tag="dwk%d_%d" % (gi, dt))
            st("ACT", [lambda e, bidx=bidx, dt=dt, n=n: e.activation(
                scr2[0][:, dt, 0:n], PS[bidx][0][:, 0:n], AF.Relu)],
                reads=[PS[bidx][1]], writes=[scr2[1]], tag="dkr%d_%d" % (gi, dt))
        st("ACT", [lambda e, bi=bi, n=n: e.activation(
            kab[bi][0][:, :, 0:n], scr2[0][:, :, 0:n], AF.Square)],
            reads=[scr2[1]], writes=[kab[bi][1]], tag="dka%d" % gi)

    def d_ffn_b(gi):
        w0, n = DG2[gi]
        bi = gi % 2
        for dt in range(CH):
            bidx = dt % 2
            ems = [lambda e, c=c, dt=dt, bidx=bidx, bi=bi, n=n: e.matmul(
                PS[bidx][0][:, 0:n], wv2_sb[0][:, c, dt * 128:(dt + 1) * 128],
                kab[bi][0][:, c, 0:n], start=(c == 0), stop=(c == CH - 1),
                skip_group_check=True) for c in range(CH)]
            st("PE", ems, reads=[wvr[3], kab[bi][1]], writes=[PS[bidx][1]],
               tag="dwv%d_%d" % (gi, dt))
            st("ACT", [lambda e, bidx=bidx, dt=dt, n=n: e.activation(
                ff_[dt % 2][0][:, 0:n], PS[bidx][0][:, 0:n], AF.Copy)],
                reads=[PS[bidx][1]], writes=[ff_[dt % 2][1]], tag="dfe%d_%d" % (gi, dt))
            st("DVE", [lambda e, dt=dt, n=n: e.tensor_mul(
                scr2[0][:, dt, 0:n], ff_[dt % 2][0][:, 0:n], rb_[0][:, dt, 0:n])],
                reads=[ff_[dt % 2][1], rb_[1]], writes=[scr2[1]], tag="df%d_%d" % (gi, dt))
        st("ACT", [lambda e, bi=bi, n=n: e.activation(
            bbf1[bi][0][:, :, 0:n], scr2[0][:, :, 0:n], AF.Copy)],
            reads=[scr2[1]], writes=[bbf1[bi][1]], tag="dfb%d" % gi)
        st("ACT", [lambda e, bi=bi, n=n: e.activation(
            bbf2[bi][0][:, :, 0:n], bbf1[bi][0][:, :, 0:n], AF.Square)],
            reads=[bbf1[bi][1]], writes=[bbf2[bi][1]], tag="dfs%d" % gi)
        stat_mms(4, 5, lambda c, bi=bi, n=n: bbf1[bi][0][:, c, 0:n],
                 lambda c, bi=bi, n=n: bbf2[bi][0][:, c, 0:n],
                 bbf1[bi][1], bbf2[bi][1], n, tag="dsm3%d" % gi)
        row_math(4, 5, n, rsbdF, msbdF, si=2, tag="drm3%d" % gi)
        ems = []
        for c in range(CH):
            ems.append(lambda e, c=c, n=n: e.tensor_mul(
                scr2[0][:, c, 0:n], scr2[0][:, c, 0:n], rsbdF[0][:, 0:n]))
            ems.append(lambda e, c=c, n=n: e.tensor_sub(
                scr2[0][:, c, 0:n], scr2[0][:, c, 0:n], msbdF[0][:, 0:n]))
            ems.append(lambda e, bi=bi, c=c, n=n: e.tensor_add(
                scr2[0][:, c, 0:n], scr2[0][:, c, 0:n], xmid[bi][0][:, c, 1:1 + n]))
        st("DVE", ems, reads=[rsbdF[1], msbdF[1], xmid[bi][1], xmidb[bi]],
           writes=[scr2[1]], tag="dout%d" % gi)
        st("SP", [lambda e, w0=w0, n=n: e.dma_start(
            out=outT[:, w0 - 1:w0 - 1 + n].rearrange("(c p) n -> p c n", p=128),
            in_=scr2[0][:, :, 0:n])], reads=[scr2[1]], dma=True, tag="dst%d" % gi)

    d_head(0)
    d_chain(0)
    for gi in range(len(DG2)):
        if gi + 1 < len(DG2):
            d_head(gi + 1)
        d_ffn_a(gi)
        if gi + 1 < len(DG2):
            d_chain(gi + 1)
        d_ffn_b(gi)

    st("SP", [lambda e: e.nop()], extra=[("qSP", B.cnt["qSP"])], tag="fin")

    nc2 = _replay(nc, B)
    for g in reversed(semg_dummy) if False else []:
        pass
    for g in reversed(dg):
        g.__exit__(None, None, None)
    for g in reversed(guards):
        g.__exit__(None, None, None)
    return nc2


def _replay(nc, B):
    sems = {}
    semg = []
    with nc.Block() as block:
        for name in ("sPE", "sACT", "sDVE", "sPOOL", "sSP", "qSP", "qACT", "qGP", "qWA", "qWB", "qW1", "qW2", "qWr", "qWk2", "qWv2"):
            g = nc.semaphore(name)
            sems[name] = g.__enter__()
            semg.append(g)
        eng_map = {"PE": block.tensor, "ACT": block.scalar, "DVE": block.vector,
                   "POOL": block.gpsimd, "SP": block.sync}

        def mk(eng_key):
            def body(e):
                for (eng, waits, emits, sem, dma, tag) in B.ops:
                    if eng != eng_key:
                        continue
                    for sname, v in sorted(waits.items()):
                        e.wait_ge(sems[sname], v)
                    last = None
                    for fn in emits:
                        inst = fn(e)
                        if dma:
                            inst.then_inc(sems[sem], 16)
                        else:
                            last = inst
                    if not dma and last is not None:
                        last.then_inc(sems[sem], 1)
            return body

        for k, dec in eng_map.items():
            dec(mk(k))
    for g in reversed(semg):
        g.__exit__(None, None, None)
    return nc


def _stage_inputs(inputs):
    f32 = np.float32
    x = np.asarray(inputs["L0_attn_src"], f32)
    col = lambda v: np.ascontiguousarray(np.asarray(v, f32).reshape(-1, 128).T)
    shared = {
        "wq": np.ascontiguousarray(np.asarray(inputs["L0_attn_Wq"], f32)),
        "wk": np.ascontiguousarray(np.asarray(inputs["L0_attn_Wk"], f32)),
        "wv": np.ascontiguousarray(np.asarray(inputs["L0_attn_Wv"], f32)),
        "wo": np.ascontiguousarray(np.asarray(inputs["L0_attn_Wo"], f32)),
        "w1": np.ascontiguousarray(np.asarray(inputs["L0_attn_W1"], f32)),
        "w2": np.ascontiguousarray(np.asarray(inputs["L0_attn_W2"], f32)),
        "wr": np.ascontiguousarray(np.asarray(inputs["L0_ffn_W_r"], f32).T),
        "wk2": np.ascontiguousarray(np.asarray(inputs["L0_ffn_W_k"], f32).T),
        "wv2": np.ascontiguousarray(np.asarray(inputs["L0_ffn_W_v"], f32).T),
        "bq": col(inputs["L0_attn_bq"]),
        "bo": col(inputs["L0_attn_bo"]),
        "b2": col(inputs["L0_attn_b2"]),
        "g1": col(inputs["L0_attn_g1"]),
        "be1": col(inputs["L0_attn_be1"]),
        "g2": col(inputs["L0_attn_g2"]),
        "be2": col(inputs["L0_attn_be2"]),
        "mur": col(inputs["L0_ffn_mu_r"]),
        "muk": col(inputs["L0_ffn_mu_k"]),
        "b1c": col(inputs["L0_attn_b1"]),
        "bkrow": np.ascontiguousarray(np.asarray(inputs["L0_attn_bk"], f32)[None, :]),
        "bvrow": np.ascontiguousarray(np.asarray(inputs["L0_attn_bv"], f32)[None, :]),
        "ident": np.eye(128, dtype=f32),
    }
    zpat = np.zeros((32, PAIRS, 128), f32)
    for p in range(PAIRS):
        for hl in range(2):
            zpat[hl * 16 + 2 * p, p, 0:64] = 1.0
            zpat[hl * 16 + 2 * p + 1, p, 64:128] = 1.0
    shared["zpat"] = zpat
    masks = [np.zeros((128, CH), f32), np.ones((128, CH), f32)]
    xTb = [np.ascontiguousarray(
        np.concatenate([np.zeros((D, 1), f32), x[b].T], axis=1)) for b in range(4)]
    in_maps = []
    for c in range(8):
        b, h = c // 2, c % 2
        m = dict(shared)
        m["xT"] = xTb[b]
        m["xTo"] = np.ascontiguousarray(xTb[b][:, h * 2048:h * 2048 + OWN])
        m["cmask"] = masks[h]
        in_maps.append(m)
    return in_maps


def kernel(**inputs):
    in_maps = _stage_inputs(inputs)
    nc = build_program(taps=0)
    res = run_bass_kernel_spmd(nc, in_maps, core_ids=list(range(8)))
    out = np.empty((4, 4096, 1024), np.float32)
    for c in range(8):
        b, h = c // 2, c % 2
        out[b, h * 2048:(h + 1) * 2048, :] = res.results[c]["outT"].T
    return out


# version 10
# speedup vs baseline: 1.0218x; 1.0218x over previous
import numpy as np
import concourse.bass as bass
import concourse.mybir as mybir
from concourse.bass_utils import run_bass_kernel_spmd

F32 = mybir.dt.float32
BF16 = mybir.dt.bfloat16
AF = mybir.ActivationFunctionType
ALU = mybir.AluOpType

D = 1024
CH = 8
H = 16
PAIRS = 8
S = 4096
OWN = 2049
FF = 4096
FFT = 32
EPS_LN = 1e-5
EPS_Z = 1e-6

AG = [(1 + 256 * g, 256) for g in range(16)]
CG = [(0, 385), (385, 416), (801, 416), (1217, 416), (1633, 416)]
C2G = [(0, 257)] + [(257 + 256 * i, 256) for i in range(7)]
DG = [(1, 410), (411, 410), (821, 410), (1231, 410), (1641, 408)]

ENGS = ("PE", "ACT", "DVE", "POOL", "SP")


class Res:
    __slots__ = ("w", "r")

    def __init__(self):
        self.w = None
        self.r = []

    def rdep(self):
        return [self.w] if self.w else []

    def wdep(self):
        d = list(self.r)
        if self.w:
            d.append(self.w)
        return d


class Builder:
    def __init__(self):
        self.ops = []
        self.cnt = {}
        self.gate = []

    def st(self, eng, emits, reads=(), writes=(), extra=(), dma=False, tag="",
           dsem=None):
        if not isinstance(emits, (list, tuple)):
            emits = [emits]
        deps = [d for d in extra if d is not None] + list(self.gate)
        for r in reads:
            deps += r.rdep()
        for r in writes:
            deps += r.wdep()
        if dma:
            sem = dsem or {"SP": "qSP", "ACT": "qACT", "POOL": "qGP"}[eng]
            amt = 16 * len(emits)
        else:
            sem = "s" + eng
            amt = 1
        own = None if dma else sem
        waits = {}
        for s, v in deps:
            if s == own:
                continue
            if waits.get(s, -1) < v:
                waits[s] = v
        cur = self.cnt.get(sem, 0) + amt
        self.cnt[sem] = cur
        self.ops.append((eng, waits, emits, sem, dma, tag))
        h = (sem, cur)
        for r in reads:
            r.r.append(h)
        for r in writes:
            r.w = h
            r.r = []
        return h


def build_program(taps=0, upto=None):
    nc = bass.Bass("TRN2", target_bir_lowering=False, debug=False)
    B = Builder()
    st = B.st

    xT = nc.dram_tensor("xT", [D, S + 1], F32, kind="ExternalInput").ap()
    xTo = nc.dram_tensor("xTo", [D, OWN], F32, kind="ExternalInput").ap()
    wdr = {}
    for nm, shp in (("wq", [D, D]), ("wk", [D, D]), ("wv", [D, D]), ("wo", [D, D]),
                    ("w1", [D, FF]), ("w2", [FF, D]),
                    ("wr", [D, D]), ("wk2", [D, D]), ("wv2", [D, D])):
        wdr[nm] = nc.dram_tensor(nm, shp, F32, kind="ExternalInput").ap()
    vnames = ("bq", "bo", "b2", "g1", "be1", "g2", "be2", "mur", "muk", "cmask")
    vd = {nm: nc.dram_tensor(nm, [128, CH], F32, kind="ExternalInput").ap() for nm in vnames}
    b1c_d = nc.dram_tensor("b1c", [128, FFT], F32, kind="ExternalInput").ap()
    bk_row_d = nc.dram_tensor("bkrow", [1, D], F32, kind="ExternalInput").ap()
    bv_row_d = nc.dram_tensor("bvrow", [1, D], F32, kind="ExternalInput").ap()
    zpat_d = nc.dram_tensor("zpat", [32, PAIRS, 128], F32, kind="ExternalInput").ap()
    ident_d = nc.dram_tensor("ident", [128, 128], F32, kind="ExternalInput").ap()
    x1f_d = nc.dram_tensor("x1f_d", [D, OWN], F32, kind="Internal").ap()
    x1h_d = nc.dram_tensor("x1h_d", [D, OWN], BF16, kind="Internal").ap()
    a_d = nc.dram_tensor("a_d", [D, OWN], F32, kind="Internal").ap()
    h1_d = nc.dram_tensor("h1_d", [FF, OWN], BF16, kind="Internal").ap()
    outT = nc.dram_tensor("outT", [D, S // 2], F32, kind="ExternalOutput").ap()
    tap_ap = {}
    if taps >= 1:
        tap_ap["rl"] = nc.dram_tensor("t_rl", [4, 512], F32, kind="ExternalOutput").ap()
        tap_ap["rsb"] = nc.dram_tensor("t_rsb", [2, 128, 512], F32, kind="ExternalOutput").ap()
        tap_ap["ln0h"] = nc.dram_tensor("t_ln0h", [128, CH, 512], F32, kind="ExternalOutput").ap()
        tap_ap["K"] = nc.dram_tensor("t_K", [128, D], F32, kind="ExternalOutput").ap()
        tap_ap["V"] = nc.dram_tensor("t_V", [128, H * 65], F32, kind="ExternalOutput").ap()
        tap_ap["kf"] = nc.dram_tensor("t_kf", [128, D], F32, kind="ExternalOutput").ap()
        tap_ap["Q"] = nc.dram_tensor("t_Q", [128, CH, 448], F32, kind="ExternalOutput").ap()
        tap_ap["z"] = nc.dram_tensor("t_z", [128, 4, 16], F32, kind="ExternalOutput").ap()
        tap_ap["o"] = nc.dram_tensor("t_o", [128, CH, 448], F32, kind="ExternalOutput").ap()
        tap_ap["u"] = nc.dram_tensor("t_u", [128, CH, 448], F32, kind="ExternalOutput").ap()
        tap_ap["zb"] = nc.dram_tensor("t_zb", [2, 128, 448], F32, kind="ExternalOutput").ap()
        tap_ap["zf"] = nc.dram_tensor("t_zf", [32, 448], F32, kind="ExternalOutput").ap()
        tap_ap["oun"] = nc.dram_tensor("t_oun", [128, 448], F32, kind="ExternalOutput").ap()
        tap_ap["u2"] = nc.dram_tensor("t_u2", [128, CH, 448], F32, kind="ExternalOutput").ap()
        tap_ap["l0"] = nc.dram_tensor("t_l0", [128, CH, 448], F32, kind="ExternalOutput").ap()
        tap_ap["zb2"] = nc.dram_tensor("t_zb2", [128, PAIRS, 448], F32, kind="ExternalOutput").ap()
        tap_ap["kvbz"] = nc.dram_tensor("t_kvbz", [128, PAIRS * 2 * 2 * 64], F32,
                                        kind="ExternalOutput").ap()
        tap_ap["rows"] = nc.dram_tensor("t_rows", [4, S + 1], F32, kind="ExternalOutput").ap()
        tap_ap["kvb"] = nc.dram_tensor("t_kvb", [128, PAIRS * 65], F32, kind="ExternalOutput").ap()
        tap_ap["x1"] = nc.dram_tensor("t_x1", [D, OWN], F32, kind="ExternalOutput").ap()
        tap_ap["a"] = nc.dram_tensor("t_a", [D, OWN], F32, kind="ExternalOutput").ap()

    guards = []

    def SB(name, shape, dt):
        g = nc.sbuf_tensor(name, shape, dt)
        t = g.__enter__()
        guards.append(g)
        return t, Res()

    vec = {nm: SB("v_" + nm, [128, CH], F32) for nm in vnames + ("bq1", "bqn")}
    b1_sb = SB("v_b1", [128, FFT], F32)
    onescol = SB("onescol", [128, 1], BF16)
    zercol = SB("zercol", [128, 64], BF16)
    ones1 = SB("ones1", [1, 128], BF16)
    rw = [{k: SB("rw%d_%s" % (s, k), [128, 512], F32) for k in ("mu", "q", "t")}
          for s in range(2)]
    rw.append(rw[1])
    ones128 = SB("ones128", [128, 128], BF16)
    ones128f = SB("ones128f", [128, 128], F32)
    constvals = (1e-5, 1e-6, -0.5, -1.0)
    ccell = SB("ccell", [128, len(constvals)], F32)

    wqg = []

    def WQSB(name, shape, dt):
        g = nc.sbuf_tensor(name, shape, dt)
        t = g.__enter__()
        wqg.append(g)
        return t, Res()

    wq_sb = WQSB("wq_sb", [128, CH, D], BF16)
    wo_sb = WQSB("wo_sb", [128, CH, D], BF16)
    wqr = [Res() for _ in range(4)]
    wor = [Res() for _ in range(4)]

    kvg = []

    def KVSB(name, shape, dt):
        g = nc.sbuf_tensor(name, shape, dt)
        t = g.__enter__()
        kvg.append(g)
        return t, Res()

    zpat_sb = KVSB("zpat_s", [32, PAIRS, 128], BF16)
    ident_sb = KVSB("ident_s", [128, 128], BF16)
    kvbz = KVSB("kvbz", [128, PAIRS, 2, 2, 64], BF16)
    zrhs = KVSB("zrhs", [128, PAIRS, 4], BF16)

    PS = []
    for i in range(7):
        g = nc.psum_tensor("ps%d" % i, [128, 512], F32)
        PS.append((g.__enter__(), Res()))
        guards.append(g)
    g = nc.psum_tensor("psb", [128, 1024], BF16)
    PSB = (g.__enter__(), Res())
    guards.append(g)

    def bcast2(bi, hi_res, hi_ap, lo_ap, n, extra=(), tag="bc"):
        pb, rb = PS[bi]
        return st("PE", [
            lambda e: e.matmul(pb[:, 0:n], ones1[0][:, :], hi_ap, start=True, stop=False,
                               skip_group_check=True),
            lambda e: e.matmul(pb[:, 0:n], ones1[0][:, :], lo_ap, start=False, stop=True,
                               skip_group_check=True),
        ], reads=[ones1[1], hi_res], writes=[rb], extra=extra, tag=tag)

    def stat_mms(bi_s, bi_q, val_ap, sq_ap, val_res, sq_res, n, extra=(), tag="st",
                 off_s=0, off_q=0, ones_val=None):
        pbs, rbs = PS[bi_s]
        pbq, rbq = PS[bi_q]
        packed = bi_s == bi_q
        ov = ones_val if ones_val is not None else ones128
        ems = []
        for c in range(CH):
            ems.append(lambda e, c=c: e.matmul(
                pbs[:, off_s:off_s + n], ov[0][:, :], val_ap(c), start=(c == 0),
                stop=(c == CH - 1), skip_group_check=True))
        for c in range(CH):
            ems.append(lambda e, c=c: e.matmul(
                pbq[:, off_q:off_q + n], ones128[0][:, :], sq_ap(c),
                start=(c == 0 and not packed),
                stop=(c == CH - 1), skip_group_check=True))
        return st("PE", ems, reads=[ones128[1], val_res, sq_res], writes=[rbs, rbq],
                  extra=extra, tag=tag)

    def row_math(bi_s, bi_q, n, rsb, msb, si=0, eps=EPS_LN, tag="rm",
                 off_s=0, off_q=0):
        rws = rw[si]
        pbs, rbs = PS[bi_s]
        pbq, rbq = PS[bi_q]
        st("DVE", [
            lambda e: e.tensor_scalar_mul(rws["mu"][0][:, 0:n],
                                          pbs[:, off_s:off_s + n], 1.0 / D),
            lambda e: e.tensor_scalar_mul(rws["q"][0][:, 0:n],
                                          pbq[:, off_q:off_q + n], 1.0 / D),
            lambda e: e.tensor_mul(rws["t"][0][:, 0:n], rws["mu"][0][:, 0:n],
                                   rws["mu"][0][:, 0:n]),
            lambda e: e.tensor_sub(rws["q"][0][:, 0:n], rws["q"][0][:, 0:n],
                                   rws["t"][0][:, 0:n]),
        ], reads=[rbs, rbq], writes=[rws["mu"][1], rws["q"][1], rws["t"][1]],
            tag=tag + "1")
        st("ACT", [
            lambda e: e.activation(rws["q"][0][:, 0:n], rws["q"][0][:, 0:n], AF.Ln,
                                   bias=float(eps)),
            lambda e: e.activation(rsb[0][:, 0:n], rws["q"][0][:, 0:n], AF.Exp,
                                   scale=-0.5),
        ], reads=[rws["q"][1]], writes=[rws["q"][1], rsb[1]], tag=tag + "2")
        h3 = st("DVE", [
            lambda e: e.tensor_mul(msb[0][:, 0:n], rws["mu"][0][:, 0:n],
                                   rsb[0][:, 0:n]),
        ], reads=[rws["mu"][1], rsb[1]], writes=[msb[1]], tag=tag + "3")
        return h3

    def ln0_group(src_dram, w0, n, xhb, xqb, bs, bq, rsb, msb, tag, si=0,
                  off_s=0, off_q=0):
        st("POOL", lambda e, w0=w0, n=n: e.dma_start(
            out=xhb[0][:, :, 0:n],
            in_=src_dram[:, w0:w0 + n].rearrange("(c p) n -> p c n", p=128)),
            writes=[xhb[1]], dma=True, tag=tag + "lh")
        st("ACT", lambda e, n=n: e.activation(
            xqb[0][:, :, 0:n], xhb[0][:, :, 0:n], AF.Square),
            reads=[xhb[1]], writes=[xqb[1]], tag=tag + "sq")
        stat_mms(bs, bq, lambda c, n=n: xhb[0][:, c, 0:n],
                 lambda c, n=n: xqb[0][:, c, 0:n], xhb[1], xqb[1], n, tag=tag + "sm",
                 off_s=off_s, off_q=off_q)
        row_math(bs, bq, n, rsb, msb, si=si, tag=tag + "rm", off_s=off_s, off_q=off_q)

    h_const = st("DVE", [
        (lambda e, i=i, v=v: e.memset(ccell[0][:, i:i + 1], float(v)))
        for i, v in enumerate(constvals)
    ], writes=[ccell[1]], tag="consts")
    for i, v in enumerate(constvals):
        nc.const_aps.register(F32, float(v), ccell[0][:, i:i + 1])
    B.gate = [h_const]

    p0g = []

    def P0SB(name, shape, dt):
        g = nc.sbuf_tensor(name, shape, dt)
        t = g.__enter__()
        p0g.append(g)
        return t, Res()

    bkrow = P0SB("bkrow_s", [1, D], F32)
    bvrow = P0SB("bvrow_s", [1, D], F32)
    rowh = P0SB("rowh", [1, D], BF16)
    rowl = P0SB("rowl", [1, D], BF16)
    bkb = P0SB("bkb", [128, D], F32)
    bvb = P0SB("bvb", [128, D], F32)

    st("SP", [
        (lambda e, nm=nm: e.dma_start(out=vec[nm][0][:, :], in_=vd[nm]))
        for nm in vnames
    ] + [
        lambda e: e.dma_start(out=b1_sb[0][:, :], in_=b1c_d),
        lambda e: e.dma_start(out=bkrow[0][:, :], in_=bk_row_d),
        lambda e: e.dma_start(out=bvrow[0][:, :], in_=bv_row_d),
    ], writes=[vec[nm][1] for nm in vnames] + [b1_sb[1], bkrow[1], bvrow[1]],
        dma=True, tag="p0vec")
    st("POOL", [
        lambda e: e.dma_start(out=zpat_sb[0][:, :, :], in_=zpat_d),
        lambda e: e.dma_start(out=ident_sb[0][:, :], in_=ident_d),
    ], writes=[zpat_sb[1], ident_sb[1]], dma=True, tag="p0zp")
    st("DVE", [
        lambda e: e.memset(onescol[0][:, :], 1.0),
        lambda e: e.memset(zercol[0][:, :], 0.0),
        lambda e: e.memset(ones1[0][:, :], 1.0),
        lambda e: e.memset(ones128[0][:, :], 1.0),
        lambda e: e.memset(ones128f[0][:, :], 1.0),
    ], writes=[onescol[1], zercol[1], ones1[1], ones128[1], ones128f[1]],
        tag="p0ones")
    st("DVE", [
        lambda e: e.tensor_scalar_add(vec["bq1"][0][:, :], vec["bq"][0][:, :], 1.0),
        lambda e: e.tensor_scalar_mul(vec["bqn"][0][:, :], vec["bq"][0][:, :], -1.0),
    ], reads=[vec["bq"][1]], writes=[vec["bq1"][1], vec["bqn"][1]], tag="p0bq1")
    for src_row, dstt in ((bkrow, bkb), (bvrow, bvb)):
        st("DVE", [
            lambda e, src_row=src_row: e.tensor_copy(rowh[0][:, :], src_row[0][:, :]),
            lambda e, src_row=src_row: e.tensor_sub(rowl[0][:, :], src_row[0][:, :],
                                                    rowh[0][:, :]),
        ], reads=[src_row[1]], writes=[rowh[1], rowl[1]], tag="p0r")
        for half in range(2):
            bcast2(6, rowh[1], rowh[0][:, 512 * half:512 * (half + 1)],
                   rowl[0][:, 512 * half:512 * (half + 1)], 512,
                   extra=rowl[1].rdep(), tag="p0bc")
            st("DVE", lambda e, dstt=dstt, half=half: e.tensor_copy(
                dstt[0][:, 512 * half:512 * (half + 1)], PS[6][0][:, 0:512]),
                reads=[PS[6][1]], writes=[dstt[1]], tag="p0bcc")

    a1g = []

    def A1SB(name, shape, dt):
        g = nc.sbuf_tensor(name, shape, dt)
        t = g.__enter__()
        a1g.append(g)
        return t, Res()

    wk_sb = A1SB("wk_sb", [128, CH, D], BF16)
    wv_sb = A1SB("wv_sb", [128, CH, D], BF16)
    xh1 = [A1SB("a1xh%d" % i, [128, CH, 256], BF16) for i in range(2)]
    xq1 = [A1SB("a1xq%d" % i, [128, CH, 256], BF16) for i in range(2)]
    ln0h1 = [A1SB("a1ln%d" % i, [128, CH, 256], BF16) for i in range(2)]
    ln0h1b = [Res() for _ in range(2)]
    scr_a = A1SB("a1scr", [128, CH, 256], F32)
    scr_ab = Res()
    rsb1 = [A1SB("a1rsb%d" % i, [128, 256], F32) for i in range(2)]
    msb1 = [A1SB("a1msb%d" % i, [128, 256], F32) for i in range(2)]
    kf1 = A1SB("a1kf", [128, D], F32)
    km1 = A1SB("a1km", [128, D], F32)
    ke1 = A1SB("a1ke", [128, D], F32)
    Kt1 = [A1SB("a1K%d" % i, [128, D], BF16) for i in range(2)]
    Vt1 = [A1SB("a1V%d" % i, [128, H, 65], BF16) for i in range(2)]
    kvf = A1SB("kvf", [128, PAIRS, 65], F32)
    kvb = A1SB("kvb", [128, PAIRS, 2, 65], BF16)

    st("POOL", [lambda e: e.dma_start(out=wk_sb[0][:, :, :],
                 in_=wdr["wk"].rearrange("(c p) d -> p c d", p=128))],
       writes=[wk_sb[1]], dma=True, dsem="qWA", tag="a1wk")
    st("POOL", [lambda e: e.dma_start(out=wv_sb[0][:, :, :],
                 in_=wdr["wv"].rearrange("(c p) d -> p c d", p=128))],
       writes=[wv_sb[1]], dma=True, dsem="qWA", tag="a1wv")
    st("DVE", [lambda e, i=i: e.memset(Vt1[i][0][:, :, 64:65], 1.0) for i in range(2)],
       writes=[Vt1[0][1], Vt1[1][1]], tag="a1vone")

    ems = []
    for b in (4, 5):
        for cg in (0, 64):
            ems.append(lambda e, b=b, cg=cg: e.matmul(
                PS[b][0][cg:cg + 64, 0:260], zercol[0][:, :], wk_sb[0][:, 0, 0:260],
                start=True, stop=True, skip_group_check=True, tile_position=(0, cg)))
    st("PE", ems, reads=[zercol[1], wk_sb[1]], writes=[PS[4][1], PS[5][1]], tag="a1kvz")

    def a1_head(gi):
        w0, n = AG[gi]
        bi = gi % 2
        ln0_group(xT, w0, n, xh1[bi], xq1[bi], 6, 6, rsb1[bi], msb1[bi],
                  tag="a1s%d" % gi, si=0, off_s=0, off_q=256)
        for eng, c0, c1, wrs in (("DVE", 0, 5, (scr_a[1], ln0h1[bi][1])),
                                 ("POOL", 5, CH, (scr_ab, ln0h1b[bi]))):
            ems = []
            for c in range(c0, c1):
                ems.append(lambda e, bi=bi, c=c, n=n: e.tensor_mul(
                    scr_a[0][:, c, 0:n], xh1[bi][0][:, c, 0:n], rsb1[bi][0][:, 0:n]))
                ems.append(lambda e, bi=bi, c=c, n=n: e.tensor_sub(
                    ln0h1[bi][0][:, c, 0:n], scr_a[0][:, c, 0:n], msb1[bi][0][:, 0:n]))
            st(eng, ems, reads=[xh1[bi][1], rsb1[bi][1], msb1[bi][1]],
               writes=list(wrs), tag="a1ln%s%d" % (eng, gi))

    def a1_main(gi):
        w0, n = AG[gi]
        bi = gi % 2
        for tt in range(2):
            t0 = tt * 128
            ti = tt % 2
            ems = []
            for c in range(CH):
                for dh in range(2):
                    ems.append(lambda e, c=c, dh=dh, bi=bi, t0=t0: e.matmul(
                        PS[dh][0][:, 0:512], ln0h1[bi][0][:, c, t0:t0 + 128],
                        wk_sb[0][:, c, dh * 512:(dh + 1) * 512],
                        start=(c == 0), stop=(c == CH - 1), skip_group_check=True))
            st("PE", ems, reads=[ln0h1[bi][1], ln0h1b[bi], wk_sb[1], wv_sb[1]],
               writes=[PS[0][1], PS[1][1]], tag="a1km%d_%d" % (gi, tt))
            ems = []
            for c in range(CH):
                for dh in range(2):
                    ems.append(lambda e, c=c, dh=dh, bi=bi, t0=t0: e.matmul(
                        PS[2 + dh][0][:, 0:512], ln0h1[bi][0][:, c, t0:t0 + 128],
                        wv_sb[0][:, c, dh * 512:(dh + 1) * 512],
                        start=(c == 0), stop=(c == CH - 1), skip_group_check=True))
            st("PE", ems, reads=[ln0h1[bi][1], ln0h1b[bi], wv_sb[1], wk_sb[1]],
               writes=[PS[2][1], PS[3][1]], tag="a1vm%d_%d" % (gi, tt))
            st("DVE", [
                lambda e, dh=dh: e.tensor_add(kf1[0][:, dh * 512:(dh + 1) * 512],
                                              PS[dh][0][:, 0:512],
                                              bkb[0][:, dh * 512:(dh + 1) * 512])
                for dh in range(2)
            ], reads=[PS[0][1], PS[1][1], bkb[1]], writes=[kf1[1]], tag="a1kf%d_%d" % (gi, tt))
            st("DVE", [lambda e: e.tensor_scalar_min(km1[0][:, :], kf1[0][:, :], 0.0)],
               reads=[kf1[1]], writes=[km1[1]], tag="a1km2%d_%d" % (gi, tt))
            st("ACT", [lambda e: e.activation(ke1[0][:, :], km1[0][:, :], AF.Exp)],
               reads=[km1[1]], writes=[ke1[1]], tag="a1ke%d_%d" % (gi, tt))
            st("DVE", [lambda e, ti=ti: e.scalar_tensor_tensor(
                Kt1[ti][0][:, :], kf1[0][:, :], 1.0, ke1[0][:, :], ALU.add, ALU.max)],
                reads=[kf1[1], ke1[1]], writes=[Kt1[ti][1]], tag="a1K%d_%d" % (gi, tt))
            if taps == 2 and gi == 0 and tt == 0:
                st("POOL", [
                    lambda e: e.dma_start(out=tap_ap["K"], in_=Kt1[0][0][:, :]),
                    lambda e: e.dma_start(out=tap_ap["kf"], in_=kf1[0][:, :]),
                ], reads=[Kt1[0][1], kf1[1]], dma=True, tag="tapa1K")
            st("DVE", [
                lambda e, dh=dh, ti=ti: e.tensor_add(
                    Vt1[ti][0][:, dh * 8:(dh + 1) * 8, 0:64],
                    PS[2 + dh][0][:, 0:512].rearrange("p (h m) -> p h m", h=8),
                    bvb[0][:, dh * 512:(dh + 1) * 512].rearrange("p (h m) -> p h m", h=8))
                for dh in range(2)
            ], reads=[PS[2][1], PS[3][1], bvb[1]], writes=[Vt1[ti][1]],
                tag="a1V%d_%d" % (gi, tt))
            ems = []
            for p in range(PAIRS):
                bidx = 4 + p // 4
                off = (p % 4) * 65
                last_kv = (gi == len(AG) - 1 and tt == 1)
                for hh in range(2):
                    hd = 2 * p + hh
                    ems.append(lambda e, bidx=bidx, off=off, hd=hd, hh=hh, ti=ti,
                               last_kv=last_kv:
                               e.matmul(PS[bidx][0][64 * hh:64 * (hh + 1), off:off + 65],
                                        Kt1[ti][0][:, 64 * hd:64 * (hd + 1)],
                                        Vt1[ti][0][:, hd, :],
                                        start=False, stop=last_kv, skip_group_check=True,
                                        tile_position=(0, 64 * hh)))
            st("PE", ems, reads=[Kt1[ti][1], Vt1[ti][1]], writes=[PS[4][1], PS[5][1]],
               tag="a1kv%d_%d" % (gi, tt))

    a1_head(0)
    a1_head(1)
    for j in range(4):
        st("POOL", [lambda e, j=j: e.dma_start(
            out=wq_sb[0][:, :, j * 256:(j + 1) * 256],
            in_=wdr["wq"][:, j * 256:(j + 1) * 256].rearrange(
                "(c p) d -> p c d", p=128))],
            writes=[wqr[j]], dma=True, dsem="qWB", tag="pfwq%d" % j)
    for j in range(4):
        st("POOL", [lambda e, j=j: e.dma_start(
            out=wo_sb[0][:, :, j * 256:(j + 1) * 256],
            in_=wdr["wo"][:, j * 256:(j + 1) * 256].rearrange(
                "(c p) d -> p c d", p=128))],
            writes=[wor[j]], dma=True, dsem="qWB", tag="pfwo%d" % j)
    for gi in range(len(AG)):
        a1_main(gi)
        if gi + 2 < len(AG):
            a1_head(gi + 2)
        if taps == 2 and gi == 0:
            st("POOL", [
                lambda e: e.dma_start(out=tap_ap["rsb"][0], in_=rsb1[0][:, :]),
                lambda e: e.dma_start(out=tap_ap["ln0h"], in_=ln0h1[0][0][:, :, :]),
            ], reads=[rsb1[1], ln0h1[0][1]], dma=True, tag="tapa1")

    st("DVE", [
        lambda e: e.tensor_copy(kvf[0][:, 0:4, :].rearrange("p a b -> p (a b)"),
                                PS[4][0][:, 0:260]),
        lambda e: e.tensor_copy(kvf[0][:, 4:8, :].rearrange("p a b -> p (a b)"),
                                PS[5][0][:, 0:260]),
    ], reads=[PS[4][1], PS[5][1]], writes=[kvf[1]], tag="a2ev")
    st("DVE", [
        lambda e: e.tensor_copy(kvb[0][:, :, 0, :], kvf[0][:, :, :]),
        lambda e: e.tensor_sub(kvb[0][:, :, 1, :], kvf[0][:, :, :], kvb[0][:, :, 0, :]),
    ], reads=[kvf[1]], writes=[kvb[1]], tag="a2hl")
    st("DVE", [
        lambda e: e.memset(kvbz[0][:, :, :, :, :].rearrange('p a b c d -> p (a b c d)'), 0.0),
        lambda e: e.tensor_copy(kvbz[0][0:64, :, :, 0, :], kvb[0][0:64, :, :, 0:64]),
        lambda e: e.tensor_copy(kvbz[0][64:128, :, :, 1, :], kvb[0][64:128, :, :, 0:64]),
    ], reads=[kvb[1]], writes=[kvbz[1]], tag="a2kvz")
    ems = [lambda e: e.memset(zrhs[0][:, :, :], 0.0)]
    for p in range(PAIRS):
        for hl in range(2):
            ems.append(lambda e, p=p, hl=hl: e.tensor_copy(
                zrhs[0][0:64, p, hl:hl + 1], kvb[0][0:64, p, hl, 64:65]))
            ems.append(lambda e, p=p, hl=hl: e.tensor_copy(
                zrhs[0][64:128, p, 2 + hl:3 + hl], kvb[0][64:128, p, hl, 64:65]))
    st("DVE", ems, reads=[kvb[1]], writes=[zrhs[1]], tag="a2zr")
    if taps >= 1:
        st("POOL", [lambda e: e.dma_start(
            out=tap_ap["kvbz"],
            in_=kvbz[0][:, :, :, :, :].rearrange("p a b c d -> p (a b c d)"))],
            reads=[kvbz[1]], dma=True, tag="tapkvz")
    if taps >= 1:
        st("SP", [lambda e: e.dma_start(
            out=tap_ap["kvb"][:, :], in_=kvf[0][:, :, :].rearrange("p a b -> p (a b)"))],
            reads=[kvf[1]], dma=True, tag="tapkv")

    def finish():
        st("SP", [lambda e: e.nop()],
           extra=[(s, B.cnt.get(s, 0)) for s in
                  ("qSP", "qACT", "qGP", "qWA", "qWB", "qW1", "qW2", "qWr",
                   "qWk2", "qWv2", "sPE", "sACT", "sDVE", "sPOOL")], tag="finX")

    def barrier(name):
        hs = [st(eng, lambda e: e.nop(),
                 extra=[(s, B.cnt.get(s, 0)) for s in
                        ("sDVE", "sPE", "sACT", "sPOOL", "sSP", "qSP", "qACT",
                         "qGP", "qWA", "qWB", "qW1", "qW2", "qWr",
                         "qWk2", "qWv2")],
                 tag=name + eng) for eng in ENGS]
        B.gate = hs
        return hs

    if upto == "A":
        finish()
        return _replay(nc, B)
    barh = barrier("barC1")
    for g in reversed(p0g + a1g):
        g.__exit__(None, None, None)

    c1g = []

    def C1SB(name, shape, dt):
        g = nc.sbuf_tensor(name, shape, dt)
        t = g.__enter__()
        c1g.append(g)
        return t, Res()

    bsc = [C1SB("c1bs%d" % i, [128, CH, 416], BF16) for i in range(2)]
    sqc = [C1SB("c1sq%d" % i, [128, CH, 416], BF16) for i in range(2)]
    ln0f = [C1SB("c1l0f%d" % i, [128, CH, 416], F32) for i in range(2)]
    ln0fb = [Res() for _ in range(2)]
    ln0h = [C1SB("c1l0h%d" % i, [128, CH, 416], BF16) for i in range(2)]
    rsbc = [C1SB("c1rsb%d" % i, [128, 416], F32) for i in range(2)]
    msbc = [C1SB("c1msb%d" % i, [128, 416], F32) for i in range(2)]
    eqm = C1SB("c1eqm", [128, 416], F32)
    eqc = eqm
    Qt = [C1SB("c1Q%d" % i, [128, PAIRS, 416], BF16) for i in range(2)]
    ztc = C1SB("c1zt", [128, 4, 16], F32)
    zthl = C1SB("c1zthl", [128, 4, 2, 16], BF16)
    zfc = [C1SB("c1zf%d" % i, [32, 416], BF16) for i in range(2)]
    zbc8 = C1SB("c1zb8", [128, PAIRS, 416], F32)
    otc = C1SB("c1ot", [128, CH, 416], BF16)
    of_ = [C1SB("c1of%d" % i, [128, 416], F32) for i in range(2)]
    qf_ = [C1SB("c1qf%d" % i, [128, 416], F32) for i in range(2)]
    uf = [C1SB("c1uf%d" % i, [128, 416], F32) for i in range(2)]
    x1c = C1SB("c1x1", [128, CH, 416], F32)
    x1cb = Res()

    rsbc1 = [C1SB("c1rsb1%d" % i, [128, 416], F32) for i in range(2)]
    msbc1 = [C1SB("c1msb1%d" % i, [128, 416], F32) for i in range(2)]

    def c1_head(gi):
        w0, n = CG[gi]
        bi = gi % 2
        ln0_group(xTo, w0, n, bsc[bi], sqc[bi], 4, 5, rsbc[bi], msbc[bi],
                  tag="c1s%d" % gi, si=0)
        for eng, c0, c1, wrs in (("DVE", 0, 5, (ln0f[bi][1],)),
                                 ("POOL", 5, CH, (ln0fb[bi],))):
            ems = []
            for c in range(c0, c1):
                ems.append(lambda e, bi=bi, c=c, n=n: e.tensor_mul(
                    ln0f[bi][0][:, c, 0:n], bsc[bi][0][:, c, 0:n], rsbc[bi][0][:, 0:n]))
                ems.append(lambda e, bi=bi, c=c, n=n: e.tensor_sub(
                    ln0f[bi][0][:, c, 0:n], ln0f[bi][0][:, c, 0:n], msbc[bi][0][:, 0:n]))
            st(eng, ems, reads=[bsc[bi][1], rsbc[bi][1], msbc[bi][1]],
               writes=list(wrs), tag="c1l0f%s%d" % (eng, gi))
        st("ACT", [lambda e, bi=bi, n=n: e.activation(
            ln0h[bi][0][:, :, 0:n], ln0f[bi][0][:, :, 0:n], AF.Copy)],
            reads=[ln0f[bi][1], ln0fb[bi]], writes=[ln0h[bi][1]], tag="c1l0h%d" % gi)

    def c1_main(gi):
        w0, n = CG[gi]
        bi = gi % 2
        nch = (n + 127) // 128
        for dt in range(PAIRS):
            bidx = dt % 2
            ems = [lambda e, c=c, dt=dt, bidx=bidx, n=n: e.matmul(
                PS[bidx][0][:, 0:n], wq_sb[0][:, c, dt * 128:(dt + 1) * 128],
                ln0h[bi][0][:, c, 0:n], start=(c == 0), stop=(c == CH - 1),
                skip_group_check=True) for c in range(CH)]
            st("PE", ems, reads=[wqr[3], wor[3], ln0h[bi][1]], writes=[PS[bidx][1]],
               tag="c1q%d_%d" % (gi, dt))
            st("ACT", [lambda e, bidx=bidx, dt=dt, n=n: e.activation(
                qf_[dt % 2][0][:, 0:n], PS[bidx][0][:, 0:n], AF.Identity,
                bias=vec["bq"][0][:, dt:dt + 1])],
                reads=[PS[bidx][1], vec["bq"][1]], writes=[qf_[dt % 2][1]],
                tag="c1qe%d_%d" % (gi, dt))
            st("DVE", [lambda e, dt=dt, n=n: e.tensor_scalar_min(
                eqm[0][:, 0:n], qf_[dt % 2][0][:, 0:n], 0.0)],
                reads=[qf_[dt % 2][1]], writes=[eqm[1]],
                tag="c1qm%d_%d" % (gi, dt))
            st("ACT", [lambda e, n=n: e.activation(
                eqc[0][:, 0:n], eqm[0][:, 0:n], AF.Exp)],
                reads=[eqm[1]], writes=[eqc[1]],
                tag="c1eq%d_%d" % (gi, dt))
            st("DVE", [lambda e, dt=dt, bi=bi, n=n: e.scalar_tensor_tensor(
                Qt[bi][0][:, dt, 0:n], qf_[dt % 2][0][:, 0:n], 1.0,
                eqc[0][:, 0:n], ALU.add, ALU.max)],
                reads=[qf_[dt % 2][1], eqc[1]], writes=[Qt[bi][1]],
                tag="c1Q%d_%d" % (gi, dt))
        ems = []
        for ch in range(nch):
            t0 = ch * 128
            m = min(128, n - t0)
            for p in range(PAIRS):
                for hl in range(2):
                    ems.append(lambda e, ch=ch, t0=t0, m=m, p=p, hl=hl, bi=bi: e.matmul(
                        PS[6][0][0:m, ch * 16 + 2 * p:ch * 16 + 2 * p + 2],
                        Qt[bi][0][:, p, t0:t0 + m], zrhs[0][:, p, hl::2],
                        start=(hl == 0), stop=(hl == 1), skip_group_check=True))
        st("PE", ems, reads=[Qt[bi][1], zrhs[1]], writes=[PS[6][1]], tag="c1zd%d" % gi)
        st("ACT", [
            lambda e, nch=nch: e.activation(
                ztc[0][:, 0:nch, :].rearrange("p a b -> p (a b)"),
                PS[6][0][:, 0:nch * 16], AF.Ln, bias=float(EPS_Z)),
            lambda e, nch=nch: e.activation(
                ztc[0][:, 0:nch, :].rearrange("p a b -> p (a b)"),
                ztc[0][:, 0:nch, :].rearrange("p a b -> p (a b)"), AF.Exp, scale=-1.0),
        ], reads=[PS[6][1]], writes=[ztc[1]], tag="c1zl%d" % gi)
        st("DVE", [
            lambda e, nch=nch: e.tensor_copy(zthl[0][:, 0:nch, 0, :], ztc[0][:, 0:nch, :]),
            lambda e, nch=nch: e.tensor_sub(zthl[0][:, 0:nch, 1, :], ztc[0][:, 0:nch, :],
                                            zthl[0][:, 0:nch, 0, :]),
        ], reads=[ztc[1]], writes=[zthl[1]], tag="c1zh%d" % gi)
        for ch in range(nch):
            t0 = ch * 128
            m = min(128, n - t0)
            st("PE", [lambda e, ch=ch: e.transpose(
                PSB[0][0:32, 0:128],
                zthl[0][:, ch, :, :].rearrange("p a b -> p (a b)"),
                ident_sb[0][:, :])],
                reads=[zthl[1], ident_sb[1]], writes=[PSB[1]], tag="c1zt%d_%d" % (gi, ch))
            st("ACT", [lambda e, t0=t0, m=m, bi=bi: e.activation(
                zfc[bi][0][:, t0:t0 + m], PSB[0][0:32, 0:m], AF.Copy)],
                reads=[PSB[1]], writes=[zfc[bi][1]], tag="c1ze%d_%d" % (gi, ch))
        for p in range(PAIRS):
            zi = p % 2
            st("PE", [lambda e, p=p, zi=zi, n=n, bi=bi: e.matmul(
                PS[zi][0][:, 0:n], zpat_sb[0][:, p, :], zfc[bi][0][:, 0:n],
                start=True, stop=True, skip_group_check=True)],
                reads=[zpat_sb[1], zfc[bi][1]], writes=[PS[zi][1]], tag="c1zb%d_%d" % (gi, p))
            st("ACT", [lambda e, p=p, zi=zi, n=n: e.activation(
                zbc8[0][:, p, 0:n], PS[zi][0][:, 0:n], AF.Copy)],
                reads=[PS[zi][1]], writes=[zbc8[1]], tag="c1zc%d_%d" % (gi, p))
        for p in range(PAIRS):
            oi = 2 + p % 2
            ems = []
            for hh in range(2):
                for hl in range(2):
                    ems.append(lambda e, oi=oi, hh=hh, hl=hl, p=p, n=n, bi=bi: e.matmul(
                        PS[oi][0][64 * hh:64 * (hh + 1), 0:n],
                        kvbz[0][:, p, hl, hh, :],
                        Qt[bi][0][:, p, 0:n],
                        start=(hl == 0), stop=(hl == 1),
                        skip_group_check=True, tile_position=(0, 64 * hh)))
            st("PE", ems, reads=[kvbz[1], Qt[bi][1]], writes=[PS[oi][1]],
               tag="c1o%d_%d" % (gi, p))
            st("ACT", [lambda e, oi=oi, p=p, n=n: e.activation(
                of_[p % 2][0][:, 0:n], PS[oi][0][:, 0:n], AF.Copy)],
                reads=[PS[oi][1]], writes=[of_[p % 2][1]], tag="c1of%d_%d" % (gi, p))
            st("POOL", [lambda e, p=p, n=n: e.tensor_mul(
                otc[0][:, p, 0:n], of_[p % 2][0][:, 0:n], zbc8[0][:, p, 0:n])],
                reads=[of_[p % 2][1], zbc8[1]], writes=[otc[1]], tag="c1oe%d_%d" % (gi, p))
        for dt in range(PAIRS):
            bidx = dt % 2
            ems = [lambda e, c=c, dt=dt, bidx=bidx, n=n: e.matmul(
                PS[bidx][0][:, 0:n], wo_sb[0][:, c, dt * 128:(dt + 1) * 128],
                otc[0][:, c, 0:n], start=(c == 0), stop=(c == CH - 1),
                skip_group_check=True) for c in range(CH)]
            st("PE", ems, reads=[wqr[3], wor[3], otc[1]], writes=[PS[bidx][1]],
               tag="c1wo%d_%d" % (gi, dt))
            st("ACT", [lambda e, bidx=bidx, dt=dt, n=n: e.activation(
                uf[dt % 2][0][:, 0:n], PS[bidx][0][:, 0:n], AF.Identity,
                bias=vec["bo"][0][:, dt:dt + 1])],
                reads=[PS[bidx][1], vec["bo"][1]], writes=[uf[dt % 2][1]],
                tag="c1uf%d_%d" % (gi, dt))
            st("POOL", [lambda e, dt=dt, bi=bi, n=n: e.tensor_add(
                ln0f[bi][0][:, dt, 0:n], ln0f[bi][0][:, dt, 0:n],
                uf[dt % 2][0][:, 0:n])],
                reads=[uf[dt % 2][1], ln0fb[bi]], writes=[ln0f[bi][1]],
                tag="c1u%d_%d" % (gi, dt))
        st("ACT", [lambda e, bi=bi, n=n: e.activation(
            bsc[bi][0][:, :, 0:n], ln0f[bi][0][:, :, 0:n], AF.Copy)],
            reads=[ln0f[bi][1], ln0fb[bi]], writes=[bsc[bi][1]], tag="c1ub%d" % gi)
        st("ACT", [lambda e, bi=bi, n=n: e.activation(
            sqc[bi][0][:, :, 0:n], bsc[bi][0][:, :, 0:n], AF.Square)],
            reads=[bsc[bi][1]], writes=[sqc[bi][1]], tag="c1us%d" % gi)
        stat_mms(2, 3, lambda c, bi=bi, n=n: bsc[bi][0][:, c, 0:n],
                 lambda c, bi=bi, n=n: sqc[bi][0][:, c, 0:n],
                 bsc[bi][1], sqc[bi][1], n, tag="c1sm%d" % gi)

    def c1_tail(gi):
        w0, n = CG[gi]
        bi = gi % 2
        row_math(2, 3, n, rsbc1[bi], msbc1[bi], si=1, tag="c1rm%d" % gi)
        for eng, c0, c1, wrs in (("DVE", 0, 5, (x1c[1],)), ("POOL", 5, CH, (x1cb,))):
            ems = []
            for c in range(c0, c1):
                ems.append(lambda e, bi=bi, c=c, n=n: e.tensor_mul(
                    x1c[0][:, c, 0:n], ln0f[bi][0][:, c, 0:n], rsbc1[bi][0][:, 0:n]))
                ems.append(lambda e, bi=bi, c=c, n=n: e.tensor_sub(
                    x1c[0][:, c, 0:n], x1c[0][:, c, 0:n], msbc1[bi][0][:, 0:n]))
            st(eng, ems, reads=[ln0f[bi][1], ln0fb[bi], rsbc1[bi][1], msbc1[bi][1]],
               writes=list(wrs), tag="c1xa%s%d" % (eng, gi))
        st("ACT", [lambda e, c=c, n=n: e.activation(
            x1c[0][:, c, 0:n], x1c[0][:, c, 0:n], AF.Identity,
            bias=vec["be1"][0][:, c:c + 1], scale=vec["g1"][0][:, c:c + 1])
            for c in range(CH)],
            reads=[vec["be1"][1], vec["g1"][1], x1c[1], x1cb],
            writes=[x1c[1], x1cb], tag="c1xg%d" % gi)
        st("ACT", [lambda e, w0=w0, n=n: e.dma_start(
            out=x1f_d[:, w0:w0 + n].rearrange("(c p) n -> p c n", p=128),
            in_=x1c[0][:, :, 0:n])], reads=[x1c[1]], dma=True, tag="c1d1%d" % gi)
        st("POOL", [lambda e, w0=w0, n=n: e.dma_start(
            out=x1h_d[:, w0:w0 + n].rearrange("(c p) n -> p c n", p=128),
            in_=x1c[0][:, :, 0:n])], reads=[x1c[1]], dma=True, tag="c1d2%d" % gi)
        if taps >= 1:
            st("SP", [lambda e, w0=w0, n=n: e.dma_start(
                out=tap_ap["x1"][:, w0:w0 + n].rearrange("(c p) n -> p c n", p=128),
                in_=x1c[0][:, :, 0:n])], reads=[x1c[1]], dma=True, tag="tapx1%d" % gi)

    c1_head(0)
    for gi in range(len(CG)):
        c1_main(gi)
        if gi + 1 < len(CG):
            c1_head(gi + 1)
        c1_tail(gi)

    if upto == "C1":
        finish()
        return _replay(nc, B)
    barh2 = barrier("barC2")
    for g in reversed(c1g):
        g.__exit__(None, None, None)
    for g in reversed(kvg):
        g.__exit__(None, None, None)
    for g in reversed(wqg):
        g.__exit__(None, None, None)

    c2ag = []

    def C2ASB(name, shape, dt):
        g = nc.sbuf_tensor(name, shape, dt)
        t = g.__enter__()
        c2ag.append(g)
        return t, Res()

    c2wg = []

    def C2WSB(name, shape, dt):
        g = nc.sbuf_tensor(name, shape, dt)
        t = g.__enter__()
        c2wg.append(g)
        return t, Res()

    w2_sb = C2WSB("w2_sb", [128, FFT, D], BF16)
    w2q = [Res() for _ in range(4)]

    w1_sb = C2ASB("w1_sb", [128, CH, FF], BF16)
    w1q = [Res() for _ in range(8)]
    x1hc = [C2ASB("c2x1h%d" % i, [128, CH, 257], BF16) for i in range(2)]
    h1c = [C2ASB("c2h1%d" % i, [128, FFT, 257], BF16) for i in range(2)]

    for q in range(8):
        st("POOL", [lambda e, q=q: e.dma_start(
            out=w1_sb[0][:, :, q * 512:(q + 1) * 512],
            in_=wdr["w1"][:, q * 512:(q + 1) * 512].rearrange("(c p) f -> p c f", p=128))],
            writes=[w1q[q]], dma=True, dsem="qW1", tag="c2aw%d" % q)
    for q in range(4):
        st("POOL", [lambda e, q=q: e.dma_start(
            out=w2_sb[0][:, :, q * 256:(q + 1) * 256],
            in_=wdr["w2"][:, q * 256:(q + 1) * 256].rearrange("(f p) d -> p f d", p=128))],
            writes=[w2q[q]], dma=True, dsem="qW2", tag="c2bw%d" % q)

    for gi, (w0, n) in enumerate(C2G):
        bi = gi % 2
        st("SP", [lambda e, bi=bi, w0=w0, n=n: e.dma_start(
            out=x1hc[bi][0][:, :, 0:n],
            in_=x1h_d[:, w0:w0 + n].rearrange("(c p) n -> p c n", p=128))],
            writes=[x1hc[bi][1]], dma=True, tag="c2alh%d" % gi)
        for f in range(FFT):
            bidx = f % 4
            ems = [lambda e, c=c, f=f, bidx=bidx, bi=bi, n=n: e.matmul(
                PS[bidx][0][:, 0:n], w1_sb[0][:, c, f * 128:(f + 1) * 128],
                x1hc[bi][0][:, c, 0:n], start=(c == 0), stop=(c == CH - 1),
                skip_group_check=True) for c in range(CH)]
            st("PE", ems, reads=[w1q[7], x1hc[bi][1]], writes=[PS[bidx][1]],
               tag="c2w1_%d_%d" % (gi, f))
            st("ACT", [lambda e, f=f, bidx=bidx, bi=bi, n=n: e.activation(
                h1c[bi][0][:, f, 0:n], PS[bidx][0][:, 0:n], AF.Gelu,
                bias=b1_sb[0][:, f:f + 1])],
                reads=[PS[bidx][1], b1_sb[1]], writes=[h1c[bi][1]], tag="c2g%d_%d" % (gi, f))
        st("ACT", [lambda e, bi=bi, w0=w0, n=n: e.dma_start(
            out=h1_d[:, w0:w0 + n].rearrange("(f p) n -> p f n", p=128),
            in_=h1c[bi][0][:, :, 0:n])], reads=[h1c[bi][1]], dma=True, tag="c2ast%d" % gi)

    barh2b = barrier("barC2b")
    for g in reversed(c2ag):
        g.__exit__(None, None, None)

    c2g = []

    def C2SB(name, shape, dt):
        g = nc.sbuf_tensor(name, shape, dt)
        t = g.__enter__()
        c2g.append(g)
        return t, Res()

    h1b = [C2SB("c2bh%d" % i, [128, FFT, 257], BF16) for i in range(2)]
    x1fc = [C2SB("c2x1f%d" % i, [128, CH, 257], F32) for i in range(2)]
    vc = C2SB("c2v", [128, CH, 257], F32)
    tc = C2SB("c2t", [128, CH, 257], F32)
    ac = tc
    vf = [C2SB("c2vf%d" % i, [128, 257], F32) for i in range(2)]
    vbc = C2SB("c2vb", [128, CH, 257], BF16)
    vsqc = C2SB("c2vsq", [128, CH, 257], BF16)
    rsb2 = C2SB("c2rsb", [128, 257], F32)
    msb2 = C2SB("c2msb", [128, 257], F32)

    def c2b_head(gi):
        w0, n = C2G[gi]
        bi = gi % 2
        st("SP", [
            lambda e, bi=bi, w0=w0, n=n: e.dma_start(
                out=h1b[bi][0][:, :, 0:n],
                in_=h1_d[:, w0:w0 + n].rearrange("(f p) n -> p f n", p=128)),
            lambda e, bi=bi, w0=w0, n=n: e.dma_start(
                out=x1fc[bi][0][:, :, 0:n],
                in_=x1f_d[:, w0:w0 + n].rearrange("(c p) n -> p c n", p=128)),
        ], writes=[h1b[bi][1], x1fc[bi][1]], dma=True, tag="c2bl%d" % gi)

    def c2b_main(gi):
        w0, n = C2G[gi]
        bi = gi % 2
        for dt in range(CH):
            bidx = 4 + dt % 2
            ems = [lambda e, f=f, dt=dt, bidx=bidx, bi=bi, n=n: e.matmul(
                PS[bidx][0][:, 0:n], w2_sb[0][:, f, dt * 128:(dt + 1) * 128],
                h1b[bi][0][:, f, 0:n], start=(f == 0), stop=(f == FFT - 1),
                skip_group_check=True) for f in range(FFT)]
            st("PE", ems, reads=[w2q[3], h1b[bi][1]], writes=[PS[bidx][1]],
               tag="c2w2_%d_%d" % (gi, dt))
            st("ACT", [lambda e, bidx=bidx, dt=dt, n=n: e.activation(
                vf[dt % 2][0][:, 0:n], PS[bidx][0][:, 0:n], AF.Identity,
                bias=vec["b2"][0][:, dt:dt + 1])],
                reads=[PS[bidx][1], vec["b2"][1]], writes=[vf[dt % 2][1]],
                tag="c2ve%d_%d" % (gi, dt))
            st("DVE", [lambda e, dt=dt, bi=bi, n=n: e.tensor_add(
                vc[0][:, dt, 0:n], vf[dt % 2][0][:, 0:n], x1fc[bi][0][:, dt, 0:n])],
                reads=[vf[dt % 2][1], x1fc[bi][1]], writes=[vc[1]],
                tag="c2v%d_%d" % (gi, dt))
        st("ACT", [lambda e, n=n: e.activation(vbc[0][:, :, 0:n], vc[0][:, :, 0:n],
                                               AF.Copy)],
           reads=[vc[1]], writes=[vbc[1]], tag="c2vb%d" % gi)
        st("ACT", [lambda e, n=n: e.activation(vsqc[0][:, :, 0:n], vbc[0][:, :, 0:n],
                                               AF.Square)],
           reads=[vbc[1]], writes=[vsqc[1]], tag="c2vs%d" % gi)
        stat_mms(2, 3, lambda c, n=n: vbc[0][:, c, 0:n], lambda c, n=n: vsqc[0][:, c, 0:n],
                 vbc[1], vsqc[1], n, tag="c2sm%d" % gi)

    def c2b_tail(gi):
        w0, n = C2G[gi]
        row_math(2, 3, n, rsb2, msb2, si=1, tag="c2rm%d" % gi)
        ems = []
        for c in range(CH):
            ems.append(lambda e, c=c, n=n: e.tensor_mul(
                tc[0][:, c, 0:n], vc[0][:, c, 0:n], rsb2[0][:, 0:n]))
            ems.append(lambda e, c=c, n=n: e.tensor_sub(
                tc[0][:, c, 0:n], tc[0][:, c, 0:n], msb2[0][:, 0:n]))
        st("DVE", ems, reads=[vc[1], rsb2[1], msb2[1]], writes=[tc[1]], tag="c2t%d" % gi)
        st("ACT", [lambda e, c=c, n=n: e.activation(
            ac[0][:, c, 0:n], tc[0][:, c, 0:n], AF.Identity,
            bias=vec["be2"][0][:, c:c + 1], scale=vec["g2"][0][:, c:c + 1])
            for c in range(CH)],
            reads=[tc[1], vec["be2"][1], vec["g2"][1]], writes=[ac[1]], tag="c2a%d" % gi)
        st("ACT", [lambda e, w0=w0, n=n: e.dma_start(
            out=a_d[:, w0:w0 + n].rearrange("(c p) n -> p c n", p=128),
            in_=ac[0][:, :, 0:n])], reads=[ac[1]], dma=True, tag="c2d%d" % gi)
        if taps >= 1:
            st("SP", [lambda e, w0=w0, n=n: e.dma_start(
                out=tap_ap["a"][:, w0:w0 + n].rearrange("(c p) n -> p c n", p=128),
                in_=ac[0][:, :, 0:n])], reads=[ac[1]], dma=True, tag="tapa%d" % gi)

    c2b_head(0)
    for gi in range(len(C2G)):
        c2b_main(gi)
        if gi + 1 < len(C2G):
            c2b_head(gi + 1)
        c2b_tail(gi)

    barh3 = barrier("barD")
    for g in reversed(c2g):
        g.__exit__(None, None, None)
    for g in reversed(c2wg):
        g.__exit__(None, None, None)

    dg = []

    def DSB(name, shape, dt):
        g = nc.sbuf_tensor(name, shape, dt)
        t = g.__enter__()
        dg.append(g)
        return t, Res()

    DG2 = [(1 + 256 * i, 256) for i in range(8)]
    NN = 257
    wr_sb = DSB("wr_sb", [128, CH, D], BF16)
    wk2_sb = DSB("wk2_sb", [128, CH, D], BF16)
    wv2_sb = DSB("wv2_sb", [128, CH, D], BF16)
    wrr = [Res() for _ in range(4)]
    wkr = [Res() for _ in range(4)]
    wvr = [Res() for _ in range(4)]
    agb = [DSB("d_a%d" % i, [128, CH, NN], F32) for i in range(2)]
    abf = [DSB("d_ab%d" % i, [128, CH, NN], BF16) for i in range(2)]
    xgb = DSB("d_xg", [128, CH, NN], F32)
    xmid = [DSB("d_xm%d" % i, [128, CH, NN], F32) for i in range(2)]
    xmidb = [Res() for _ in range(2)]
    ln2 = [DSB("d_ln2%d" % i, [128, CH, NN], F32) for i in range(2)]
    ln2b = [Res() for _ in range(2)]
    asq = DSB("d_asq", [128, CH, NN], BF16)
    bb1 = DSB("d_bb1", [128, CH, NN], BF16)
    bb2 = DSB("d_bb2", [128, CH, NN], BF16)
    xrb = [DSB("d_xr%d" % i, [128, CH, 256], BF16) for i in range(2)]
    xkb = [DSB("d_xk%d" % i, [128, CH, 256], BF16) for i in range(2)]
    scr2 = DSB("d_s2", [128, CH, 256], F32)
    rb_ = DSB("d_rb", [128, CH, 256], F32)
    ff_ = [DSB("d_ff%d" % i, [128, 256], F32) for i in range(2)]
    dif = abf
    kab = xrb
    rsbd = [DSB("d_rsb%d" % i, [128, NN], F32) for i in range(2)]
    msbd = [DSB("d_msb%d" % i, [128, NN], F32) for i in range(2)]
    rsbdF = DSB("d_rsbF", [128, 256], F32)
    msbdF = DSB("d_msbF", [128, 256], F32)
    bbf1 = xkb
    bbf2 = xrb

    for nm, tsb, rr, ds in (("wr", wr_sb, wrr, "qWr"), ("wk2", wk2_sb, wkr, "qWk2"),
                            ("wv2", wv2_sb, wvr, "qWv2")):
        for j in range(4):
            st("POOL", [lambda e, nm=nm, tsb=tsb, j=j: e.dma_start(
                out=tsb[0][:, :, j * 256:(j + 1) * 256],
                in_=wdr[nm][:, j * 256:(j + 1) * 256].rearrange(
                    "(c p) d -> p c d", p=128))],
                writes=[rr[j]], dma=True, dsem=ds, tag="dw%s%d" % (nm, j))

    def d_head(gi):
        w0, n = DG2[gi]
        cw0 = w0 - 1
        nn = n + 1
        bi = gi % 2
        st("SP", [
            lambda e, bi=bi, cw0=cw0, nn=nn: e.dma_start(
                out=agb[bi][0][:, :, 0:nn],
                in_=a_d[:, cw0:cw0 + nn].rearrange("(c p) n -> p c n", p=128)),
            lambda e, bi=bi, cw0=cw0, nn=nn: e.dma_start(
                out=xgb[0][:, :, 0:nn],
                in_=xTo[:, cw0:cw0 + nn].rearrange("(c p) n -> p c n", p=128)),
        ], writes=[agb[bi][1], xgb[1]], dma=True, tag="dl%d" % gi)
        st("POOL", [lambda e, bi=bi, cw0=cw0, nn=nn: e.dma_start(
            out=abf[bi][0][:, :, 0:nn],
            in_=a_d[:, cw0:cw0 + nn].rearrange("(c p) n -> p c n", p=128))],
            writes=[abf[bi][1]], dma=True, tag="dlb%d" % gi)
        st("ACT", [lambda e, bi=bi, nn=nn: e.activation(
            asq[0][:, :, 0:nn], abf[bi][0][:, :, 0:nn], AF.Square)],
            reads=[abf[bi][1]], writes=[asq[1]], tag="das%d" % gi)
        stat_mms(2, 3, lambda c, bi=bi, nn=nn: abf[bi][0][:, c, 0:nn],
                 lambda c, nn=nn: asq[0][:, c, 0:nn],
                 abf[bi][1], asq[1], nn, tag="dsm%d" % gi)

    def d_chain(gi):
        w0, n = DG2[gi]
        nn = n + 1
        bi = gi % 2
        row_math(2, 3, nn, rsbd[bi], msbd[bi], si=0, tag="drm%d" % gi)
        for eng, c0, c1, wrs in (("DVE", 0, 6, (xmid[bi][1],)),
                                 ("POOL", 6, CH, (xmidb[bi],))):
            ems = []
            for c in range(c0, c1):
                ems.append(lambda e, bi=bi, c=c, nn=nn: e.tensor_mul(
                    xmid[bi][0][:, c, 0:nn], agb[bi][0][:, c, 0:nn], rsbd[bi][0][:, 0:nn]))
                ems.append(lambda e, bi=bi, c=c, nn=nn: e.tensor_sub(
                    xmid[bi][0][:, c, 0:nn], xmid[bi][0][:, c, 0:nn], msbd[bi][0][:, 0:nn]))
                ems.append(lambda e, bi=bi, c=c, nn=nn: e.tensor_add(
                    xmid[bi][0][:, c, 0:nn], xmid[bi][0][:, c, 0:nn], xgb[0][:, c, 0:nn]))
            st(eng, ems, reads=[agb[bi][1], rsbd[bi][1], msbd[bi][1], xgb[1]],
               writes=list(wrs), tag="dxm%s%d" % (eng, gi))
        st("ACT", [lambda e, bi=bi, nn=nn: e.activation(
            bb1[0][:, :, 0:nn], xmid[bi][0][:, :, 0:nn], AF.Copy)],
            reads=[xmid[bi][1], xmidb[bi]], writes=[bb1[1]], tag="dxb%d" % gi)
        st("ACT", [lambda e, nn=nn: e.activation(bb2[0][:, :, 0:nn], bb1[0][:, :, 0:nn],
                                                 AF.Square)],
           reads=[bb1[1]], writes=[bb2[1]], tag="dxs%d" % gi)
        stat_mms(2, 3, lambda c, nn=nn: bb1[0][:, c, 0:nn],
                 lambda c, nn=nn: bb2[0][:, c, 0:nn],
                 bb1[1], bb2[1], nn, tag="dsm2%d" % gi)
        row_math(2, 3, nn, rsbd[bi], msbd[bi], si=0, tag="drm2%d" % gi)
        if gi == 0:
            st("DVE", [
                lambda e: e.tensor_mul(rsbd[0][0][:, 0:1], rsbd[0][0][:, 0:1],
                                       vec["cmask"][0][:, 0:1]),
                lambda e: e.tensor_mul(msbd[0][0][:, 0:1], msbd[0][0][:, 0:1],
                                       vec["cmask"][0][:, 0:1]),
            ], reads=[vec["cmask"][1]], writes=[rsbd[0][1], msbd[0][1]], tag="dmask")
        for eng, c0, c1, wrs in (("DVE", 0, 4, (ln2[bi][1],)),
                                 ("POOL", 4, CH, (ln2b[bi],))):
            ems = []
            for c in range(c0, c1):
                ems.append(lambda e, bi=bi, c=c, nn=nn: e.tensor_mul(
                    ln2[bi][0][:, c, 0:nn], xmid[bi][0][:, c, 0:nn], rsbd[bi][0][:, 0:nn]))
                ems.append(lambda e, bi=bi, c=c, nn=nn: e.tensor_sub(
                    ln2[bi][0][:, c, 0:nn], ln2[bi][0][:, c, 0:nn], msbd[bi][0][:, 0:nn]))
            st(eng, ems, reads=[xmid[bi][1], xmidb[bi], rsbd[bi][1], msbd[bi][1]],
               writes=list(wrs), tag="dln2%s%d" % (eng, gi))
        st("POOL", [lambda e, bi=bi, c=c, n=n: e.tensor_sub(
            dif[bi][0][:, c, 0:n], ln2[bi][0][:, c, 1:n + 1], ln2[bi][0][:, c, 0:n])
            for c in range(CH)],
            reads=[ln2[bi][1], ln2b[bi]], writes=[dif[bi][1]], tag="ddif%d" % gi)
        ems = []
        for c in range(CH):
            ems.append(lambda e, bi=bi, c=c, n=n: e.scalar_tensor_tensor(
                xrb[bi][0][:, c, 0:n], dif[bi][0][:, c, 0:n], vec["mur"][0][:, c:c + 1],
                ln2[bi][0][:, c, 0:n], ALU.mult, ALU.add))
            ems.append(lambda e, bi=bi, c=c, n=n: e.scalar_tensor_tensor(
                xkb[bi][0][:, c, 0:n], dif[bi][0][:, c, 0:n], vec["muk"][0][:, c:c + 1],
                ln2[bi][0][:, c, 0:n], ALU.mult, ALU.add))
        st("DVE", ems, reads=[ln2[bi][1], ln2b[bi], dif[bi][1],
                              vec["mur"][1], vec["muk"][1]],
           writes=[xrb[bi][1], xkb[bi][1]], tag="dxrk%d" % gi)

    def d_ffn_a(gi):
        w0, n = DG2[gi]
        bi = gi % 2
        for dt in range(CH):
            bidx = dt % 2
            ems = [lambda e, c=c, dt=dt, bidx=bidx, bi=bi, n=n: e.matmul(
                PS[bidx][0][:, 0:n], wr_sb[0][:, c, dt * 128:(dt + 1) * 128],
                xrb[bi][0][:, c, 0:n], start=(c == 0), stop=(c == CH - 1),
                skip_group_check=True) for c in range(CH)]
            st("PE", ems, reads=[wrr[3], xrb[bi][1]], writes=[PS[bidx][1]],
               tag="dwr%d_%d" % (gi, dt))
            st("ACT", [lambda e, bidx=bidx, dt=dt, n=n: e.activation(
                rb_[0][:, dt, 0:n], PS[bidx][0][:, 0:n], AF.Sigmoid)],
                reads=[PS[bidx][1]], writes=[rb_[1]], tag="drs%d_%d" % (gi, dt))
        for dt in range(CH):
            bidx = dt % 2
            ems = [lambda e, c=c, dt=dt, bidx=bidx, bi=bi, n=n: e.matmul(
                PS[bidx][0][:, 0:n], wk2_sb[0][:, c, dt * 128:(dt + 1) * 128],
                xkb[bi][0][:, c, 0:n], start=(c == 0), stop=(c == CH - 1),
                skip_group_check=True) for c in range(CH)]
            st("PE", ems, reads=[wkr[3], xkb[bi][1]], writes=[PS[bidx][1]],
               tag="dwk%d_%d" % (gi, dt))
            st("ACT", [lambda e, bidx=bidx, dt=dt, n=n: e.activation(
                scr2[0][:, dt, 0:n], PS[bidx][0][:, 0:n], AF.Relu)],
                reads=[PS[bidx][1]], writes=[scr2[1]], tag="dkr%d_%d" % (gi, dt))
        st("ACT", [lambda e, bi=bi, n=n: e.activation(
            kab[bi][0][:, :, 0:n], scr2[0][:, :, 0:n], AF.Square)],
            reads=[scr2[1]], writes=[kab[bi][1]], tag="dka%d" % gi)

    def d_ffn_b(gi):
        w0, n = DG2[gi]
        bi = gi % 2
        for dt in range(CH):
            bidx = dt % 2
            ems = [lambda e, c=c, dt=dt, bidx=bidx, bi=bi, n=n: e.matmul(
                PS[bidx][0][:, 0:n], wv2_sb[0][:, c, dt * 128:(dt + 1) * 128],
                kab[bi][0][:, c, 0:n], start=(c == 0), stop=(c == CH - 1),
                skip_group_check=True) for c in range(CH)]
            st("PE", ems, reads=[wvr[3], kab[bi][1]], writes=[PS[bidx][1]],
               tag="dwv%d_%d" % (gi, dt))
            st("ACT", [lambda e, bidx=bidx, dt=dt, n=n: e.activation(
                ff_[dt % 2][0][:, 0:n], PS[bidx][0][:, 0:n], AF.Copy)],
                reads=[PS[bidx][1]], writes=[ff_[dt % 2][1]], tag="dfe%d_%d" % (gi, dt))
            st("DVE", [lambda e, dt=dt, n=n: e.tensor_mul(
                scr2[0][:, dt, 0:n], ff_[dt % 2][0][:, 0:n], rb_[0][:, dt, 0:n])],
                reads=[ff_[dt % 2][1], rb_[1]], writes=[scr2[1]], tag="df%d_%d" % (gi, dt))
        st("ACT", [lambda e, bi=bi, n=n: e.activation(
            bbf1[bi][0][:, :, 0:n], scr2[0][:, :, 0:n], AF.Copy)],
            reads=[scr2[1]], writes=[bbf1[bi][1]], tag="dfb%d" % gi)
        st("ACT", [lambda e, bi=bi, n=n: e.activation(
            bbf2[bi][0][:, :, 0:n], bbf1[bi][0][:, :, 0:n], AF.Square)],
            reads=[bbf1[bi][1]], writes=[bbf2[bi][1]], tag="dfs%d" % gi)
        stat_mms(4, 5, lambda c, bi=bi, n=n: bbf1[bi][0][:, c, 0:n],
                 lambda c, bi=bi, n=n: bbf2[bi][0][:, c, 0:n],
                 bbf1[bi][1], bbf2[bi][1], n, tag="dsm3%d" % gi)
        row_math(4, 5, n, rsbdF, msbdF, si=2, tag="drm3%d" % gi)
        ems = []
        for c in range(CH):
            ems.append(lambda e, c=c, n=n: e.tensor_mul(
                scr2[0][:, c, 0:n], scr2[0][:, c, 0:n], rsbdF[0][:, 0:n]))
            ems.append(lambda e, c=c, n=n: e.tensor_sub(
                scr2[0][:, c, 0:n], scr2[0][:, c, 0:n], msbdF[0][:, 0:n]))
            ems.append(lambda e, bi=bi, c=c, n=n: e.tensor_add(
                scr2[0][:, c, 0:n], scr2[0][:, c, 0:n], xmid[bi][0][:, c, 1:1 + n]))
        st("DVE", ems, reads=[rsbdF[1], msbdF[1], xmid[bi][1], xmidb[bi]],
           writes=[scr2[1]], tag="dout%d" % gi)
        st("SP", [lambda e, w0=w0, n=n: e.dma_start(
            out=outT[:, w0 - 1:w0 - 1 + n].rearrange("(c p) n -> p c n", p=128),
            in_=scr2[0][:, :, 0:n])], reads=[scr2[1]], dma=True, tag="dst%d" % gi)

    d_head(0)
    d_chain(0)
    for gi in range(len(DG2)):
        if gi + 1 < len(DG2):
            d_head(gi + 1)
        d_ffn_a(gi)
        if gi + 1 < len(DG2):
            d_chain(gi + 1)
        d_ffn_b(gi)

    st("SP", [lambda e: e.nop()], extra=[("qSP", B.cnt["qSP"])], tag="fin")

    nc2 = _replay(nc, B)
    for g in reversed(semg_dummy) if False else []:
        pass
    for g in reversed(dg):
        g.__exit__(None, None, None)
    for g in reversed(guards):
        g.__exit__(None, None, None)
    return nc2


def _replay(nc, B):
    sems = {}
    semg = []
    with nc.Block() as block:
        for name in ("sPE", "sACT", "sDVE", "sPOOL", "sSP", "qSP", "qACT", "qGP", "qWA", "qWB", "qW1", "qW2", "qWr", "qWk2", "qWv2"):
            g = nc.semaphore(name)
            sems[name] = g.__enter__()
            semg.append(g)
        eng_map = {"PE": block.tensor, "ACT": block.scalar, "DVE": block.vector,
                   "POOL": block.gpsimd, "SP": block.sync}

        def mk(eng_key):
            def body(e):
                for (eng, waits, emits, sem, dma, tag) in B.ops:
                    if eng != eng_key:
                        continue
                    for sname, v in sorted(waits.items()):
                        e.wait_ge(sems[sname], v)
                    last = None
                    for fn in emits:
                        inst = fn(e)
                        if dma:
                            inst.then_inc(sems[sem], 16)
                        else:
                            last = inst
                    if not dma and last is not None:
                        last.then_inc(sems[sem], 1)
            return body

        for k, dec in eng_map.items():
            dec(mk(k))
    for g in reversed(semg):
        g.__exit__(None, None, None)
    return nc


def _stage_inputs(inputs):
    f32 = np.float32
    x = np.asarray(inputs["L0_attn_src"], f32)
    col = lambda v: np.ascontiguousarray(np.asarray(v, f32).reshape(-1, 128).T)
    shared = {
        "wq": np.ascontiguousarray(np.asarray(inputs["L0_attn_Wq"], f32)),
        "wk": np.ascontiguousarray(np.asarray(inputs["L0_attn_Wk"], f32)),
        "wv": np.ascontiguousarray(np.asarray(inputs["L0_attn_Wv"], f32)),
        "wo": np.ascontiguousarray(np.asarray(inputs["L0_attn_Wo"], f32)),
        "w1": np.ascontiguousarray(np.asarray(inputs["L0_attn_W1"], f32)),
        "w2": np.ascontiguousarray(np.asarray(inputs["L0_attn_W2"], f32)),
        "wr": np.ascontiguousarray(np.asarray(inputs["L0_ffn_W_r"], f32).T),
        "wk2": np.ascontiguousarray(np.asarray(inputs["L0_ffn_W_k"], f32).T),
        "wv2": np.ascontiguousarray(np.asarray(inputs["L0_ffn_W_v"], f32).T),
        "bq": col(inputs["L0_attn_bq"]),
        "bo": col(inputs["L0_attn_bo"]),
        "b2": col(inputs["L0_attn_b2"]),
        "g1": col(inputs["L0_attn_g1"]),
        "be1": col(inputs["L0_attn_be1"]),
        "g2": col(inputs["L0_attn_g2"]),
        "be2": col(inputs["L0_attn_be2"]),
        "mur": col(inputs["L0_ffn_mu_r"]),
        "muk": col(inputs["L0_ffn_mu_k"]),
        "b1c": col(inputs["L0_attn_b1"]),
        "bkrow": np.ascontiguousarray(np.asarray(inputs["L0_attn_bk"], f32)[None, :]),
        "bvrow": np.ascontiguousarray(np.asarray(inputs["L0_attn_bv"], f32)[None, :]),
        "ident": np.eye(128, dtype=f32),
    }
    zpat = np.zeros((32, PAIRS, 128), f32)
    for p in range(PAIRS):
        for hl in range(2):
            zpat[hl * 16 + 2 * p, p, 0:64] = 1.0
            zpat[hl * 16 + 2 * p + 1, p, 64:128] = 1.0
    shared["zpat"] = zpat
    masks = [np.zeros((128, CH), f32), np.ones((128, CH), f32)]
    xTb = [np.ascontiguousarray(
        np.concatenate([np.zeros((D, 1), f32), x[b].T], axis=1)) for b in range(4)]
    in_maps = []
    for c in range(8):
        b, h = c // 2, c % 2
        m = dict(shared)
        m["xT"] = xTb[b]
        m["xTo"] = np.ascontiguousarray(xTb[b][:, h * 2048:h * 2048 + OWN])
        m["cmask"] = masks[h]
        in_maps.append(m)
    return in_maps


def kernel(**inputs):
    in_maps = _stage_inputs(inputs)
    nc = build_program(taps=0)
    res = run_bass_kernel_spmd(nc, in_maps, core_ids=list(range(8)))
    out = np.empty((4, 4096, 1024), np.float32)
    for c in range(8):
        b, h = c // 2, c % 2
        out[b, h * 2048:(h + 1) * 2048, :] = res.results[c]["outT"].T
    return out


# version 12 (final)
# speedup vs baseline: 1.0244x; 1.0025x over previous
import numpy as np
import concourse.bass as bass
import concourse.mybir as mybir
from concourse.bass_utils import run_bass_kernel_spmd

F32 = mybir.dt.float32
BF16 = mybir.dt.bfloat16
AF = mybir.ActivationFunctionType
ALU = mybir.AluOpType

D = 1024
CH = 8
H = 16
PAIRS = 8
S = 4096
OWN = 2049
FF = 4096
FFT = 32
EPS_LN = 1e-5
EPS_Z = 1e-6

AG = [(1 + 256 * g, 256) for g in range(16)]
CG = [(0, 385), (385, 416), (801, 416), (1217, 416), (1633, 416)]
C2G = [(0, 257)] + [(257 + 256 * i, 256) for i in range(7)]
DG = [(1, 410), (411, 410), (821, 410), (1231, 410), (1641, 408)]

ENGS = ("PE", "ACT", "DVE", "POOL", "SP")


class Res:
    __slots__ = ("w", "r")

    def __init__(self):
        self.w = None
        self.r = []

    def rdep(self):
        return [self.w] if self.w else []

    def wdep(self):
        d = list(self.r)
        if self.w:
            d.append(self.w)
        return d


class Builder:
    def __init__(self):
        self.ops = []
        self.cnt = {}
        self.gate = []

    def st(self, eng, emits, reads=(), writes=(), extra=(), dma=False, tag="",
           dsem=None):
        if not isinstance(emits, (list, tuple)):
            emits = [emits]
        deps = [d for d in extra if d is not None] + list(self.gate)
        for r in reads:
            deps += r.rdep()
        for r in writes:
            deps += r.wdep()
        if dma:
            sem = dsem or {"SP": "qSP", "ACT": "qACT", "POOL": "qGP"}[eng]
            amt = 16 * len(emits)
        else:
            sem = "s" + eng
            amt = 1
        own = None if dma else sem
        waits = {}
        for s, v in deps:
            if s == own:
                continue
            if waits.get(s, -1) < v:
                waits[s] = v
        cur = self.cnt.get(sem, 0) + amt
        self.cnt[sem] = cur
        self.ops.append((eng, waits, emits, sem, dma, tag))
        h = (sem, cur)
        for r in reads:
            r.r.append(h)
        for r in writes:
            r.w = h
            r.r = []
        return h


def build_program(taps=0, upto=None):
    nc = bass.Bass("TRN2", target_bir_lowering=False, debug=False)
    B = Builder()
    st = B.st

    xT = nc.dram_tensor("xT", [D, S + 1], F32, kind="ExternalInput").ap()
    xTo = nc.dram_tensor("xTo", [D, OWN], F32, kind="ExternalInput").ap()
    wdr = {}
    for nm, shp in (("wq", [D, D]), ("wk", [D, D]), ("wv", [D, D]), ("wo", [D, D]),
                    ("w1", [D, FF]), ("w2", [FF, D]),
                    ("wr", [D, D]), ("wk2", [D, D]), ("wv2", [D, D])):
        wdr[nm] = nc.dram_tensor(nm, shp, F32, kind="ExternalInput").ap()
    vnames = ("bq", "bo", "b2", "g1", "be1", "g2", "be2", "mur", "muk", "cmask")
    vd = {nm: nc.dram_tensor(nm, [128, CH], F32, kind="ExternalInput").ap() for nm in vnames}
    b1c_d = nc.dram_tensor("b1c", [128, FFT], F32, kind="ExternalInput").ap()
    bk_row_d = nc.dram_tensor("bkrow", [1, D], F32, kind="ExternalInput").ap()
    bv_row_d = nc.dram_tensor("bvrow", [1, D], F32, kind="ExternalInput").ap()
    zpat_d = nc.dram_tensor("zpat", [32, PAIRS, 128], F32, kind="ExternalInput").ap()
    ident_d = nc.dram_tensor("ident", [128, 128], F32, kind="ExternalInput").ap()
    x1f_d = nc.dram_tensor("x1f_d", [D, OWN], F32, kind="Internal").ap()
    x1h_d = nc.dram_tensor("x1h_d", [D, OWN], BF16, kind="Internal").ap()
    a_d = nc.dram_tensor("a_d", [D, OWN], F32, kind="Internal").ap()
    h1_d = nc.dram_tensor("h1_d", [FF, OWN], BF16, kind="Internal").ap()
    outT = nc.dram_tensor("outT", [D, S // 2], F32, kind="ExternalOutput").ap()
    tap_ap = {}
    if taps >= 1:
        tap_ap["rl"] = nc.dram_tensor("t_rl", [4, 512], F32, kind="ExternalOutput").ap()
        tap_ap["rsb"] = nc.dram_tensor("t_rsb", [2, 128, 512], F32, kind="ExternalOutput").ap()
        tap_ap["ln0h"] = nc.dram_tensor("t_ln0h", [128, CH, 512], F32, kind="ExternalOutput").ap()
        tap_ap["K"] = nc.dram_tensor("t_K", [128, D], F32, kind="ExternalOutput").ap()
        tap_ap["V"] = nc.dram_tensor("t_V", [128, H * 65], F32, kind="ExternalOutput").ap()
        tap_ap["kf"] = nc.dram_tensor("t_kf", [128, D], F32, kind="ExternalOutput").ap()
        tap_ap["Q"] = nc.dram_tensor("t_Q", [128, CH, 448], F32, kind="ExternalOutput").ap()
        tap_ap["z"] = nc.dram_tensor("t_z", [128, 4, 16], F32, kind="ExternalOutput").ap()
        tap_ap["o"] = nc.dram_tensor("t_o", [128, CH, 448], F32, kind="ExternalOutput").ap()
        tap_ap["u"] = nc.dram_tensor("t_u", [128, CH, 448], F32, kind="ExternalOutput").ap()
        tap_ap["zb"] = nc.dram_tensor("t_zb", [2, 128, 448], F32, kind="ExternalOutput").ap()
        tap_ap["zf"] = nc.dram_tensor("t_zf", [32, 448], F32, kind="ExternalOutput").ap()
        tap_ap["oun"] = nc.dram_tensor("t_oun", [128, 448], F32, kind="ExternalOutput").ap()
        tap_ap["u2"] = nc.dram_tensor("t_u2", [128, CH, 448], F32, kind="ExternalOutput").ap()
        tap_ap["l0"] = nc.dram_tensor("t_l0", [128, CH, 448], F32, kind="ExternalOutput").ap()
        tap_ap["zb2"] = nc.dram_tensor("t_zb2", [128, PAIRS, 448], F32, kind="ExternalOutput").ap()
        tap_ap["kvbz"] = nc.dram_tensor("t_kvbz", [128, PAIRS * 2 * 2 * 64], F32,
                                        kind="ExternalOutput").ap()
        tap_ap["rows"] = nc.dram_tensor("t_rows", [4, S + 1], F32, kind="ExternalOutput").ap()
        tap_ap["kvb"] = nc.dram_tensor("t_kvb", [128, PAIRS * 65], F32, kind="ExternalOutput").ap()
        tap_ap["x1"] = nc.dram_tensor("t_x1", [D, OWN], F32, kind="ExternalOutput").ap()
        tap_ap["a"] = nc.dram_tensor("t_a", [D, OWN], F32, kind="ExternalOutput").ap()

    guards = []

    def SB(name, shape, dt):
        g = nc.sbuf_tensor(name, shape, dt)
        t = g.__enter__()
        guards.append(g)
        return t, Res()

    vec = {nm: SB("v_" + nm, [128, CH], F32) for nm in vnames + ("bq1", "bqn")}
    b1_sb = SB("v_b1", [128, FFT], F32)
    onescol = SB("onescol", [128, 1], BF16)
    zercol = SB("zercol", [128, 64], BF16)
    ones1 = SB("ones1", [1, 128], BF16)
    rw = [{k: SB("rw%d_%s" % (s, k), [128, 512], F32) for k in ("mu", "q", "t")}
          for s in range(2)]
    rw.append(rw[1])
    ones128 = SB("ones128", [128, 128], BF16)
    ones128f = SB("ones128f", [128, 128], F32)
    constvals = (1e-5, 1e-6, -0.5, -1.0)
    ccell = SB("ccell", [128, len(constvals)], F32)

    wqg = []

    def WQSB(name, shape, dt):
        g = nc.sbuf_tensor(name, shape, dt)
        t = g.__enter__()
        wqg.append(g)
        return t, Res()

    wq_sb = WQSB("wq_sb", [128, CH, D], BF16)
    wo_sb = WQSB("wo_sb", [128, CH, D], BF16)
    wqr = [Res() for _ in range(4)]
    wor = [Res() for _ in range(4)]

    kvg = []

    def KVSB(name, shape, dt):
        g = nc.sbuf_tensor(name, shape, dt)
        t = g.__enter__()
        kvg.append(g)
        return t, Res()

    zpat_sb = KVSB("zpat_s", [32, PAIRS, 128], BF16)
    ident_sb = KVSB("ident_s", [128, 128], BF16)
    kvbz = KVSB("kvbz", [128, PAIRS, 2, 2, 64], BF16)
    zrhs = KVSB("zrhs", [128, PAIRS, 4], BF16)

    PS = []
    for i in range(7):
        g = nc.psum_tensor("ps%d" % i, [128, 512], F32)
        PS.append((g.__enter__(), Res()))
        guards.append(g)
    g = nc.psum_tensor("psb", [128, 1024], BF16)
    PSB = (g.__enter__(), Res())
    guards.append(g)

    def bcast2(bi, hi_res, hi_ap, lo_ap, n, extra=(), tag="bc"):
        pb, rb = PS[bi]
        return st("PE", [
            lambda e: e.matmul(pb[:, 0:n], ones1[0][:, :], hi_ap, start=True, stop=False,
                               skip_group_check=True),
            lambda e: e.matmul(pb[:, 0:n], ones1[0][:, :], lo_ap, start=False, stop=True,
                               skip_group_check=True),
        ], reads=[ones1[1], hi_res], writes=[rb], extra=extra, tag=tag)

    def stat_mms(bi_s, bi_q, val_ap, sq_ap, val_res, sq_res, n, extra=(), tag="st",
                 off_s=0, off_q=0, ones_val=None):
        pbs, rbs = PS[bi_s]
        pbq, rbq = PS[bi_q]
        packed = bi_s == bi_q
        ov = ones_val if ones_val is not None else ones128
        ems = []
        for c in range(CH):
            ems.append(lambda e, c=c: e.matmul(
                pbs[:, off_s:off_s + n], ov[0][:, :], val_ap(c), start=(c == 0),
                stop=(c == CH - 1), skip_group_check=True))
        for c in range(CH):
            ems.append(lambda e, c=c: e.matmul(
                pbq[:, off_q:off_q + n], ones128[0][:, :], sq_ap(c),
                start=(c == 0 and not packed),
                stop=(c == CH - 1), skip_group_check=True))
        return st("PE", ems, reads=[ones128[1], val_res, sq_res], writes=[rbs, rbq],
                  extra=extra, tag=tag)

    def row_math(bi_s, bi_q, n, rsb, msb, si=0, eps=EPS_LN, tag="rm",
                 off_s=0, off_q=0):
        rws = rw[si]
        pbs, rbs = PS[bi_s]
        pbq, rbq = PS[bi_q]
        st("DVE", [
            lambda e: e.tensor_scalar_mul(rws["mu"][0][:, 0:n],
                                          pbs[:, off_s:off_s + n], 1.0 / D),
            lambda e: e.tensor_scalar_mul(rws["q"][0][:, 0:n],
                                          pbq[:, off_q:off_q + n], 1.0 / D),
            lambda e: e.tensor_mul(rws["t"][0][:, 0:n], rws["mu"][0][:, 0:n],
                                   rws["mu"][0][:, 0:n]),
            lambda e: e.tensor_sub(rws["q"][0][:, 0:n], rws["q"][0][:, 0:n],
                                   rws["t"][0][:, 0:n]),
        ], reads=[rbs, rbq], writes=[rws["mu"][1], rws["q"][1], rws["t"][1]],
            tag=tag + "1")
        st("ACT", [
            lambda e: e.activation(rws["q"][0][:, 0:n], rws["q"][0][:, 0:n], AF.Ln,
                                   bias=float(eps)),
            lambda e: e.activation(rsb[0][:, 0:n], rws["q"][0][:, 0:n], AF.Exp,
                                   scale=-0.5),
        ], reads=[rws["q"][1]], writes=[rws["q"][1], rsb[1]], tag=tag + "2")
        h3 = st("DVE", [
            lambda e: e.tensor_mul(msb[0][:, 0:n], rws["mu"][0][:, 0:n],
                                   rsb[0][:, 0:n]),
        ], reads=[rws["mu"][1], rsb[1]], writes=[msb[1]], tag=tag + "3")
        return h3

    def ln0_group(src_dram, w0, n, xhb, xqb, bs, bq, rsb, msb, tag, si=0,
                  off_s=0, off_q=0):
        st("POOL", lambda e, w0=w0, n=n: e.dma_start(
            out=xhb[0][:, :, 0:n],
            in_=src_dram[:, w0:w0 + n].rearrange("(c p) n -> p c n", p=128)),
            writes=[xhb[1]], dma=True, tag=tag + "lh")
        st("ACT", lambda e, n=n: e.activation(
            xqb[0][:, :, 0:n], xhb[0][:, :, 0:n], AF.Square),
            reads=[xhb[1]], writes=[xqb[1]], tag=tag + "sq")
        stat_mms(bs, bq, lambda c, n=n: xhb[0][:, c, 0:n],
                 lambda c, n=n: xqb[0][:, c, 0:n], xhb[1], xqb[1], n, tag=tag + "sm",
                 off_s=off_s, off_q=off_q)
        row_math(bs, bq, n, rsb, msb, si=si, tag=tag + "rm", off_s=off_s, off_q=off_q)

    h_const = st("DVE", [
        (lambda e, i=i, v=v: e.memset(ccell[0][:, i:i + 1], float(v)))
        for i, v in enumerate(constvals)
    ], writes=[ccell[1]], tag="consts")
    for i, v in enumerate(constvals):
        nc.const_aps.register(F32, float(v), ccell[0][:, i:i + 1])
    B.gate = [h_const]

    p0g = []

    def P0SB(name, shape, dt):
        g = nc.sbuf_tensor(name, shape, dt)
        t = g.__enter__()
        p0g.append(g)
        return t, Res()

    bkrow = P0SB("bkrow_s", [1, D], F32)
    bvrow = P0SB("bvrow_s", [1, D], F32)
    rowh = P0SB("rowh", [1, D], BF16)
    rowl = P0SB("rowl", [1, D], BF16)
    bkb = P0SB("bkb", [128, D], F32)
    bvb = P0SB("bvb", [128, D], F32)

    st("SP", [
        (lambda e, nm=nm: e.dma_start(out=vec[nm][0][:, :], in_=vd[nm]))
        for nm in vnames
    ] + [
        lambda e: e.dma_start(out=b1_sb[0][:, :], in_=b1c_d),
        lambda e: e.dma_start(out=bkrow[0][:, :], in_=bk_row_d),
        lambda e: e.dma_start(out=bvrow[0][:, :], in_=bv_row_d),
    ], writes=[vec[nm][1] for nm in vnames] + [b1_sb[1], bkrow[1], bvrow[1]],
        dma=True, tag="p0vec")
    st("POOL", [
        lambda e: e.dma_start(out=zpat_sb[0][:, :, :], in_=zpat_d),
        lambda e: e.dma_start(out=ident_sb[0][:, :], in_=ident_d),
    ], writes=[zpat_sb[1], ident_sb[1]], dma=True, tag="p0zp")
    st("DVE", [
        lambda e: e.memset(onescol[0][:, :], 1.0),
        lambda e: e.memset(zercol[0][:, :], 0.0),
        lambda e: e.memset(ones1[0][:, :], 1.0),
        lambda e: e.memset(ones128[0][:, :], 1.0),
        lambda e: e.memset(ones128f[0][:, :], 1.0),
    ], writes=[onescol[1], zercol[1], ones1[1], ones128[1], ones128f[1]],
        tag="p0ones")
    st("DVE", [
        lambda e: e.tensor_scalar_add(vec["bq1"][0][:, :], vec["bq"][0][:, :], 1.0),
        lambda e: e.tensor_scalar_mul(vec["bqn"][0][:, :], vec["bq"][0][:, :], -1.0),
    ], reads=[vec["bq"][1]], writes=[vec["bq1"][1], vec["bqn"][1]], tag="p0bq1")
    for src_row, dstt in ((bkrow, bkb), (bvrow, bvb)):
        st("DVE", [
            lambda e, src_row=src_row: e.tensor_copy(rowh[0][:, :], src_row[0][:, :]),
            lambda e, src_row=src_row: e.tensor_sub(rowl[0][:, :], src_row[0][:, :],
                                                    rowh[0][:, :]),
        ], reads=[src_row[1]], writes=[rowh[1], rowl[1]], tag="p0r")
        for half in range(2):
            bcast2(6, rowh[1], rowh[0][:, 512 * half:512 * (half + 1)],
                   rowl[0][:, 512 * half:512 * (half + 1)], 512,
                   extra=rowl[1].rdep(), tag="p0bc")
            st("DVE", lambda e, dstt=dstt, half=half: e.tensor_copy(
                dstt[0][:, 512 * half:512 * (half + 1)], PS[6][0][:, 0:512]),
                reads=[PS[6][1]], writes=[dstt[1]], tag="p0bcc")

    a1g = []

    def A1SB(name, shape, dt):
        g = nc.sbuf_tensor(name, shape, dt)
        t = g.__enter__()
        a1g.append(g)
        return t, Res()

    wk_sb = A1SB("wk_sb", [128, CH, D], BF16)
    wv_sb = A1SB("wv_sb", [128, CH, D], BF16)
    xh1 = [A1SB("a1xh%d" % i, [128, CH, 256], BF16) for i in range(2)]
    xq1 = [A1SB("a1xq%d" % i, [128, CH, 256], BF16) for i in range(2)]
    ln0h1 = [A1SB("a1ln%d" % i, [128, CH, 256], BF16) for i in range(2)]
    ln0h1b = [Res() for _ in range(2)]
    scr_a = A1SB("a1scr", [128, CH, 256], F32)
    scr_ab = Res()
    rsb1 = [A1SB("a1rsb%d" % i, [128, 256], F32) for i in range(2)]
    msb1 = [A1SB("a1msb%d" % i, [128, 256], F32) for i in range(2)]
    kf1 = A1SB("a1kf", [128, D], F32)
    km1 = A1SB("a1km", [128, D], F32)
    ke1 = A1SB("a1ke", [128, D], F32)
    Kt1 = [A1SB("a1K%d" % i, [128, D], BF16) for i in range(2)]
    Vt1 = [A1SB("a1V%d" % i, [128, H, 65], BF16) for i in range(2)]
    kvf = A1SB("kvf", [128, PAIRS, 65], F32)
    kvb = A1SB("kvb", [128, PAIRS, 2, 65], BF16)

    st("POOL", [lambda e: e.dma_start(out=wk_sb[0][:, :, :],
                 in_=wdr["wk"].rearrange("(c p) d -> p c d", p=128))],
       writes=[wk_sb[1]], dma=True, dsem="qWA", tag="a1wk")
    st("POOL", [lambda e: e.dma_start(out=wv_sb[0][:, :, :],
                 in_=wdr["wv"].rearrange("(c p) d -> p c d", p=128))],
       writes=[wv_sb[1]], dma=True, dsem="qWA", tag="a1wv")
    st("DVE", [lambda e, i=i: e.memset(Vt1[i][0][:, :, 64:65], 1.0) for i in range(2)],
       writes=[Vt1[0][1], Vt1[1][1]], tag="a1vone")

    ems = []
    for b in (4, 5):
        for cg in (0, 64):
            ems.append(lambda e, b=b, cg=cg: e.matmul(
                PS[b][0][cg:cg + 64, 0:260], zercol[0][:, :], wk_sb[0][:, 0, 0:260],
                start=True, stop=True, skip_group_check=True, tile_position=(0, cg)))
    st("PE", ems, reads=[zercol[1], wk_sb[1]], writes=[PS[4][1], PS[5][1]], tag="a1kvz")

    def a1_head(gi):
        w0, n = AG[gi]
        bi = gi % 2
        ln0_group(xT, w0, n, xh1[bi], xq1[bi], 6, 6, rsb1[bi], msb1[bi],
                  tag="a1s%d" % gi, si=0, off_s=0, off_q=256)
        for eng, c0, c1, wrs in (("DVE", 0, 5, (scr_a[1], ln0h1[bi][1])),
                                 ("POOL", 5, CH, (scr_ab, ln0h1b[bi]))):
            ems = []
            for c in range(c0, c1):
                ems.append(lambda e, bi=bi, c=c, n=n: e.tensor_mul(
                    scr_a[0][:, c, 0:n], xh1[bi][0][:, c, 0:n], rsb1[bi][0][:, 0:n]))
                ems.append(lambda e, bi=bi, c=c, n=n: e.tensor_sub(
                    ln0h1[bi][0][:, c, 0:n], scr_a[0][:, c, 0:n], msb1[bi][0][:, 0:n]))
            st(eng, ems, reads=[xh1[bi][1], rsb1[bi][1], msb1[bi][1]],
               writes=list(wrs), tag="a1ln%s%d" % (eng, gi))

    def a1_main(gi):
        w0, n = AG[gi]
        bi = gi % 2
        for tt in range(2):
            t0 = tt * 128
            ti = tt % 2
            ems = []
            for c in range(CH):
                for dh in range(2):
                    ems.append(lambda e, c=c, dh=dh, bi=bi, t0=t0: e.matmul(
                        PS[dh][0][:, 0:512], ln0h1[bi][0][:, c, t0:t0 + 128],
                        wk_sb[0][:, c, dh * 512:(dh + 1) * 512],
                        start=(c == 0), stop=(c == CH - 1), skip_group_check=True))
            st("PE", ems, reads=[ln0h1[bi][1], ln0h1b[bi], wk_sb[1], wv_sb[1]],
               writes=[PS[0][1], PS[1][1]], tag="a1km%d_%d" % (gi, tt))
            ems = []
            for c in range(CH):
                for dh in range(2):
                    ems.append(lambda e, c=c, dh=dh, bi=bi, t0=t0: e.matmul(
                        PS[2 + dh][0][:, 0:512], ln0h1[bi][0][:, c, t0:t0 + 128],
                        wv_sb[0][:, c, dh * 512:(dh + 1) * 512],
                        start=(c == 0), stop=(c == CH - 1), skip_group_check=True))
            st("PE", ems, reads=[ln0h1[bi][1], ln0h1b[bi], wv_sb[1], wk_sb[1]],
               writes=[PS[2][1], PS[3][1]], tag="a1vm%d_%d" % (gi, tt))
            st("DVE", [
                lambda e, dh=dh: e.tensor_add(kf1[0][:, dh * 512:(dh + 1) * 512],
                                              PS[dh][0][:, 0:512],
                                              bkb[0][:, dh * 512:(dh + 1) * 512])
                for dh in range(2)
            ], reads=[PS[0][1], PS[1][1], bkb[1]], writes=[kf1[1]], tag="a1kf%d_%d" % (gi, tt))
            st("DVE", [lambda e: e.tensor_scalar_min(km1[0][:, :], kf1[0][:, :], 0.0)],
               reads=[kf1[1]], writes=[km1[1]], tag="a1km2%d_%d" % (gi, tt))
            st("ACT", [lambda e: e.activation(ke1[0][:, :], km1[0][:, :], AF.Exp)],
               reads=[km1[1]], writes=[ke1[1]], tag="a1ke%d_%d" % (gi, tt))
            st("DVE", [lambda e, ti=ti: e.scalar_tensor_tensor(
                Kt1[ti][0][:, :], kf1[0][:, :], 1.0, ke1[0][:, :], ALU.add, ALU.max)],
                reads=[kf1[1], ke1[1]], writes=[Kt1[ti][1]], tag="a1K%d_%d" % (gi, tt))
            if taps == 2 and gi == 0 and tt == 0:
                st("POOL", [
                    lambda e: e.dma_start(out=tap_ap["K"], in_=Kt1[0][0][:, :]),
                    lambda e: e.dma_start(out=tap_ap["kf"], in_=kf1[0][:, :]),
                ], reads=[Kt1[0][1], kf1[1]], dma=True, tag="tapa1K")
            st("DVE", [
                lambda e, dh=dh, ti=ti: e.tensor_add(
                    Vt1[ti][0][:, dh * 8:(dh + 1) * 8, 0:64],
                    PS[2 + dh][0][:, 0:512].rearrange("p (h m) -> p h m", h=8),
                    bvb[0][:, dh * 512:(dh + 1) * 512].rearrange("p (h m) -> p h m", h=8))
                for dh in range(2)
            ], reads=[PS[2][1], PS[3][1], bvb[1]], writes=[Vt1[ti][1]],
                tag="a1V%d_%d" % (gi, tt))
            ems = []
            for p in range(PAIRS):
                bidx = 4 + p // 4
                off = (p % 4) * 65
                last_kv = (gi == len(AG) - 1 and tt == 1)
                for hh in range(2):
                    hd = 2 * p + hh
                    ems.append(lambda e, bidx=bidx, off=off, hd=hd, hh=hh, ti=ti,
                               last_kv=last_kv:
                               e.matmul(PS[bidx][0][64 * hh:64 * (hh + 1), off:off + 65],
                                        Kt1[ti][0][:, 64 * hd:64 * (hd + 1)],
                                        Vt1[ti][0][:, hd, :],
                                        start=False, stop=last_kv, skip_group_check=True,
                                        tile_position=(0, 64 * hh)))
            st("PE", ems, reads=[Kt1[ti][1], Vt1[ti][1]], writes=[PS[4][1], PS[5][1]],
               tag="a1kv%d_%d" % (gi, tt))

    a1_head(0)
    a1_head(1)
    for j in range(4):
        st("POOL", [lambda e, j=j: e.dma_start(
            out=wq_sb[0][:, :, j * 256:(j + 1) * 256],
            in_=wdr["wq"][:, j * 256:(j + 1) * 256].rearrange(
                "(c p) d -> p c d", p=128))],
            writes=[wqr[j]], dma=True, dsem="qWB", tag="pfwq%d" % j)
    for j in range(4):
        st("POOL", [lambda e, j=j: e.dma_start(
            out=wo_sb[0][:, :, j * 256:(j + 1) * 256],
            in_=wdr["wo"][:, j * 256:(j + 1) * 256].rearrange(
                "(c p) d -> p c d", p=128))],
            writes=[wor[j]], dma=True, dsem="qWB", tag="pfwo%d" % j)
    for gi in range(len(AG)):
        a1_main(gi)
        if gi + 2 < len(AG):
            a1_head(gi + 2)
        if taps == 2 and gi == 0:
            st("POOL", [
                lambda e: e.dma_start(out=tap_ap["rsb"][0], in_=rsb1[0][:, :]),
                lambda e: e.dma_start(out=tap_ap["ln0h"], in_=ln0h1[0][0][:, :, :]),
            ], reads=[rsb1[1], ln0h1[0][1]], dma=True, tag="tapa1")

    st("DVE", [
        lambda e: e.tensor_copy(kvf[0][:, 0:4, :].rearrange("p a b -> p (a b)"),
                                PS[4][0][:, 0:260]),
        lambda e: e.tensor_copy(kvf[0][:, 4:8, :].rearrange("p a b -> p (a b)"),
                                PS[5][0][:, 0:260]),
    ], reads=[PS[4][1], PS[5][1]], writes=[kvf[1]], tag="a2ev")
    st("DVE", [
        lambda e: e.tensor_copy(kvb[0][:, :, 0, :], kvf[0][:, :, :]),
        lambda e: e.tensor_sub(kvb[0][:, :, 1, :], kvf[0][:, :, :], kvb[0][:, :, 0, :]),
    ], reads=[kvf[1]], writes=[kvb[1]], tag="a2hl")
    st("DVE", [
        lambda e: e.memset(kvbz[0][:, :, :, :, :].rearrange('p a b c d -> p (a b c d)'), 0.0),
        lambda e: e.tensor_copy(kvbz[0][0:64, :, :, 0, :], kvb[0][0:64, :, :, 0:64]),
        lambda e: e.tensor_copy(kvbz[0][64:128, :, :, 1, :], kvb[0][64:128, :, :, 0:64]),
    ], reads=[kvb[1]], writes=[kvbz[1]], tag="a2kvz")
    ems = [lambda e: e.memset(zrhs[0][:, :, :], 0.0)]
    for p in range(PAIRS):
        for hl in range(2):
            ems.append(lambda e, p=p, hl=hl: e.tensor_copy(
                zrhs[0][0:64, p, hl:hl + 1], kvb[0][0:64, p, hl, 64:65]))
            ems.append(lambda e, p=p, hl=hl: e.tensor_copy(
                zrhs[0][64:128, p, 2 + hl:3 + hl], kvb[0][64:128, p, hl, 64:65]))
    st("DVE", ems, reads=[kvb[1]], writes=[zrhs[1]], tag="a2zr")
    if taps >= 1:
        st("POOL", [lambda e: e.dma_start(
            out=tap_ap["kvbz"],
            in_=kvbz[0][:, :, :, :, :].rearrange("p a b c d -> p (a b c d)"))],
            reads=[kvbz[1]], dma=True, tag="tapkvz")
    if taps >= 1:
        st("SP", [lambda e: e.dma_start(
            out=tap_ap["kvb"][:, :], in_=kvf[0][:, :, :].rearrange("p a b -> p (a b)"))],
            reads=[kvf[1]], dma=True, tag="tapkv")

    def finish():
        st("SP", [lambda e: e.nop()],
           extra=[(s, B.cnt.get(s, 0)) for s in
                  ("qSP", "qACT", "qGP", "qWA", "qWB", "qW1", "qW2", "qWr",
                   "qWk2", "qWv2", "sPE", "sACT", "sDVE", "sPOOL")], tag="finX")

    def barrier(name):
        hs = [st(eng, lambda e: e.nop(),
                 extra=[(s, B.cnt.get(s, 0)) for s in
                        ("sDVE", "sPE", "sACT", "sPOOL", "sSP", "qSP", "qACT",
                         "qGP", "qWA", "qWB", "qW1", "qW2", "qWr",
                         "qWk2", "qWv2")],
                 tag=name + eng) for eng in ENGS]
        B.gate = hs
        return hs

    if upto == "A":
        finish()
        return _replay(nc, B)
    barh = barrier("barC1")
    for g in reversed(p0g + a1g):
        g.__exit__(None, None, None)

    c1g = []

    def C1SB(name, shape, dt):
        g = nc.sbuf_tensor(name, shape, dt)
        t = g.__enter__()
        c1g.append(g)
        return t, Res()

    bsc = [C1SB("c1bs%d" % i, [128, CH, 416], BF16) for i in range(2)]
    sqc = [C1SB("c1sq%d" % i, [128, CH, 416], BF16) for i in range(2)]
    ln0f = [C1SB("c1l0f%d" % i, [128, CH, 416], F32) for i in range(2)]
    ln0fb = [Res() for _ in range(2)]
    ln0h = [C1SB("c1l0h%d" % i, [128, CH, 416], BF16) for i in range(2)]
    rsbc = [C1SB("c1rsb%d" % i, [128, 416], F32) for i in range(2)]
    msbc = [C1SB("c1msb%d" % i, [128, 416], F32) for i in range(2)]
    eqm = C1SB("c1eqm", [128, 416], F32)
    eqc = eqm
    Qt = [C1SB("c1Q%d" % i, [128, PAIRS, 416], BF16) for i in range(2)]
    ztc = C1SB("c1zt", [128, 4, 16], F32)
    zthl = C1SB("c1zthl", [128, 4, 2, 16], BF16)
    zfc = [C1SB("c1zf%d" % i, [32, 416], BF16) for i in range(2)]
    zbc8 = C1SB("c1zb8", [128, PAIRS, 416], F32)
    otc = C1SB("c1ot", [128, CH, 416], BF16)
    of_ = [C1SB("c1of%d" % i, [128, 416], F32) for i in range(2)]
    uf = [C1SB("c1uf%d" % i, [128, 416], F32) for i in range(2)]
    x1c = C1SB("c1x1", [128, CH, 416], F32)
    x1cb = Res()

    rsbc1 = [C1SB("c1rsb1%d" % i, [128, 416], F32) for i in range(2)]
    msbc1 = [C1SB("c1msb1%d" % i, [128, 416], F32) for i in range(2)]

    def c1_head(gi):
        w0, n = CG[gi]
        bi = gi % 2
        ln0_group(xTo, w0, n, bsc[bi], sqc[bi], 4, 5, rsbc[bi], msbc[bi],
                  tag="c1s%d" % gi, si=0)
        for eng, c0, c1, wrs in (("DVE", 0, 5, (ln0f[bi][1],)),
                                 ("POOL", 5, CH, (ln0fb[bi],))):
            ems = []
            for c in range(c0, c1):
                ems.append(lambda e, bi=bi, c=c, n=n: e.tensor_mul(
                    ln0f[bi][0][:, c, 0:n], bsc[bi][0][:, c, 0:n], rsbc[bi][0][:, 0:n]))
                ems.append(lambda e, bi=bi, c=c, n=n: e.tensor_sub(
                    ln0f[bi][0][:, c, 0:n], ln0f[bi][0][:, c, 0:n], msbc[bi][0][:, 0:n]))
            st(eng, ems, reads=[bsc[bi][1], rsbc[bi][1], msbc[bi][1]],
               writes=list(wrs), tag="c1l0f%s%d" % (eng, gi))
        st("ACT", [lambda e, bi=bi, n=n: e.activation(
            ln0h[bi][0][:, :, 0:n], ln0f[bi][0][:, :, 0:n], AF.Copy)],
            reads=[ln0f[bi][1], ln0fb[bi]], writes=[ln0h[bi][1]], tag="c1l0h%d" % gi)

    def c1_main(gi):
        w0, n = CG[gi]
        bi = gi % 2
        nch = (n + 127) // 128
        for dt in range(PAIRS):
            bidx = dt % 2
            ems = [lambda e, c=c, dt=dt, bidx=bidx, n=n: e.matmul(
                PS[bidx][0][:, 0:n], wq_sb[0][:, c, dt * 128:(dt + 1) * 128],
                ln0h[bi][0][:, c, 0:n], start=(c == 0), stop=(c == CH - 1),
                skip_group_check=True) for c in range(CH)]
            st("PE", ems, reads=[wqr[3], wor[3], ln0h[bi][1]], writes=[PS[bidx][1]],
               tag="c1q%d_%d" % (gi, dt))
            st("DVE", [lambda e, bidx=bidx, dt=dt, n=n: e.tensor_scalar(
                eqm[0][:, 0:n], PS[bidx][0][:, 0:n], vec["bq"][0][:, dt:dt + 1], 0.0,
                ALU.add, ALU.min)],
                reads=[PS[bidx][1], vec["bq"][1]], writes=[eqm[1]],
                tag="c1qm%d_%d" % (gi, dt))
            st("ACT", [lambda e, n=n: e.activation(
                eqc[0][:, 0:n], eqm[0][:, 0:n], AF.Exp)],
                reads=[eqm[1]], writes=[eqc[1]],
                tag="c1eq%d_%d" % (gi, dt))
            st("DVE", [lambda e, bidx=bidx, dt=dt, bi=bi, n=n: e.scalar_tensor_tensor(
                Qt[bi][0][:, dt, 0:n], PS[bidx][0][:, 0:n], vec["bq1"][0][:, dt:dt + 1],
                eqc[0][:, 0:n], ALU.add, ALU.max)],
                reads=[PS[bidx][1], eqc[1], vec["bq1"][1]], writes=[Qt[bi][1]],
                tag="c1Q%d_%d" % (gi, dt))
        ems = []
        for ch in range(nch):
            t0 = ch * 128
            m = min(128, n - t0)
            for p in range(PAIRS):
                for hl in range(2):
                    ems.append(lambda e, ch=ch, t0=t0, m=m, p=p, hl=hl, bi=bi: e.matmul(
                        PS[6][0][0:m, ch * 16 + 2 * p:ch * 16 + 2 * p + 2],
                        Qt[bi][0][:, p, t0:t0 + m], zrhs[0][:, p, hl::2],
                        start=(hl == 0), stop=(hl == 1), skip_group_check=True))
        st("PE", ems, reads=[Qt[bi][1], zrhs[1]], writes=[PS[6][1]], tag="c1zd%d" % gi)
        st("ACT", [
            lambda e, nch=nch: e.activation(
                ztc[0][:, 0:nch, :].rearrange("p a b -> p (a b)"),
                PS[6][0][:, 0:nch * 16], AF.Ln, bias=float(EPS_Z)),
            lambda e, nch=nch: e.activation(
                ztc[0][:, 0:nch, :].rearrange("p a b -> p (a b)"),
                ztc[0][:, 0:nch, :].rearrange("p a b -> p (a b)"), AF.Exp, scale=-1.0),
        ], reads=[PS[6][1]], writes=[ztc[1]], tag="c1zl%d" % gi)
        st("DVE", [
            lambda e, nch=nch: e.tensor_copy(zthl[0][:, 0:nch, 0, :], ztc[0][:, 0:nch, :]),
            lambda e, nch=nch: e.tensor_sub(zthl[0][:, 0:nch, 1, :], ztc[0][:, 0:nch, :],
                                            zthl[0][:, 0:nch, 0, :]),
        ], reads=[ztc[1]], writes=[zthl[1]], tag="c1zh%d" % gi)
        for ch in range(nch):
            t0 = ch * 128
            m = min(128, n - t0)
            st("PE", [lambda e, ch=ch: e.transpose(
                PSB[0][0:32, 0:128],
                zthl[0][:, ch, :, :].rearrange("p a b -> p (a b)"),
                ident_sb[0][:, :])],
                reads=[zthl[1], ident_sb[1]], writes=[PSB[1]], tag="c1zt%d_%d" % (gi, ch))
            st("ACT", [lambda e, t0=t0, m=m, bi=bi: e.activation(
                zfc[bi][0][:, t0:t0 + m], PSB[0][0:32, 0:m], AF.Copy)],
                reads=[PSB[1]], writes=[zfc[bi][1]], tag="c1ze%d_%d" % (gi, ch))
        for p in range(PAIRS):
            zi = p % 2
            st("PE", [lambda e, p=p, zi=zi, n=n, bi=bi: e.matmul(
                PS[zi][0][:, 0:n], zpat_sb[0][:, p, :], zfc[bi][0][:, 0:n],
                start=True, stop=True, skip_group_check=True)],
                reads=[zpat_sb[1], zfc[bi][1]], writes=[PS[zi][1]], tag="c1zb%d_%d" % (gi, p))
            st("ACT", [lambda e, p=p, zi=zi, n=n: e.activation(
                zbc8[0][:, p, 0:n], PS[zi][0][:, 0:n], AF.Copy)],
                reads=[PS[zi][1]], writes=[zbc8[1]], tag="c1zc%d_%d" % (gi, p))
        for p in range(PAIRS):
            oi = 2 + p % 2
            ems = []
            for hh in range(2):
                for hl in range(2):
                    ems.append(lambda e, oi=oi, hh=hh, hl=hl, p=p, n=n, bi=bi: e.matmul(
                        PS[oi][0][64 * hh:64 * (hh + 1), 0:n],
                        kvbz[0][:, p, hl, hh, :],
                        Qt[bi][0][:, p, 0:n],
                        start=(hl == 0), stop=(hl == 1),
                        skip_group_check=True, tile_position=(0, 64 * hh)))
            st("PE", ems, reads=[kvbz[1], Qt[bi][1]], writes=[PS[oi][1]],
               tag="c1o%d_%d" % (gi, p))
            st("ACT", [lambda e, oi=oi, p=p, n=n: e.activation(
                of_[p % 2][0][:, 0:n], PS[oi][0][:, 0:n], AF.Copy)],
                reads=[PS[oi][1]], writes=[of_[p % 2][1]], tag="c1of%d_%d" % (gi, p))
            st("DVE", [lambda e, p=p, n=n: e.tensor_mul(
                otc[0][:, p, 0:n], of_[p % 2][0][:, 0:n], zbc8[0][:, p, 0:n])],
                reads=[of_[p % 2][1], zbc8[1]], writes=[otc[1]], tag="c1oe%d_%d" % (gi, p))
        for dt in range(PAIRS):
            bidx = dt % 2
            ems = [lambda e, c=c, dt=dt, bidx=bidx, n=n: e.matmul(
                PS[bidx][0][:, 0:n], wo_sb[0][:, c, dt * 128:(dt + 1) * 128],
                otc[0][:, c, 0:n], start=(c == 0), stop=(c == CH - 1),
                skip_group_check=True) for c in range(CH)]
            st("PE", ems, reads=[wqr[3], wor[3], otc[1]], writes=[PS[bidx][1]],
               tag="c1wo%d_%d" % (gi, dt))
            st("ACT", [lambda e, bidx=bidx, dt=dt, n=n: e.activation(
                uf[dt % 2][0][:, 0:n], PS[bidx][0][:, 0:n], AF.Identity,
                bias=vec["bo"][0][:, dt:dt + 1])],
                reads=[PS[bidx][1], vec["bo"][1]], writes=[uf[dt % 2][1]],
                tag="c1uf%d_%d" % (gi, dt))
            st("DVE", [lambda e, dt=dt, bi=bi, n=n: e.tensor_add(
                ln0f[bi][0][:, dt, 0:n], ln0f[bi][0][:, dt, 0:n],
                uf[dt % 2][0][:, 0:n])],
                reads=[uf[dt % 2][1], ln0fb[bi]], writes=[ln0f[bi][1]],
                tag="c1u%d_%d" % (gi, dt))
        st("ACT", [lambda e, bi=bi, n=n: e.activation(
            bsc[bi][0][:, :, 0:n], ln0f[bi][0][:, :, 0:n], AF.Copy)],
            reads=[ln0f[bi][1], ln0fb[bi]], writes=[bsc[bi][1]], tag="c1ub%d" % gi)
        st("ACT", [lambda e, bi=bi, n=n: e.activation(
            sqc[bi][0][:, :, 0:n], bsc[bi][0][:, :, 0:n], AF.Square)],
            reads=[bsc[bi][1]], writes=[sqc[bi][1]], tag="c1us%d" % gi)
        stat_mms(2, 3, lambda c, bi=bi, n=n: bsc[bi][0][:, c, 0:n],
                 lambda c, bi=bi, n=n: sqc[bi][0][:, c, 0:n],
                 bsc[bi][1], sqc[bi][1], n, tag="c1sm%d" % gi)

    def c1_tail(gi):
        w0, n = CG[gi]
        bi = gi % 2
        row_math(2, 3, n, rsbc1[bi], msbc1[bi], si=1, tag="c1rm%d" % gi)
        for eng, c0, c1, wrs in (("DVE", 0, 5, (x1c[1],)), ("POOL", 5, CH, (x1cb,))):
            ems = []
            for c in range(c0, c1):
                ems.append(lambda e, bi=bi, c=c, n=n: e.tensor_mul(
                    x1c[0][:, c, 0:n], ln0f[bi][0][:, c, 0:n], rsbc1[bi][0][:, 0:n]))
                ems.append(lambda e, bi=bi, c=c, n=n: e.tensor_sub(
                    x1c[0][:, c, 0:n], x1c[0][:, c, 0:n], msbc1[bi][0][:, 0:n]))
            st(eng, ems, reads=[ln0f[bi][1], ln0fb[bi], rsbc1[bi][1], msbc1[bi][1]],
               writes=list(wrs), tag="c1xa%s%d" % (eng, gi))
        st("ACT", [lambda e, c=c, n=n: e.activation(
            x1c[0][:, c, 0:n], x1c[0][:, c, 0:n], AF.Identity,
            bias=vec["be1"][0][:, c:c + 1], scale=vec["g1"][0][:, c:c + 1])
            for c in range(CH)],
            reads=[vec["be1"][1], vec["g1"][1], x1c[1], x1cb],
            writes=[x1c[1], x1cb], tag="c1xg%d" % gi)
        st("ACT", [lambda e, w0=w0, n=n: e.dma_start(
            out=x1f_d[:, w0:w0 + n].rearrange("(c p) n -> p c n", p=128),
            in_=x1c[0][:, :, 0:n])], reads=[x1c[1]], dma=True, tag="c1d1%d" % gi)
        st("POOL", [lambda e, w0=w0, n=n: e.dma_start(
            out=x1h_d[:, w0:w0 + n].rearrange("(c p) n -> p c n", p=128),
            in_=x1c[0][:, :, 0:n])], reads=[x1c[1]], dma=True, tag="c1d2%d" % gi)
        if taps >= 1:
            st("SP", [lambda e, w0=w0, n=n: e.dma_start(
                out=tap_ap["x1"][:, w0:w0 + n].rearrange("(c p) n -> p c n", p=128),
                in_=x1c[0][:, :, 0:n])], reads=[x1c[1]], dma=True, tag="tapx1%d" % gi)

    c1_head(0)
    for gi in range(len(CG)):
        c1_main(gi)
        if gi + 1 < len(CG):
            c1_head(gi + 1)
        c1_tail(gi)

    if upto == "C1":
        finish()
        return _replay(nc, B)
    barh2 = barrier("barC2")
    for g in reversed(c1g):
        g.__exit__(None, None, None)
    for g in reversed(kvg):
        g.__exit__(None, None, None)
    for g in reversed(wqg):
        g.__exit__(None, None, None)

    c2ag = []

    def C2ASB(name, shape, dt):
        g = nc.sbuf_tensor(name, shape, dt)
        t = g.__enter__()
        c2ag.append(g)
        return t, Res()

    c2wg = []

    def C2WSB(name, shape, dt):
        g = nc.sbuf_tensor(name, shape, dt)
        t = g.__enter__()
        c2wg.append(g)
        return t, Res()

    w2_sb = C2WSB("w2_sb", [128, FFT, D], BF16)
    w2q = [Res() for _ in range(4)]

    w1_sb = C2ASB("w1_sb", [128, CH, FF], BF16)
    w1q = [Res() for _ in range(8)]
    x1hc = [C2ASB("c2x1h%d" % i, [128, CH, 257], BF16) for i in range(2)]
    h1c = [C2ASB("c2h1%d" % i, [128, FFT, 257], BF16) for i in range(2)]

    for q in range(8):
        st("POOL", [lambda e, q=q: e.dma_start(
            out=w1_sb[0][:, :, q * 512:(q + 1) * 512],
            in_=wdr["w1"][:, q * 512:(q + 1) * 512].rearrange("(c p) f -> p c f", p=128))],
            writes=[w1q[q]], dma=True, dsem="qW1", tag="c2aw%d" % q)
    for q in range(4):
        st("POOL", [lambda e, q=q: e.dma_start(
            out=w2_sb[0][:, :, q * 256:(q + 1) * 256],
            in_=wdr["w2"][:, q * 256:(q + 1) * 256].rearrange("(f p) d -> p f d", p=128))],
            writes=[w2q[q]], dma=True, dsem="qW2", tag="c2bw%d" % q)

    for gi, (w0, n) in enumerate(C2G):
        bi = gi % 2
        st("SP", [lambda e, bi=bi, w0=w0, n=n: e.dma_start(
            out=x1hc[bi][0][:, :, 0:n],
            in_=x1h_d[:, w0:w0 + n].rearrange("(c p) n -> p c n", p=128))],
            writes=[x1hc[bi][1]], dma=True, tag="c2alh%d" % gi)
        for f in range(FFT):
            bidx = f % 4
            ems = [lambda e, c=c, f=f, bidx=bidx, bi=bi, n=n: e.matmul(
                PS[bidx][0][:, 0:n], w1_sb[0][:, c, f * 128:(f + 1) * 128],
                x1hc[bi][0][:, c, 0:n], start=(c == 0), stop=(c == CH - 1),
                skip_group_check=True) for c in range(CH)]
            st("PE", ems, reads=[w1q[7], x1hc[bi][1]], writes=[PS[bidx][1]],
               tag="c2w1_%d_%d" % (gi, f))
            st("ACT", [lambda e, f=f, bidx=bidx, bi=bi, n=n: e.activation(
                h1c[bi][0][:, f, 0:n], PS[bidx][0][:, 0:n], AF.Gelu,
                bias=b1_sb[0][:, f:f + 1])],
                reads=[PS[bidx][1], b1_sb[1]], writes=[h1c[bi][1]], tag="c2g%d_%d" % (gi, f))
        st("ACT", [lambda e, bi=bi, w0=w0, n=n: e.dma_start(
            out=h1_d[:, w0:w0 + n].rearrange("(f p) n -> p f n", p=128),
            in_=h1c[bi][0][:, :, 0:n])], reads=[h1c[bi][1]], dma=True, tag="c2ast%d" % gi)

    barh2b = barrier("barC2b")
    for g in reversed(c2ag):
        g.__exit__(None, None, None)

    c2g = []

    def C2SB(name, shape, dt):
        g = nc.sbuf_tensor(name, shape, dt)
        t = g.__enter__()
        c2g.append(g)
        return t, Res()

    h1b = [C2SB("c2bh%d" % i, [128, FFT, 257], BF16) for i in range(2)]
    x1fc = [C2SB("c2x1f%d" % i, [128, CH, 257], F32) for i in range(2)]
    vc = C2SB("c2v", [128, CH, 257], F32)
    tc = C2SB("c2t", [128, CH, 257], F32)
    ac = tc
    vf = [C2SB("c2vf%d" % i, [128, 257], F32) for i in range(2)]
    vbc = C2SB("c2vb", [128, CH, 257], BF16)
    vsqc = C2SB("c2vsq", [128, CH, 257], BF16)
    rsb2 = C2SB("c2rsb", [128, 257], F32)
    msb2 = C2SB("c2msb", [128, 257], F32)

    def c2b_head(gi):
        w0, n = C2G[gi]
        bi = gi % 2
        st("SP", [
            lambda e, bi=bi, w0=w0, n=n: e.dma_start(
                out=h1b[bi][0][:, :, 0:n],
                in_=h1_d[:, w0:w0 + n].rearrange("(f p) n -> p f n", p=128)),
            lambda e, bi=bi, w0=w0, n=n: e.dma_start(
                out=x1fc[bi][0][:, :, 0:n],
                in_=x1f_d[:, w0:w0 + n].rearrange("(c p) n -> p c n", p=128)),
        ], writes=[h1b[bi][1], x1fc[bi][1]], dma=True, tag="c2bl%d" % gi)

    def c2b_main(gi):
        w0, n = C2G[gi]
        bi = gi % 2
        for dt in range(CH):
            bidx = 4 + dt % 2
            ems = [lambda e, f=f, dt=dt, bidx=bidx, bi=bi, n=n: e.matmul(
                PS[bidx][0][:, 0:n], w2_sb[0][:, f, dt * 128:(dt + 1) * 128],
                h1b[bi][0][:, f, 0:n], start=(f == 0), stop=(f == FFT - 1),
                skip_group_check=True) for f in range(FFT)]
            st("PE", ems, reads=[w2q[3], h1b[bi][1]], writes=[PS[bidx][1]],
               tag="c2w2_%d_%d" % (gi, dt))
            st("ACT", [lambda e, bidx=bidx, dt=dt, n=n: e.activation(
                vf[dt % 2][0][:, 0:n], PS[bidx][0][:, 0:n], AF.Identity,
                bias=vec["b2"][0][:, dt:dt + 1])],
                reads=[PS[bidx][1], vec["b2"][1]], writes=[vf[dt % 2][1]],
                tag="c2ve%d_%d" % (gi, dt))
            st("DVE", [lambda e, dt=dt, bi=bi, n=n: e.tensor_add(
                vc[0][:, dt, 0:n], vf[dt % 2][0][:, 0:n], x1fc[bi][0][:, dt, 0:n])],
                reads=[vf[dt % 2][1], x1fc[bi][1]], writes=[vc[1]],
                tag="c2v%d_%d" % (gi, dt))
        st("ACT", [lambda e, n=n: e.activation(vbc[0][:, :, 0:n], vc[0][:, :, 0:n],
                                               AF.Copy)],
           reads=[vc[1]], writes=[vbc[1]], tag="c2vb%d" % gi)
        st("ACT", [lambda e, n=n: e.activation(vsqc[0][:, :, 0:n], vbc[0][:, :, 0:n],
                                               AF.Square)],
           reads=[vbc[1]], writes=[vsqc[1]], tag="c2vs%d" % gi)
        stat_mms(2, 3, lambda c, n=n: vbc[0][:, c, 0:n], lambda c, n=n: vsqc[0][:, c, 0:n],
                 vbc[1], vsqc[1], n, tag="c2sm%d" % gi)

    def c2b_tail(gi):
        w0, n = C2G[gi]
        row_math(2, 3, n, rsb2, msb2, si=1, tag="c2rm%d" % gi)
        ems = []
        for c in range(CH):
            ems.append(lambda e, c=c, n=n: e.tensor_mul(
                tc[0][:, c, 0:n], vc[0][:, c, 0:n], rsb2[0][:, 0:n]))
            ems.append(lambda e, c=c, n=n: e.tensor_sub(
                tc[0][:, c, 0:n], tc[0][:, c, 0:n], msb2[0][:, 0:n]))
        st("DVE", ems, reads=[vc[1], rsb2[1], msb2[1]], writes=[tc[1]], tag="c2t%d" % gi)
        st("ACT", [lambda e, c=c, n=n: e.activation(
            ac[0][:, c, 0:n], tc[0][:, c, 0:n], AF.Identity,
            bias=vec["be2"][0][:, c:c + 1], scale=vec["g2"][0][:, c:c + 1])
            for c in range(CH)],
            reads=[tc[1], vec["be2"][1], vec["g2"][1]], writes=[ac[1]], tag="c2a%d" % gi)
        st("ACT", [lambda e, w0=w0, n=n: e.dma_start(
            out=a_d[:, w0:w0 + n].rearrange("(c p) n -> p c n", p=128),
            in_=ac[0][:, :, 0:n])], reads=[ac[1]], dma=True, tag="c2d%d" % gi)
        if taps >= 1:
            st("SP", [lambda e, w0=w0, n=n: e.dma_start(
                out=tap_ap["a"][:, w0:w0 + n].rearrange("(c p) n -> p c n", p=128),
                in_=ac[0][:, :, 0:n])], reads=[ac[1]], dma=True, tag="tapa%d" % gi)

    c2b_head(0)
    for gi in range(len(C2G)):
        c2b_main(gi)
        if gi + 1 < len(C2G):
            c2b_head(gi + 1)
        c2b_tail(gi)

    barh3 = barrier("barD")
    for g in reversed(c2g):
        g.__exit__(None, None, None)
    for g in reversed(c2wg):
        g.__exit__(None, None, None)

    dg = []

    def DSB(name, shape, dt):
        g = nc.sbuf_tensor(name, shape, dt)
        t = g.__enter__()
        dg.append(g)
        return t, Res()

    DG2 = [(1 + 256 * i, 256) for i in range(8)]
    NN = 257
    wr_sb = DSB("wr_sb", [128, CH, D], BF16)
    wk2_sb = DSB("wk2_sb", [128, CH, D], BF16)
    wv2_sb = DSB("wv2_sb", [128, CH, D], BF16)
    wrr = [Res() for _ in range(4)]
    wkr = [Res() for _ in range(4)]
    wvr = [Res() for _ in range(4)]
    agb = [DSB("d_a%d" % i, [128, CH, NN], F32) for i in range(2)]
    abf = [DSB("d_ab%d" % i, [128, CH, NN], BF16) for i in range(2)]
    xgb = DSB("d_xg", [128, CH, NN], F32)
    xmid = [DSB("d_xm%d" % i, [128, CH, NN], F32) for i in range(2)]
    xmidb = [Res() for _ in range(2)]
    ln2 = [DSB("d_ln2%d" % i, [128, CH, NN], F32) for i in range(2)]
    ln2b = [Res() for _ in range(2)]
    asq = DSB("d_asq", [128, CH, NN], BF16)
    bb1 = DSB("d_bb1", [128, CH, NN], BF16)
    bb2 = DSB("d_bb2", [128, CH, NN], BF16)
    xrb = [DSB("d_xr%d" % i, [128, CH, 256], BF16) for i in range(2)]
    xkb = [DSB("d_xk%d" % i, [128, CH, 256], BF16) for i in range(2)]
    scr2 = DSB("d_s2", [128, CH, 256], F32)
    rb_ = DSB("d_rb", [128, CH, 256], F32)
    ff_ = [DSB("d_ff%d" % i, [128, 256], F32) for i in range(2)]
    dif = abf
    kab = xrb
    rsbd = [DSB("d_rsb%d" % i, [128, NN], F32) for i in range(2)]
    msbd = [DSB("d_msb%d" % i, [128, NN], F32) for i in range(2)]
    rsbdF = DSB("d_rsbF", [128, 256], F32)
    msbdF = DSB("d_msbF", [128, 256], F32)
    bbf1 = xkb
    bbf2 = xrb

    for nm, tsb, rr, ds in (("wr", wr_sb, wrr, "qWr"), ("wk2", wk2_sb, wkr, "qWk2"),
                            ("wv2", wv2_sb, wvr, "qWv2")):
        for j in range(4):
            st("POOL", [lambda e, nm=nm, tsb=tsb, j=j: e.dma_start(
                out=tsb[0][:, :, j * 256:(j + 1) * 256],
                in_=wdr[nm][:, j * 256:(j + 1) * 256].rearrange(
                    "(c p) d -> p c d", p=128))],
                writes=[rr[j]], dma=True, dsem=ds, tag="dw%s%d" % (nm, j))

    def d_head(gi):
        w0, n = DG2[gi]
        cw0 = w0 - 1
        nn = n + 1
        bi = gi % 2
        st("SP", [
            lambda e, bi=bi, cw0=cw0, nn=nn: e.dma_start(
                out=agb[bi][0][:, :, 0:nn],
                in_=a_d[:, cw0:cw0 + nn].rearrange("(c p) n -> p c n", p=128)),
            lambda e, bi=bi, cw0=cw0, nn=nn: e.dma_start(
                out=xgb[0][:, :, 0:nn],
                in_=xTo[:, cw0:cw0 + nn].rearrange("(c p) n -> p c n", p=128)),
        ], writes=[agb[bi][1], xgb[1]], dma=True, tag="dl%d" % gi)
        st("POOL", [lambda e, bi=bi, cw0=cw0, nn=nn: e.dma_start(
            out=abf[bi][0][:, :, 0:nn],
            in_=a_d[:, cw0:cw0 + nn].rearrange("(c p) n -> p c n", p=128))],
            writes=[abf[bi][1]], dma=True, tag="dlb%d" % gi)
        st("ACT", [lambda e, bi=bi, nn=nn: e.activation(
            asq[0][:, :, 0:nn], abf[bi][0][:, :, 0:nn], AF.Square)],
            reads=[abf[bi][1]], writes=[asq[1]], tag="das%d" % gi)
        stat_mms(2, 3, lambda c, bi=bi, nn=nn: abf[bi][0][:, c, 0:nn],
                 lambda c, nn=nn: asq[0][:, c, 0:nn],
                 abf[bi][1], asq[1], nn, tag="dsm%d" % gi)

    def d_chain(gi):
        w0, n = DG2[gi]
        nn = n + 1
        bi = gi % 2
        row_math(2, 3, nn, rsbd[bi], msbd[bi], si=0, tag="drm%d" % gi)
        for eng, c0, c1, wrs in (("DVE", 0, 6, (xmid[bi][1],)),
                                 ("POOL", 6, CH, (xmidb[bi],))):
            ems = []
            for c in range(c0, c1):
                ems.append(lambda e, bi=bi, c=c, nn=nn: e.tensor_mul(
                    xmid[bi][0][:, c, 0:nn], agb[bi][0][:, c, 0:nn], rsbd[bi][0][:, 0:nn]))
                ems.append(lambda e, bi=bi, c=c, nn=nn: e.tensor_sub(
                    xmid[bi][0][:, c, 0:nn], xmid[bi][0][:, c, 0:nn], msbd[bi][0][:, 0:nn]))
                ems.append(lambda e, bi=bi, c=c, nn=nn: e.tensor_add(
                    xmid[bi][0][:, c, 0:nn], xmid[bi][0][:, c, 0:nn], xgb[0][:, c, 0:nn]))
            st(eng, ems, reads=[agb[bi][1], rsbd[bi][1], msbd[bi][1], xgb[1]],
               writes=list(wrs), tag="dxm%s%d" % (eng, gi))
        st("ACT", [lambda e, bi=bi, nn=nn: e.activation(
            bb1[0][:, :, 0:nn], xmid[bi][0][:, :, 0:nn], AF.Copy)],
            reads=[xmid[bi][1], xmidb[bi]], writes=[bb1[1]], tag="dxb%d" % gi)
        st("ACT", [lambda e, nn=nn: e.activation(bb2[0][:, :, 0:nn], bb1[0][:, :, 0:nn],
                                                 AF.Square)],
           reads=[bb1[1]], writes=[bb2[1]], tag="dxs%d" % gi)
        stat_mms(2, 3, lambda c, nn=nn: bb1[0][:, c, 0:nn],
                 lambda c, nn=nn: bb2[0][:, c, 0:nn],
                 bb1[1], bb2[1], nn, tag="dsm2%d" % gi)
        row_math(2, 3, nn, rsbd[bi], msbd[bi], si=0, tag="drm2%d" % gi)
        if gi == 0:
            st("DVE", [
                lambda e: e.tensor_mul(rsbd[0][0][:, 0:1], rsbd[0][0][:, 0:1],
                                       vec["cmask"][0][:, 0:1]),
                lambda e: e.tensor_mul(msbd[0][0][:, 0:1], msbd[0][0][:, 0:1],
                                       vec["cmask"][0][:, 0:1]),
            ], reads=[vec["cmask"][1]], writes=[rsbd[0][1], msbd[0][1]], tag="dmask")
        for eng, c0, c1, wrs in (("DVE", 0, 6, (ln2[bi][1],)),
                                 ("POOL", 6, CH, (ln2b[bi],))):
            ems = []
            for c in range(c0, c1):
                ems.append(lambda e, bi=bi, c=c, nn=nn: e.tensor_mul(
                    ln2[bi][0][:, c, 0:nn], xmid[bi][0][:, c, 0:nn], rsbd[bi][0][:, 0:nn]))
                ems.append(lambda e, bi=bi, c=c, nn=nn: e.tensor_sub(
                    ln2[bi][0][:, c, 0:nn], ln2[bi][0][:, c, 0:nn], msbd[bi][0][:, 0:nn]))
            st(eng, ems, reads=[xmid[bi][1], xmidb[bi], rsbd[bi][1], msbd[bi][1]],
               writes=list(wrs), tag="dln2%s%d" % (eng, gi))
        st("DVE", [lambda e, bi=bi, c=c, n=n: e.tensor_sub(
            dif[bi][0][:, c, 0:n], ln2[bi][0][:, c, 1:n + 1], ln2[bi][0][:, c, 0:n])
            for c in range(CH)],
            reads=[ln2[bi][1], ln2b[bi]], writes=[dif[bi][1]], tag="ddif%d" % gi)
        ems = []
        for c in range(CH):
            ems.append(lambda e, bi=bi, c=c, n=n: e.scalar_tensor_tensor(
                xrb[bi][0][:, c, 0:n], dif[bi][0][:, c, 0:n], vec["mur"][0][:, c:c + 1],
                ln2[bi][0][:, c, 0:n], ALU.mult, ALU.add))
            ems.append(lambda e, bi=bi, c=c, n=n: e.scalar_tensor_tensor(
                xkb[bi][0][:, c, 0:n], dif[bi][0][:, c, 0:n], vec["muk"][0][:, c:c + 1],
                ln2[bi][0][:, c, 0:n], ALU.mult, ALU.add))
        st("DVE", ems, reads=[ln2[bi][1], ln2b[bi], dif[bi][1],
                              vec["mur"][1], vec["muk"][1]],
           writes=[xrb[bi][1], xkb[bi][1]], tag="dxrk%d" % gi)

    def d_ffn_a(gi):
        w0, n = DG2[gi]
        bi = gi % 2
        for dt in range(CH):
            bidx = dt % 2
            ems = [lambda e, c=c, dt=dt, bidx=bidx, bi=bi, n=n: e.matmul(
                PS[bidx][0][:, 0:n], wr_sb[0][:, c, dt * 128:(dt + 1) * 128],
                xrb[bi][0][:, c, 0:n], start=(c == 0), stop=(c == CH - 1),
                skip_group_check=True) for c in range(CH)]
            st("PE", ems, reads=[wrr[3], xrb[bi][1]], writes=[PS[bidx][1]],
               tag="dwr%d_%d" % (gi, dt))
            st("ACT", [lambda e, bidx=bidx, dt=dt, n=n: e.activation(
                rb_[0][:, dt, 0:n], PS[bidx][0][:, 0:n], AF.Sigmoid)],
                reads=[PS[bidx][1]], writes=[rb_[1]], tag="drs%d_%d" % (gi, dt))
        for dt in range(CH):
            bidx = dt % 2
            ems = [lambda e, c=c, dt=dt, bidx=bidx, bi=bi, n=n: e.matmul(
                PS[bidx][0][:, 0:n], wk2_sb[0][:, c, dt * 128:(dt + 1) * 128],
                xkb[bi][0][:, c, 0:n], start=(c == 0), stop=(c == CH - 1),
                skip_group_check=True) for c in range(CH)]
            st("PE", ems, reads=[wkr[3], xkb[bi][1]], writes=[PS[bidx][1]],
               tag="dwk%d_%d" % (gi, dt))
            st("ACT", [lambda e, bidx=bidx, dt=dt, n=n: e.activation(
                scr2[0][:, dt, 0:n], PS[bidx][0][:, 0:n], AF.Relu)],
                reads=[PS[bidx][1]], writes=[scr2[1]], tag="dkr%d_%d" % (gi, dt))
        st("ACT", [lambda e, bi=bi, n=n: e.activation(
            kab[bi][0][:, :, 0:n], scr2[0][:, :, 0:n], AF.Square)],
            reads=[scr2[1]], writes=[kab[bi][1]], tag="dka%d" % gi)

    def d_ffn_b(gi):
        w0, n = DG2[gi]
        bi = gi % 2
        for dt in range(CH):
            bidx = dt % 2
            ems = [lambda e, c=c, dt=dt, bidx=bidx, bi=bi, n=n: e.matmul(
                PS[bidx][0][:, 0:n], wv2_sb[0][:, c, dt * 128:(dt + 1) * 128],
                kab[bi][0][:, c, 0:n], start=(c == 0), stop=(c == CH - 1),
                skip_group_check=True) for c in range(CH)]
            st("PE", ems, reads=[wvr[3], kab[bi][1]], writes=[PS[bidx][1]],
               tag="dwv%d_%d" % (gi, dt))
            st("ACT", [lambda e, bidx=bidx, dt=dt, n=n: e.activation(
                ff_[dt % 2][0][:, 0:n], PS[bidx][0][:, 0:n], AF.Copy)],
                reads=[PS[bidx][1]], writes=[ff_[dt % 2][1]], tag="dfe%d_%d" % (gi, dt))
            st("DVE", [lambda e, dt=dt, n=n: e.tensor_mul(
                scr2[0][:, dt, 0:n], ff_[dt % 2][0][:, 0:n], rb_[0][:, dt, 0:n])],
                reads=[ff_[dt % 2][1], rb_[1]], writes=[scr2[1]], tag="df%d_%d" % (gi, dt))
        st("ACT", [lambda e, bi=bi, n=n: e.activation(
            bbf1[bi][0][:, :, 0:n], scr2[0][:, :, 0:n], AF.Copy)],
            reads=[scr2[1]], writes=[bbf1[bi][1]], tag="dfb%d" % gi)
        st("ACT", [lambda e, bi=bi, n=n: e.activation(
            bbf2[bi][0][:, :, 0:n], bbf1[bi][0][:, :, 0:n], AF.Square)],
            reads=[bbf1[bi][1]], writes=[bbf2[bi][1]], tag="dfs%d" % gi)
        stat_mms(4, 5, lambda c, bi=bi, n=n: bbf1[bi][0][:, c, 0:n],
                 lambda c, bi=bi, n=n: bbf2[bi][0][:, c, 0:n],
                 bbf1[bi][1], bbf2[bi][1], n, tag="dsm3%d" % gi)
        row_math(4, 5, n, rsbdF, msbdF, si=2, tag="drm3%d" % gi)
        ems = []
        for c in range(CH):
            ems.append(lambda e, c=c, n=n: e.tensor_mul(
                scr2[0][:, c, 0:n], scr2[0][:, c, 0:n], rsbdF[0][:, 0:n]))
            ems.append(lambda e, c=c, n=n: e.tensor_sub(
                scr2[0][:, c, 0:n], scr2[0][:, c, 0:n], msbdF[0][:, 0:n]))
            ems.append(lambda e, bi=bi, c=c, n=n: e.tensor_add(
                scr2[0][:, c, 0:n], scr2[0][:, c, 0:n], xmid[bi][0][:, c, 1:1 + n]))
        st("DVE", ems, reads=[rsbdF[1], msbdF[1], xmid[bi][1], xmidb[bi]],
           writes=[scr2[1]], tag="dout%d" % gi)
        st("SP", [lambda e, w0=w0, n=n: e.dma_start(
            out=outT[:, w0 - 1:w0 - 1 + n].rearrange("(c p) n -> p c n", p=128),
            in_=scr2[0][:, :, 0:n])], reads=[scr2[1]], dma=True, tag="dst%d" % gi)

    d_head(0)
    d_chain(0)
    for gi in range(len(DG2)):
        if gi + 1 < len(DG2):
            d_head(gi + 1)
        d_ffn_a(gi)
        if gi + 1 < len(DG2):
            d_chain(gi + 1)
        d_ffn_b(gi)

    st("SP", [lambda e: e.nop()], extra=[("qSP", B.cnt["qSP"])], tag="fin")

    nc2 = _replay(nc, B)
    for g in reversed(semg_dummy) if False else []:
        pass
    for g in reversed(dg):
        g.__exit__(None, None, None)
    for g in reversed(guards):
        g.__exit__(None, None, None)
    return nc2


def _replay(nc, B):
    sems = {}
    semg = []
    with nc.Block() as block:
        for name in ("sPE", "sACT", "sDVE", "sPOOL", "sSP", "qSP", "qACT", "qGP", "qWA", "qWB", "qW1", "qW2", "qWr", "qWk2", "qWv2"):
            g = nc.semaphore(name)
            sems[name] = g.__enter__()
            semg.append(g)
        eng_map = {"PE": block.tensor, "ACT": block.scalar, "DVE": block.vector,
                   "POOL": block.gpsimd, "SP": block.sync}

        def mk(eng_key):
            def body(e):
                for (eng, waits, emits, sem, dma, tag) in B.ops:
                    if eng != eng_key:
                        continue
                    for sname, v in sorted(waits.items()):
                        e.wait_ge(sems[sname], v)
                    last = None
                    for fn in emits:
                        inst = fn(e)
                        if dma:
                            inst.then_inc(sems[sem], 16)
                        else:
                            last = inst
                    if not dma and last is not None:
                        last.then_inc(sems[sem], 1)
            return body

        for k, dec in eng_map.items():
            dec(mk(k))
    for g in reversed(semg):
        g.__exit__(None, None, None)
    return nc


def _stage_inputs(inputs):
    f32 = np.float32
    x = np.asarray(inputs["L0_attn_src"], f32)
    col = lambda v: np.ascontiguousarray(np.asarray(v, f32).reshape(-1, 128).T)
    shared = {
        "wq": np.ascontiguousarray(np.asarray(inputs["L0_attn_Wq"], f32)),
        "wk": np.ascontiguousarray(np.asarray(inputs["L0_attn_Wk"], f32)),
        "wv": np.ascontiguousarray(np.asarray(inputs["L0_attn_Wv"], f32)),
        "wo": np.ascontiguousarray(np.asarray(inputs["L0_attn_Wo"], f32)),
        "w1": np.ascontiguousarray(np.asarray(inputs["L0_attn_W1"], f32)),
        "w2": np.ascontiguousarray(np.asarray(inputs["L0_attn_W2"], f32)),
        "wr": np.ascontiguousarray(np.asarray(inputs["L0_ffn_W_r"], f32).T),
        "wk2": np.ascontiguousarray(np.asarray(inputs["L0_ffn_W_k"], f32).T),
        "wv2": np.ascontiguousarray(np.asarray(inputs["L0_ffn_W_v"], f32).T),
        "bq": col(inputs["L0_attn_bq"]),
        "bo": col(inputs["L0_attn_bo"]),
        "b2": col(inputs["L0_attn_b2"]),
        "g1": col(inputs["L0_attn_g1"]),
        "be1": col(inputs["L0_attn_be1"]),
        "g2": col(inputs["L0_attn_g2"]),
        "be2": col(inputs["L0_attn_be2"]),
        "mur": col(inputs["L0_ffn_mu_r"]),
        "muk": col(inputs["L0_ffn_mu_k"]),
        "b1c": col(inputs["L0_attn_b1"]),
        "bkrow": np.ascontiguousarray(np.asarray(inputs["L0_attn_bk"], f32)[None, :]),
        "bvrow": np.ascontiguousarray(np.asarray(inputs["L0_attn_bv"], f32)[None, :]),
        "ident": np.eye(128, dtype=f32),
    }
    zpat = np.zeros((32, PAIRS, 128), f32)
    for p in range(PAIRS):
        for hl in range(2):
            zpat[hl * 16 + 2 * p, p, 0:64] = 1.0
            zpat[hl * 16 + 2 * p + 1, p, 64:128] = 1.0
    shared["zpat"] = zpat
    masks = [np.zeros((128, CH), f32), np.ones((128, CH), f32)]
    xTb = [np.ascontiguousarray(
        np.concatenate([np.zeros((D, 1), f32), x[b].T], axis=1)) for b in range(4)]
    in_maps = []
    for c in range(8):
        b, h = c // 2, c % 2
        m = dict(shared)
        m["xT"] = xTb[b]
        m["xTo"] = np.ascontiguousarray(xTb[b][:, h * 2048:h * 2048 + OWN])
        m["cmask"] = masks[h]
        in_maps.append(m)
    return in_maps


def kernel(**inputs):
    in_maps = _stage_inputs(inputs)
    nc = build_program(taps=0)
    res = run_bass_kernel_spmd(nc, in_maps, core_ids=list(range(8)))
    out = np.empty((4, 4096, 1024), np.float32)
    for c in range(8):
        b, h = c // 2, c % 2
        out[b, h * 2048:(h + 1) * 2048, :] = res.results[c]["outT"].T
    return out


# version 13
# speedup vs baseline: 1.0348x; 1.0102x over previous
import numpy as np
import concourse.bass as bass
import concourse.mybir as mybir
from concourse.bass_utils import run_bass_kernel_spmd

F32 = mybir.dt.float32
BF16 = mybir.dt.bfloat16
AF = mybir.ActivationFunctionType
ALU = mybir.AluOpType

D = 1024
CH = 8
H = 16
PAIRS = 8
S = 4096
OWN = 2049
FF = 4096
FFT = 32
EPS_LN = 1e-5
EPS_Z = 1e-6

AG = [(1 + 256 * g, 256) for g in range(16)]
CG = [(0, 385), (385, 416), (801, 416), (1217, 416), (1633, 416)]
C2G = [(0, 257)] + [(257 + 256 * i, 256) for i in range(7)]
DG = [(1, 410), (411, 410), (821, 410), (1231, 410), (1641, 408)]

ENGS = ("PE", "ACT", "DVE", "POOL", "SP")


class Res:
    __slots__ = ("w", "r")

    def __init__(self):
        self.w = None
        self.r = []

    def rdep(self):
        return [self.w] if self.w else []

    def wdep(self):
        d = list(self.r)
        if self.w:
            d.append(self.w)
        return d


class Builder:
    def __init__(self):
        self.ops = []
        self.cnt = {}
        self.gate = []

    def st(self, eng, emits, reads=(), writes=(), extra=(), dma=False, tag="",
           dsem=None):
        if not isinstance(emits, (list, tuple)):
            emits = [emits]
        deps = [d for d in extra if d is not None] + list(self.gate)
        for r in reads:
            deps += r.rdep()
        for r in writes:
            deps += r.wdep()
        if dma:
            sem = dsem or {"SP": "qSP", "ACT": "qACT", "POOL": "qGP"}[eng]
            amt = 16 * len(emits)
        else:
            sem = "s" + eng
            amt = 1
        own = None if dma else sem
        waits = {}
        for s, v in deps:
            if s == own:
                continue
            if waits.get(s, -1) < v:
                waits[s] = v
        cur = self.cnt.get(sem, 0) + amt
        self.cnt[sem] = cur
        self.ops.append((eng, waits, emits, sem, dma, tag))
        h = (sem, cur)
        for r in reads:
            r.r.append(h)
        for r in writes:
            r.w = h
            r.r = []
        return h


def build_program(taps=0, upto=None):
    nc = bass.Bass("TRN2", target_bir_lowering=False, debug=False)
    B = Builder()
    st = B.st

    xT = nc.dram_tensor("xT", [D, S + 1], F32, kind="ExternalInput").ap()
    xTo = nc.dram_tensor("xTo", [D, OWN], F32, kind="ExternalInput").ap()
    wdr = {}
    for nm, shp in (("wq", [D, D]), ("wk", [D, D]), ("wv", [D, D]), ("wo", [D, D]),
                    ("w1", [D, FF]), ("w2", [FF, D]),
                    ("wr", [D, D]), ("wk2", [D, D]), ("wv2", [D, D])):
        wdr[nm] = nc.dram_tensor(nm, shp, F32, kind="ExternalInput").ap()
    vnames = ("bq", "bo", "b2", "g1", "be1", "g2", "be2", "mur", "muk", "cmask")
    vd = {nm: nc.dram_tensor(nm, [128, CH], F32, kind="ExternalInput").ap() for nm in vnames}
    b1c_d = nc.dram_tensor("b1c", [128, FFT], F32, kind="ExternalInput").ap()
    bk_row_d = nc.dram_tensor("bkrow", [1, D], F32, kind="ExternalInput").ap()
    bv_row_d = nc.dram_tensor("bvrow", [1, D], F32, kind="ExternalInput").ap()
    zpat_d = nc.dram_tensor("zpat", [32, PAIRS, 128], F32, kind="ExternalInput").ap()
    ident_d = nc.dram_tensor("ident", [128, 128], F32, kind="ExternalInput").ap()
    x1f_d = nc.dram_tensor("x1f_d", [D, OWN], F32, kind="Internal").ap()
    x1h_d = nc.dram_tensor("x1h_d", [D, OWN], BF16, kind="Internal").ap()
    a_d = nc.dram_tensor("a_d", [D, OWN], F32, kind="Internal").ap()
    h1_d = nc.dram_tensor("h1_d", [FF, OWN], BF16, kind="Internal").ap()
    outT = nc.dram_tensor("outT", [D, S // 2], F32, kind="ExternalOutput").ap()
    tap_ap = {}
    if taps >= 1:
        tap_ap["rl"] = nc.dram_tensor("t_rl", [4, 512], F32, kind="ExternalOutput").ap()
        tap_ap["rsb"] = nc.dram_tensor("t_rsb", [2, 128, 512], F32, kind="ExternalOutput").ap()
        tap_ap["ln0h"] = nc.dram_tensor("t_ln0h", [128, CH, 512], F32, kind="ExternalOutput").ap()
        tap_ap["K"] = nc.dram_tensor("t_K", [128, D], F32, kind="ExternalOutput").ap()
        tap_ap["V"] = nc.dram_tensor("t_V", [128, H * 65], F32, kind="ExternalOutput").ap()
        tap_ap["kf"] = nc.dram_tensor("t_kf", [128, D], F32, kind="ExternalOutput").ap()
        tap_ap["Q"] = nc.dram_tensor("t_Q", [128, CH, 448], F32, kind="ExternalOutput").ap()
        tap_ap["z"] = nc.dram_tensor("t_z", [128, 4, 16], F32, kind="ExternalOutput").ap()
        tap_ap["o"] = nc.dram_tensor("t_o", [128, CH, 448], F32, kind="ExternalOutput").ap()
        tap_ap["u"] = nc.dram_tensor("t_u", [128, CH, 448], F32, kind="ExternalOutput").ap()
        tap_ap["zb"] = nc.dram_tensor("t_zb", [2, 128, 448], F32, kind="ExternalOutput").ap()
        tap_ap["zf"] = nc.dram_tensor("t_zf", [32, 448], F32, kind="ExternalOutput").ap()
        tap_ap["oun"] = nc.dram_tensor("t_oun", [128, 448], F32, kind="ExternalOutput").ap()
        tap_ap["u2"] = nc.dram_tensor("t_u2", [128, CH, 448], F32, kind="ExternalOutput").ap()
        tap_ap["l0"] = nc.dram_tensor("t_l0", [128, CH, 448], F32, kind="ExternalOutput").ap()
        tap_ap["zb2"] = nc.dram_tensor("t_zb2", [128, PAIRS, 448], F32, kind="ExternalOutput").ap()
        tap_ap["kvbz"] = nc.dram_tensor("t_kvbz", [128, PAIRS * 2 * 2 * 64], F32,
                                        kind="ExternalOutput").ap()
        tap_ap["rows"] = nc.dram_tensor("t_rows", [4, S + 1], F32, kind="ExternalOutput").ap()
        tap_ap["kvb"] = nc.dram_tensor("t_kvb", [128, PAIRS * 65], F32, kind="ExternalOutput").ap()
        tap_ap["x1"] = nc.dram_tensor("t_x1", [D, OWN], F32, kind="ExternalOutput").ap()
        tap_ap["a"] = nc.dram_tensor("t_a", [D, OWN], F32, kind="ExternalOutput").ap()

    guards = []

    def SB(name, shape, dt):
        g = nc.sbuf_tensor(name, shape, dt)
        t = g.__enter__()
        guards.append(g)
        return t, Res()

    vec = {nm: SB("v_" + nm, [128, CH], F32) for nm in vnames + ("bq1", "bqn")}
    b1_sb = SB("v_b1", [128, FFT], F32)
    onescol = SB("onescol", [128, 1], BF16)
    zercol = SB("zercol", [128, 64], BF16)
    ones1 = SB("ones1", [1, 128], BF16)
    rw = [{k: SB("rw%d_%s" % (s, k), [128, 512], F32) for k in ("mu", "q", "t")}
          for s in range(2)]
    rw.append(rw[1])
    ones128 = SB("ones128", [128, 128], BF16)
    ones128f = SB("ones128f", [128, 128], F32)
    constvals = (1e-5, 1e-6, -0.5, -1.0)
    ccell = SB("ccell", [128, len(constvals)], F32)

    wqg = []

    def WQSB(name, shape, dt):
        g = nc.sbuf_tensor(name, shape, dt)
        t = g.__enter__()
        wqg.append(g)
        return t, Res()

    wq_sb = WQSB("wq_sb", [128, CH, D], BF16)
    wo_sb = WQSB("wo_sb", [128, CH, D], BF16)
    wqr = [Res() for _ in range(4)]
    wor = [Res() for _ in range(4)]

    kvg = []

    def KVSB(name, shape, dt):
        g = nc.sbuf_tensor(name, shape, dt)
        t = g.__enter__()
        kvg.append(g)
        return t, Res()

    zpat_sb = KVSB("zpat_s", [32, PAIRS, 128], BF16)
    ident_sb = KVSB("ident_s", [128, 128], BF16)
    kvbz = KVSB("kvbz", [128, PAIRS, 2, 2, 64], BF16)
    zrhs = KVSB("zrhs", [128, PAIRS, 4], BF16)

    PS = []
    for i in range(7):
        g = nc.psum_tensor("ps%d" % i, [128, 512], F32)
        PS.append((g.__enter__(), Res()))
        guards.append(g)
    g = nc.psum_tensor("psb", [128, 1024], BF16)
    PSB = (g.__enter__(), Res())
    guards.append(g)

    def bcast2(bi, hi_res, hi_ap, lo_ap, n, extra=(), tag="bc"):
        pb, rb = PS[bi]
        return st("PE", [
            lambda e: e.matmul(pb[:, 0:n], ones1[0][:, :], hi_ap, start=True, stop=False,
                               skip_group_check=True),
            lambda e: e.matmul(pb[:, 0:n], ones1[0][:, :], lo_ap, start=False, stop=True,
                               skip_group_check=True),
        ], reads=[ones1[1], hi_res], writes=[rb], extra=extra, tag=tag)

    def stat_mms(bi_s, bi_q, val_ap, sq_ap, val_res, sq_res, n, extra=(), tag="st",
                 off_s=0, off_q=0, ones_val=None):
        pbs, rbs = PS[bi_s]
        pbq, rbq = PS[bi_q]
        packed = bi_s == bi_q
        ov = ones_val if ones_val is not None else ones128
        ems = []
        for c in range(CH):
            ems.append(lambda e, c=c: e.matmul(
                pbs[:, off_s:off_s + n], ov[0][:, :], val_ap(c), start=(c == 0),
                stop=(c == CH - 1), skip_group_check=True))
        for c in range(CH):
            ems.append(lambda e, c=c: e.matmul(
                pbq[:, off_q:off_q + n], ones128[0][:, :], sq_ap(c),
                start=(c == 0 and not packed),
                stop=(c == CH - 1), skip_group_check=True))
        return st("PE", ems, reads=[ones128[1], val_res, sq_res], writes=[rbs, rbq],
                  extra=extra, tag=tag)

    def row_math(bi_s, bi_q, n, rsb, msb, si=0, eps=EPS_LN, tag="rm",
                 off_s=0, off_q=0):
        rws = rw[si]
        pbs, rbs = PS[bi_s]
        pbq, rbq = PS[bi_q]
        st("DVE", [
            lambda e: e.tensor_scalar_mul(rws["mu"][0][:, 0:n],
                                          pbs[:, off_s:off_s + n], 1.0 / D),
            lambda e: e.tensor_scalar_mul(rws["q"][0][:, 0:n],
                                          pbq[:, off_q:off_q + n], 1.0 / D),
            lambda e: e.tensor_mul(rws["t"][0][:, 0:n], rws["mu"][0][:, 0:n],
                                   rws["mu"][0][:, 0:n]),
            lambda e: e.tensor_sub(rws["q"][0][:, 0:n], rws["q"][0][:, 0:n],
                                   rws["t"][0][:, 0:n]),
        ], reads=[rbs, rbq], writes=[rws["mu"][1], rws["q"][1], rws["t"][1]],
            tag=tag + "1")
        st("ACT", [
            lambda e: e.activation(rws["q"][0][:, 0:n], rws["q"][0][:, 0:n], AF.Ln,
                                   bias=float(eps)),
            lambda e: e.activation(rsb[0][:, 0:n], rws["q"][0][:, 0:n], AF.Exp,
                                   scale=-0.5),
        ], reads=[rws["q"][1]], writes=[rws["q"][1], rsb[1]], tag=tag + "2")
        h3 = st("DVE", [
            lambda e: e.tensor_mul(msb[0][:, 0:n], rws["mu"][0][:, 0:n],
                                   rsb[0][:, 0:n]),
        ], reads=[rws["mu"][1], rsb[1]], writes=[msb[1]], tag=tag + "3")
        return h3

    def ln0_group(src_dram, w0, n, xhb, xqb, bs, bq, rsb, msb, tag, si=0,
                  off_s=0, off_q=0):
        st("POOL", lambda e, w0=w0, n=n: e.dma_start(
            out=xhb[0][:, :, 0:n],
            in_=src_dram[:, w0:w0 + n].rearrange("(c p) n -> p c n", p=128)),
            writes=[xhb[1]], dma=True, tag=tag + "lh")
        st("ACT", lambda e, n=n: e.activation(
            xqb[0][:, :, 0:n], xhb[0][:, :, 0:n], AF.Square),
            reads=[xhb[1]], writes=[xqb[1]], tag=tag + "sq")
        stat_mms(bs, bq, lambda c, n=n: xhb[0][:, c, 0:n],
                 lambda c, n=n: xqb[0][:, c, 0:n], xhb[1], xqb[1], n, tag=tag + "sm",
                 off_s=off_s, off_q=off_q)
        row_math(bs, bq, n, rsb, msb, si=si, tag=tag + "rm", off_s=off_s, off_q=off_q)

    h_const = st("DVE", [
        (lambda e, i=i, v=v: e.memset(ccell[0][:, i:i + 1], float(v)))
        for i, v in enumerate(constvals)
    ], writes=[ccell[1]], tag="consts")
    for i, v in enumerate(constvals):
        nc.const_aps.register(F32, float(v), ccell[0][:, i:i + 1])
    B.gate = [h_const]

    p0g = []

    def P0SB(name, shape, dt):
        g = nc.sbuf_tensor(name, shape, dt)
        t = g.__enter__()
        p0g.append(g)
        return t, Res()

    bkrow = P0SB("bkrow_s", [1, D], F32)
    bvrow = P0SB("bvrow_s", [1, D], F32)
    rowh = P0SB("rowh", [1, D], BF16)
    rowl = P0SB("rowl", [1, D], BF16)
    bkb = P0SB("bkb", [128, D], F32)
    bvb = P0SB("bvb", [128, D], F32)

    st("SP", [
        (lambda e, nm=nm: e.dma_start(out=vec[nm][0][:, :], in_=vd[nm]))
        for nm in vnames
    ] + [
        lambda e: e.dma_start(out=b1_sb[0][:, :], in_=b1c_d),
        lambda e: e.dma_start(out=bkrow[0][:, :], in_=bk_row_d),
        lambda e: e.dma_start(out=bvrow[0][:, :], in_=bv_row_d),
    ], writes=[vec[nm][1] for nm in vnames] + [b1_sb[1], bkrow[1], bvrow[1]],
        dma=True, tag="p0vec")
    st("POOL", [
        lambda e: e.dma_start(out=zpat_sb[0][:, :, :], in_=zpat_d),
        lambda e: e.dma_start(out=ident_sb[0][:, :], in_=ident_d),
    ], writes=[zpat_sb[1], ident_sb[1]], dma=True, tag="p0zp")
    st("DVE", [
        lambda e: e.memset(onescol[0][:, :], 1.0),
        lambda e: e.memset(zercol[0][:, :], 0.0),
        lambda e: e.memset(ones1[0][:, :], 1.0),
        lambda e: e.memset(ones128[0][:, :], 1.0),
        lambda e: e.memset(ones128f[0][:, :], 1.0),
    ], writes=[onescol[1], zercol[1], ones1[1], ones128[1], ones128f[1]],
        tag="p0ones")
    st("DVE", [
        lambda e: e.tensor_scalar_add(vec["bq1"][0][:, :], vec["bq"][0][:, :], 1.0),
        lambda e: e.tensor_scalar_mul(vec["bqn"][0][:, :], vec["bq"][0][:, :], -1.0),
    ], reads=[vec["bq"][1]], writes=[vec["bq1"][1], vec["bqn"][1]], tag="p0bq1")
    for src_row, dstt in ((bkrow, bkb), (bvrow, bvb)):
        st("DVE", [
            lambda e, src_row=src_row: e.tensor_copy(rowh[0][:, :], src_row[0][:, :]),
            lambda e, src_row=src_row: e.tensor_sub(rowl[0][:, :], src_row[0][:, :],
                                                    rowh[0][:, :]),
        ], reads=[src_row[1]], writes=[rowh[1], rowl[1]], tag="p0r")
        for half in range(2):
            bcast2(6, rowh[1], rowh[0][:, 512 * half:512 * (half + 1)],
                   rowl[0][:, 512 * half:512 * (half + 1)], 512,
                   extra=rowl[1].rdep(), tag="p0bc")
            st("DVE", lambda e, dstt=dstt, half=half: e.tensor_copy(
                dstt[0][:, 512 * half:512 * (half + 1)], PS[6][0][:, 0:512]),
                reads=[PS[6][1]], writes=[dstt[1]], tag="p0bcc")

    a1g = []

    def A1SB(name, shape, dt):
        g = nc.sbuf_tensor(name, shape, dt)
        t = g.__enter__()
        a1g.append(g)
        return t, Res()

    wk_sb = A1SB("wk_sb", [128, CH, D], BF16)
    wv_sb = A1SB("wv_sb", [128, CH, D], BF16)
    xh1 = [A1SB("a1xh%d" % i, [128, CH, 256], BF16) for i in range(2)]
    xq1 = [A1SB("a1xq%d" % i, [128, CH, 256], BF16) for i in range(2)]
    ln0h1 = [A1SB("a1ln%d" % i, [128, CH, 256], BF16) for i in range(2)]
    ln0h1b = [Res() for _ in range(2)]
    scr_a = A1SB("a1scr", [128, CH, 256], F32)
    scr_ab = Res()
    rsb1 = [A1SB("a1rsb%d" % i, [128, 256], F32) for i in range(2)]
    msb1 = [A1SB("a1msb%d" % i, [128, 256], F32) for i in range(2)]
    kf1 = A1SB("a1kf", [128, D], F32)
    km1 = A1SB("a1km", [128, D], F32)
    ke1 = A1SB("a1ke", [128, D], F32)
    Kt1 = [A1SB("a1K%d" % i, [128, D], BF16) for i in range(2)]
    Vt1 = [A1SB("a1V%d" % i, [128, H, 65], BF16) for i in range(2)]
    kvf = A1SB("kvf", [128, PAIRS, 65], F32)
    kvb = A1SB("kvb", [128, PAIRS, 2, 65], BF16)

    st("POOL", [lambda e: e.dma_start(out=wk_sb[0][:, :, :],
                 in_=wdr["wk"].rearrange("(c p) d -> p c d", p=128))],
       writes=[wk_sb[1]], dma=True, dsem="qWA", tag="a1wk")
    st("POOL", [lambda e: e.dma_start(out=wv_sb[0][:, :, :],
                 in_=wdr["wv"].rearrange("(c p) d -> p c d", p=128))],
       writes=[wv_sb[1]], dma=True, dsem="qWA", tag="a1wv")
    st("DVE", [lambda e, i=i: e.memset(Vt1[i][0][:, :, 64:65], 1.0) for i in range(2)],
       writes=[Vt1[0][1], Vt1[1][1]], tag="a1vone")

    ems = []
    for b in (4, 5):
        for cg in (0, 64):
            ems.append(lambda e, b=b, cg=cg: e.matmul(
                PS[b][0][cg:cg + 64, 0:260], zercol[0][:, :], wk_sb[0][:, 0, 0:260],
                start=True, stop=True, skip_group_check=True, tile_position=(0, cg)))
    st("PE", ems, reads=[zercol[1], wk_sb[1]], writes=[PS[4][1], PS[5][1]], tag="a1kvz")

    def a1_head(gi):
        w0, n = AG[gi]
        bi = gi % 2
        ln0_group(xT, w0, n, xh1[bi], xq1[bi], 6, 6, rsb1[bi], msb1[bi],
                  tag="a1s%d" % gi, si=0, off_s=0, off_q=256)
        for eng, c0, c1, wrs in (("DVE", 0, 5, (scr_a[1], ln0h1[bi][1])),
                                 ("POOL", 5, CH, (scr_ab, ln0h1b[bi]))):
            ems = []
            for c in range(c0, c1):
                ems.append(lambda e, bi=bi, c=c, n=n: e.tensor_mul(
                    scr_a[0][:, c, 0:n], xh1[bi][0][:, c, 0:n], rsb1[bi][0][:, 0:n]))
                ems.append(lambda e, bi=bi, c=c, n=n: e.tensor_sub(
                    ln0h1[bi][0][:, c, 0:n], scr_a[0][:, c, 0:n], msb1[bi][0][:, 0:n]))
            st(eng, ems, reads=[xh1[bi][1], rsb1[bi][1], msb1[bi][1]],
               writes=list(wrs), tag="a1ln%s%d" % (eng, gi))

    def a1_main(gi):
        w0, n = AG[gi]
        bi = gi % 2
        for tt in range(2):
            t0 = tt * 128
            ti = tt % 2
            ems = []
            for c in range(CH):
                for dh in range(2):
                    ems.append(lambda e, c=c, dh=dh, bi=bi, t0=t0: e.matmul(
                        PS[dh][0][:, 0:512], ln0h1[bi][0][:, c, t0:t0 + 128],
                        wk_sb[0][:, c, dh * 512:(dh + 1) * 512],
                        start=(c == 0), stop=(c == CH - 1), skip_group_check=True))
            st("PE", ems, reads=[ln0h1[bi][1], ln0h1b[bi], wk_sb[1], wv_sb[1]],
               writes=[PS[0][1], PS[1][1]], tag="a1km%d_%d" % (gi, tt))
            ems = []
            for c in range(CH):
                for dh in range(2):
                    ems.append(lambda e, c=c, dh=dh, bi=bi, t0=t0: e.matmul(
                        PS[2 + dh][0][:, 0:512], ln0h1[bi][0][:, c, t0:t0 + 128],
                        wv_sb[0][:, c, dh * 512:(dh + 1) * 512],
                        start=(c == 0), stop=(c == CH - 1), skip_group_check=True))
            st("PE", ems, reads=[ln0h1[bi][1], ln0h1b[bi], wv_sb[1], wk_sb[1]],
               writes=[PS[2][1], PS[3][1]], tag="a1vm%d_%d" % (gi, tt))
            st("DVE", [
                lambda e, dh=dh: e.tensor_add(kf1[0][:, dh * 512:(dh + 1) * 512],
                                              PS[dh][0][:, 0:512],
                                              bkb[0][:, dh * 512:(dh + 1) * 512])
                for dh in range(2)
            ], reads=[PS[0][1], PS[1][1], bkb[1]], writes=[kf1[1]], tag="a1kf%d_%d" % (gi, tt))
            st("DVE", [lambda e: e.tensor_scalar_min(km1[0][:, :], kf1[0][:, :], 0.0)],
               reads=[kf1[1]], writes=[km1[1]], tag="a1km2%d_%d" % (gi, tt))
            st("ACT", [lambda e: e.activation(ke1[0][:, :], km1[0][:, :], AF.Exp)],
               reads=[km1[1]], writes=[ke1[1]], tag="a1ke%d_%d" % (gi, tt))
            st("DVE", [lambda e, ti=ti: e.scalar_tensor_tensor(
                Kt1[ti][0][:, :], kf1[0][:, :], 1.0, ke1[0][:, :], ALU.add, ALU.max)],
                reads=[kf1[1], ke1[1]], writes=[Kt1[ti][1]], tag="a1K%d_%d" % (gi, tt))
            if taps == 2 and gi == 0 and tt == 0:
                st("POOL", [
                    lambda e: e.dma_start(out=tap_ap["K"], in_=Kt1[0][0][:, :]),
                    lambda e: e.dma_start(out=tap_ap["kf"], in_=kf1[0][:, :]),
                ], reads=[Kt1[0][1], kf1[1]], dma=True, tag="tapa1K")
            st("DVE", [
                lambda e, dh=dh, ti=ti: e.tensor_add(
                    Vt1[ti][0][:, dh * 8:(dh + 1) * 8, 0:64],
                    PS[2 + dh][0][:, 0:512].rearrange("p (h m) -> p h m", h=8),
                    bvb[0][:, dh * 512:(dh + 1) * 512].rearrange("p (h m) -> p h m", h=8))
                for dh in range(2)
            ], reads=[PS[2][1], PS[3][1], bvb[1]], writes=[Vt1[ti][1]],
                tag="a1V%d_%d" % (gi, tt))
            ems = []
            for p in range(PAIRS):
                bidx = 4 + p // 4
                off = (p % 4) * 65
                last_kv = (gi == len(AG) - 1 and tt == 1)
                for hh in range(2):
                    hd = 2 * p + hh
                    ems.append(lambda e, bidx=bidx, off=off, hd=hd, hh=hh, ti=ti,
                               last_kv=last_kv:
                               e.matmul(PS[bidx][0][64 * hh:64 * (hh + 1), off:off + 65],
                                        Kt1[ti][0][:, 64 * hd:64 * (hd + 1)],
                                        Vt1[ti][0][:, hd, :],
                                        start=False, stop=last_kv, skip_group_check=True,
                                        tile_position=(0, 64 * hh)))
            st("PE", ems, reads=[Kt1[ti][1], Vt1[ti][1]], writes=[PS[4][1], PS[5][1]],
               tag="a1kv%d_%d" % (gi, tt))

    a1_head(0)
    a1_head(1)
    for j in range(4):
        st("POOL", [lambda e, j=j: e.dma_start(
            out=wq_sb[0][:, :, j * 256:(j + 1) * 256],
            in_=wdr["wq"][:, j * 256:(j + 1) * 256].rearrange(
                "(c p) d -> p c d", p=128))],
            writes=[wqr[j]], dma=True, dsem="qWB", tag="pfwq%d" % j)
    for j in range(4):
        st("POOL", [lambda e, j=j: e.dma_start(
            out=wo_sb[0][:, :, j * 256:(j + 1) * 256],
            in_=wdr["wo"][:, j * 256:(j + 1) * 256].rearrange(
                "(c p) d -> p c d", p=128))],
            writes=[wor[j]], dma=True, dsem="qWB", tag="pfwo%d" % j)
    for gi in range(len(AG)):
        a1_main(gi)
        if gi + 2 < len(AG):
            a1_head(gi + 2)
        if taps == 2 and gi == 0:
            st("POOL", [
                lambda e: e.dma_start(out=tap_ap["rsb"][0], in_=rsb1[0][:, :]),
                lambda e: e.dma_start(out=tap_ap["ln0h"], in_=ln0h1[0][0][:, :, :]),
            ], reads=[rsb1[1], ln0h1[0][1]], dma=True, tag="tapa1")

    st("DVE", [
        lambda e: e.tensor_copy(kvf[0][:, 0:4, :].rearrange("p a b -> p (a b)"),
                                PS[4][0][:, 0:260]),
        lambda e: e.tensor_copy(kvf[0][:, 4:8, :].rearrange("p a b -> p (a b)"),
                                PS[5][0][:, 0:260]),
    ], reads=[PS[4][1], PS[5][1]], writes=[kvf[1]], tag="a2ev")
    st("DVE", [
        lambda e: e.tensor_copy(kvb[0][:, :, 0, :], kvf[0][:, :, :]),
        lambda e: e.tensor_sub(kvb[0][:, :, 1, :], kvf[0][:, :, :], kvb[0][:, :, 0, :]),
    ], reads=[kvf[1]], writes=[kvb[1]], tag="a2hl")
    st("DVE", [
        lambda e: e.memset(kvbz[0][:, :, :, :, :].rearrange('p a b c d -> p (a b c d)'), 0.0),
        lambda e: e.tensor_copy(kvbz[0][0:64, :, :, 0, :], kvb[0][0:64, :, :, 0:64]),
        lambda e: e.tensor_copy(kvbz[0][64:128, :, :, 1, :], kvb[0][64:128, :, :, 0:64]),
    ], reads=[kvb[1]], writes=[kvbz[1]], tag="a2kvz")
    ems = [lambda e: e.memset(zrhs[0][:, :, :], 0.0)]
    for p in range(PAIRS):
        for hl in range(2):
            ems.append(lambda e, p=p, hl=hl: e.tensor_copy(
                zrhs[0][0:64, p, hl:hl + 1], kvb[0][0:64, p, hl, 64:65]))
            ems.append(lambda e, p=p, hl=hl: e.tensor_copy(
                zrhs[0][64:128, p, 2 + hl:3 + hl], kvb[0][64:128, p, hl, 64:65]))
    st("DVE", ems, reads=[kvb[1]], writes=[zrhs[1]], tag="a2zr")
    if taps >= 1:
        st("POOL", [lambda e: e.dma_start(
            out=tap_ap["kvbz"],
            in_=kvbz[0][:, :, :, :, :].rearrange("p a b c d -> p (a b c d)"))],
            reads=[kvbz[1]], dma=True, tag="tapkvz")
    if taps >= 1:
        st("SP", [lambda e: e.dma_start(
            out=tap_ap["kvb"][:, :], in_=kvf[0][:, :, :].rearrange("p a b -> p (a b)"))],
            reads=[kvf[1]], dma=True, tag="tapkv")

    def finish():
        st("SP", [lambda e: e.nop()],
           extra=[(s, B.cnt.get(s, 0)) for s in
                  ("qSP", "qACT", "qGP", "qWA", "qWB", "qW1", "qW2", "qWr",
                   "qWk2", "qWv2", "sPE", "sACT", "sDVE", "sPOOL")], tag="finX")

    def barrier(name):
        hs = [st(eng, lambda e: e.nop(),
                 extra=[(s, B.cnt.get(s, 0)) for s in
                        ("sDVE", "sPE", "sACT", "sPOOL", "sSP", "qSP", "qACT",
                         "qGP", "qWA", "qWB", "qW1", "qW2", "qWr",
                         "qWk2", "qWv2")],
                 tag=name + eng) for eng in ENGS]
        B.gate = hs
        return hs

    if upto == "A":
        finish()
        return _replay(nc, B)
    barh = barrier("barC1")
    for g in reversed(p0g + a1g):
        g.__exit__(None, None, None)

    c1g = []

    def C1SB(name, shape, dt):
        g = nc.sbuf_tensor(name, shape, dt)
        t = g.__enter__()
        c1g.append(g)
        return t, Res()

    bsc = [C1SB("c1bs%d" % i, [128, CH, 416], BF16) for i in range(2)]
    sqc = [C1SB("c1sq%d" % i, [128, CH, 416], BF16) for i in range(2)]
    ln0f = [C1SB("c1l0f%d" % i, [128, CH, 416], F32) for i in range(2)]
    ln0fb = [Res() for _ in range(2)]
    ln0h = [C1SB("c1l0h%d" % i, [128, CH, 416], BF16) for i in range(2)]
    rsbc = [C1SB("c1rsb%d" % i, [128, 416], F32) for i in range(2)]
    msbc = [C1SB("c1msb%d" % i, [128, 416], F32) for i in range(2)]
    eqm = C1SB("c1eqm", [128, 416], F32)
    eqc = eqm
    Qt = [C1SB("c1Q%d" % i, [128, PAIRS, 416], BF16) for i in range(2)]
    ztc = C1SB("c1zt", [128, 4, 16], F32)
    zthl = C1SB("c1zthl", [128, 4, 2, 16], BF16)
    zfc = [C1SB("c1zf%d" % i, [32, 416], BF16) for i in range(2)]
    zbc8 = C1SB("c1zb8", [128, PAIRS, 416], F32)
    otc = C1SB("c1ot", [128, CH, 416], BF16)
    of_ = [C1SB("c1of%d" % i, [128, 416], F32) for i in range(2)]
    uf = [C1SB("c1uf%d" % i, [128, 416], F32) for i in range(2)]
    x1c = C1SB("c1x1", [128, CH, 416], F32)
    x1cb = Res()

    rsbc1 = [C1SB("c1rsb1%d" % i, [128, 416], F32) for i in range(2)]
    msbc1 = [C1SB("c1msb1%d" % i, [128, 416], F32) for i in range(2)]

    def c1_head(gi):
        w0, n = CG[gi]
        bi = gi % 2
        ln0_group(xTo, w0, n, bsc[bi], sqc[bi], 4, 5, rsbc[bi], msbc[bi],
                  tag="c1s%d" % gi, si=0)
        for eng, c0, c1, wrs in (("DVE", 0, 5, (ln0f[bi][1],)),
                                 ("POOL", 5, CH, (ln0fb[bi],))):
            ems = []
            for c in range(c0, c1):
                ems.append(lambda e, bi=bi, c=c, n=n: e.tensor_mul(
                    ln0f[bi][0][:, c, 0:n], bsc[bi][0][:, c, 0:n], rsbc[bi][0][:, 0:n]))
                ems.append(lambda e, bi=bi, c=c, n=n: e.tensor_sub(
                    ln0f[bi][0][:, c, 0:n], ln0f[bi][0][:, c, 0:n], msbc[bi][0][:, 0:n]))
            st(eng, ems, reads=[bsc[bi][1], rsbc[bi][1], msbc[bi][1]],
               writes=list(wrs), tag="c1l0f%s%d" % (eng, gi))
        st("ACT", [lambda e, bi=bi, n=n: e.activation(
            ln0h[bi][0][:, :, 0:n], ln0f[bi][0][:, :, 0:n], AF.Copy)],
            reads=[ln0f[bi][1], ln0fb[bi]], writes=[ln0h[bi][1]], tag="c1l0h%d" % gi)

    def c1_main(gi):
        w0, n = CG[gi]
        bi = gi % 2
        nch = (n + 127) // 128
        for dt in range(PAIRS):
            bidx = dt % 2
            ems = [lambda e, c=c, dt=dt, bidx=bidx, n=n: e.matmul(
                PS[bidx][0][:, 0:n], wq_sb[0][:, c, dt * 128:(dt + 1) * 128],
                ln0h[bi][0][:, c, 0:n], start=(c == 0), stop=(c == CH - 1),
                skip_group_check=True) for c in range(CH)]
            st("PE", ems, reads=[wqr[3], wor[3], ln0h[bi][1]], writes=[PS[bidx][1]],
               tag="c1q%d_%d" % (gi, dt))
            st("DVE", [lambda e, bidx=bidx, dt=dt, n=n: e.tensor_scalar(
                eqm[0][:, 0:n], PS[bidx][0][:, 0:n], vec["bq"][0][:, dt:dt + 1], 0.0,
                ALU.add, ALU.min)],
                reads=[PS[bidx][1], vec["bq"][1]], writes=[eqm[1]],
                tag="c1qm%d_%d" % (gi, dt))
            st("ACT", [lambda e, n=n: e.activation(
                eqc[0][:, 0:n], eqm[0][:, 0:n], AF.Exp)],
                reads=[eqm[1]], writes=[eqc[1]],
                tag="c1eq%d_%d" % (gi, dt))
            st("DVE", [lambda e, bidx=bidx, dt=dt, bi=bi, n=n: e.scalar_tensor_tensor(
                Qt[bi][0][:, dt, 0:n], PS[bidx][0][:, 0:n], vec["bq1"][0][:, dt:dt + 1],
                eqc[0][:, 0:n], ALU.add, ALU.max)],
                reads=[PS[bidx][1], eqc[1], vec["bq1"][1]], writes=[Qt[bi][1]],
                tag="c1Q%d_%d" % (gi, dt))
        ems = []
        for ch in range(nch):
            t0 = ch * 128
            m = min(128, n - t0)
            for p in range(PAIRS):
                for hl in range(2):
                    ems.append(lambda e, ch=ch, t0=t0, m=m, p=p, hl=hl, bi=bi: e.matmul(
                        PS[6][0][0:m, ch * 16 + 2 * p:ch * 16 + 2 * p + 2],
                        Qt[bi][0][:, p, t0:t0 + m], zrhs[0][:, p, hl::2],
                        start=(hl == 0), stop=(hl == 1), skip_group_check=True))
        st("PE", ems, reads=[Qt[bi][1], zrhs[1]], writes=[PS[6][1]], tag="c1zd%d" % gi)
        st("ACT", [
            lambda e, nch=nch: e.activation(
                ztc[0][:, 0:nch, :].rearrange("p a b -> p (a b)"),
                PS[6][0][:, 0:nch * 16], AF.Ln, bias=float(EPS_Z)),
            lambda e, nch=nch: e.activation(
                ztc[0][:, 0:nch, :].rearrange("p a b -> p (a b)"),
                ztc[0][:, 0:nch, :].rearrange("p a b -> p (a b)"), AF.Exp, scale=-1.0),
        ], reads=[PS[6][1]], writes=[ztc[1]], tag="c1zl%d" % gi)
        st("DVE", [
            lambda e, nch=nch: e.tensor_copy(zthl[0][:, 0:nch, 0, :], ztc[0][:, 0:nch, :]),
            lambda e, nch=nch: e.tensor_sub(zthl[0][:, 0:nch, 1, :], ztc[0][:, 0:nch, :],
                                            zthl[0][:, 0:nch, 0, :]),
        ], reads=[ztc[1]], writes=[zthl[1]], tag="c1zh%d" % gi)
        for ch in range(nch):
            t0 = ch * 128
            m = min(128, n - t0)
            st("PE", [lambda e, ch=ch: e.transpose(
                PSB[0][0:32, 0:128],
                zthl[0][:, ch, :, :].rearrange("p a b -> p (a b)"),
                ident_sb[0][:, :])],
                reads=[zthl[1], ident_sb[1]], writes=[PSB[1]], tag="c1zt%d_%d" % (gi, ch))
            st("ACT", [lambda e, t0=t0, m=m, bi=bi: e.activation(
                zfc[bi][0][:, t0:t0 + m], PSB[0][0:32, 0:m], AF.Copy)],
                reads=[PSB[1]], writes=[zfc[bi][1]], tag="c1ze%d_%d" % (gi, ch))
        for p in range(PAIRS):
            zi = p % 2
            st("PE", [lambda e, p=p, zi=zi, n=n, bi=bi: e.matmul(
                PS[zi][0][:, 0:n], zpat_sb[0][:, p, :], zfc[bi][0][:, 0:n],
                start=True, stop=True, skip_group_check=True)],
                reads=[zpat_sb[1], zfc[bi][1]], writes=[PS[zi][1]], tag="c1zb%d_%d" % (gi, p))
            st("ACT", [lambda e, p=p, zi=zi, n=n: e.activation(
                zbc8[0][:, p, 0:n], PS[zi][0][:, 0:n], AF.Copy)],
                reads=[PS[zi][1]], writes=[zbc8[1]], tag="c1zc%d_%d" % (gi, p))
        for p in range(PAIRS):
            oi = 2 + p % 2
            ems = []
            for hh in range(2):
                for hl in range(2):
                    ems.append(lambda e, oi=oi, hh=hh, hl=hl, p=p, n=n, bi=bi: e.matmul(
                        PS[oi][0][64 * hh:64 * (hh + 1), 0:n],
                        kvbz[0][:, p, hl, hh, :],
                        Qt[bi][0][:, p, 0:n],
                        start=(hl == 0), stop=(hl == 1),
                        skip_group_check=True, tile_position=(0, 64 * hh)))
            st("PE", ems, reads=[kvbz[1], Qt[bi][1]], writes=[PS[oi][1]],
               tag="c1o%d_%d" % (gi, p))
            st("ACT", [lambda e, oi=oi, p=p, n=n: e.activation(
                of_[p % 2][0][:, 0:n], PS[oi][0][:, 0:n], AF.Copy)],
                reads=[PS[oi][1]], writes=[of_[p % 2][1]], tag="c1of%d_%d" % (gi, p))
            st("DVE", [lambda e, p=p, n=n: e.tensor_mul(
                otc[0][:, p, 0:n], of_[p % 2][0][:, 0:n], zbc8[0][:, p, 0:n])],
                reads=[of_[p % 2][1], zbc8[1]], writes=[otc[1]], tag="c1oe%d_%d" % (gi, p))
        for dt in range(PAIRS):
            bidx = dt % 2
            ems = [lambda e, c=c, dt=dt, bidx=bidx, n=n: e.matmul(
                PS[bidx][0][:, 0:n], wo_sb[0][:, c, dt * 128:(dt + 1) * 128],
                otc[0][:, c, 0:n], start=(c == 0), stop=(c == CH - 1),
                skip_group_check=True) for c in range(CH)]
            st("PE", ems, reads=[wqr[3], wor[3], otc[1]], writes=[PS[bidx][1]],
               tag="c1wo%d_%d" % (gi, dt))
            st("ACT", [lambda e, bidx=bidx, dt=dt, n=n: e.activation(
                uf[dt % 2][0][:, 0:n], PS[bidx][0][:, 0:n], AF.Identity,
                bias=vec["bo"][0][:, dt:dt + 1])],
                reads=[PS[bidx][1], vec["bo"][1]], writes=[uf[dt % 2][1]],
                tag="c1uf%d_%d" % (gi, dt))
            st("DVE", [lambda e, dt=dt, bi=bi, n=n: e.tensor_add(
                ln0f[bi][0][:, dt, 0:n], ln0f[bi][0][:, dt, 0:n],
                uf[dt % 2][0][:, 0:n])],
                reads=[uf[dt % 2][1], ln0fb[bi]], writes=[ln0f[bi][1]],
                tag="c1u%d_%d" % (gi, dt))
        st("ACT", [lambda e, bi=bi, n=n: e.activation(
            bsc[bi][0][:, :, 0:n], ln0f[bi][0][:, :, 0:n], AF.Copy)],
            reads=[ln0f[bi][1], ln0fb[bi]], writes=[bsc[bi][1]], tag="c1ub%d" % gi)
        st("ACT", [lambda e, bi=bi, n=n: e.activation(
            sqc[bi][0][:, :, 0:n], bsc[bi][0][:, :, 0:n], AF.Square)],
            reads=[bsc[bi][1]], writes=[sqc[bi][1]], tag="c1us%d" % gi)
        stat_mms(2, 3, lambda c, bi=bi, n=n: bsc[bi][0][:, c, 0:n],
                 lambda c, bi=bi, n=n: sqc[bi][0][:, c, 0:n],
                 bsc[bi][1], sqc[bi][1], n, tag="c1sm%d" % gi)

    def c1_tail(gi):
        w0, n = CG[gi]
        bi = gi % 2
        row_math(2, 3, n, rsbc1[bi], msbc1[bi], si=1, tag="c1rm%d" % gi)
        for eng, c0, c1, wrs in (("DVE", 0, 5, (x1c[1],)), ("POOL", 5, CH, (x1cb,))):
            ems = []
            for c in range(c0, c1):
                ems.append(lambda e, bi=bi, c=c, n=n: e.tensor_mul(
                    x1c[0][:, c, 0:n], ln0f[bi][0][:, c, 0:n], rsbc1[bi][0][:, 0:n]))
                ems.append(lambda e, bi=bi, c=c, n=n: e.tensor_sub(
                    x1c[0][:, c, 0:n], x1c[0][:, c, 0:n], msbc1[bi][0][:, 0:n]))
            st(eng, ems, reads=[ln0f[bi][1], ln0fb[bi], rsbc1[bi][1], msbc1[bi][1]],
               writes=list(wrs), tag="c1xa%s%d" % (eng, gi))
        st("ACT", [lambda e, c=c, n=n: e.activation(
            x1c[0][:, c, 0:n], x1c[0][:, c, 0:n], AF.Identity,
            bias=vec["be1"][0][:, c:c + 1], scale=vec["g1"][0][:, c:c + 1])
            for c in range(CH)],
            reads=[vec["be1"][1], vec["g1"][1], x1c[1], x1cb],
            writes=[x1c[1], x1cb], tag="c1xg%d" % gi)
        st("ACT", [lambda e, w0=w0, n=n: e.dma_start(
            out=x1f_d[:, w0:w0 + n].rearrange("(c p) n -> p c n", p=128),
            in_=x1c[0][:, :, 0:n])], reads=[x1c[1]], dma=True, tag="c1d1%d" % gi)
        st("POOL", [lambda e, w0=w0, n=n: e.dma_start(
            out=x1h_d[:, w0:w0 + n].rearrange("(c p) n -> p c n", p=128),
            in_=x1c[0][:, :, 0:n])], reads=[x1c[1]], dma=True, tag="c1d2%d" % gi)
        if taps >= 1:
            st("SP", [lambda e, w0=w0, n=n: e.dma_start(
                out=tap_ap["x1"][:, w0:w0 + n].rearrange("(c p) n -> p c n", p=128),
                in_=x1c[0][:, :, 0:n])], reads=[x1c[1]], dma=True, tag="tapx1%d" % gi)

    c1_head(0)
    for gi in range(len(CG)):
        c1_main(gi)
        if gi + 1 < len(CG):
            c1_head(gi + 1)
        c1_tail(gi)

    if upto == "C1":
        finish()
        return _replay(nc, B)
    barh2 = barrier("barC2")
    for g in reversed(c1g):
        g.__exit__(None, None, None)
    for g in reversed(kvg):
        g.__exit__(None, None, None)
    for g in reversed(wqg):
        g.__exit__(None, None, None)

    c2ag = []

    def C2ASB(name, shape, dt):
        g = nc.sbuf_tensor(name, shape, dt)
        t = g.__enter__()
        c2ag.append(g)
        return t, Res()

    c2wg = []

    def C2WSB(name, shape, dt):
        g = nc.sbuf_tensor(name, shape, dt)
        t = g.__enter__()
        c2wg.append(g)
        return t, Res()

    w2_sb = C2WSB("w2_sb", [128, FFT, D], BF16)
    w2q = [Res() for _ in range(4)]

    w1_sb = C2ASB("w1_sb", [128, CH, FF], BF16)
    w1q = [Res() for _ in range(8)]
    x1hc = [C2ASB("c2x1h%d" % i, [128, CH, 257], BF16) for i in range(2)]
    h1c = [C2ASB("c2h1%d" % i, [128, FFT, 257], BF16) for i in range(2)]

    for q in range(8):
        st("POOL", [lambda e, q=q: e.dma_start(
            out=w1_sb[0][:, :, q * 512:(q + 1) * 512],
            in_=wdr["w1"][:, q * 512:(q + 1) * 512].rearrange("(c p) f -> p c f", p=128))],
            writes=[w1q[q]], dma=True, dsem="qW1", tag="c2aw%d" % q)
    for q in range(4):
        st("POOL", [lambda e, q=q: e.dma_start(
            out=w2_sb[0][:, :, q * 256:(q + 1) * 256],
            in_=wdr["w2"][:, q * 256:(q + 1) * 256].rearrange("(f p) d -> p f d", p=128))],
            writes=[w2q[q]], dma=True, dsem="qW2", tag="c2bw%d" % q)

    for gi, (w0, n) in enumerate(C2G):
        bi = gi % 2
        st("SP", [lambda e, bi=bi, w0=w0, n=n: e.dma_start(
            out=x1hc[bi][0][:, :, 0:n],
            in_=x1h_d[:, w0:w0 + n].rearrange("(c p) n -> p c n", p=128))],
            writes=[x1hc[bi][1]], dma=True, tag="c2alh%d" % gi)
        for f in range(FFT):
            bidx = f % 4
            ems = [lambda e, c=c, f=f, bidx=bidx, bi=bi, n=n: e.matmul(
                PS[bidx][0][:, 0:n], w1_sb[0][:, c, f * 128:(f + 1) * 128],
                x1hc[bi][0][:, c, 0:n], start=(c == 0), stop=(c == CH - 1),
                skip_group_check=True) for c in range(CH)]
            st("PE", ems, reads=[w1q[7], x1hc[bi][1]], writes=[PS[bidx][1]],
               tag="c2w1_%d_%d" % (gi, f))
            st("ACT", [lambda e, f=f, bidx=bidx, bi=bi, n=n: e.activation(
                h1c[bi][0][:, f, 0:n], PS[bidx][0][:, 0:n], AF.Gelu,
                bias=b1_sb[0][:, f:f + 1])],
                reads=[PS[bidx][1], b1_sb[1]], writes=[h1c[bi][1]], tag="c2g%d_%d" % (gi, f))
        st("ACT", [lambda e, bi=bi, w0=w0, n=n: e.dma_start(
            out=h1_d[:, w0:w0 + n].rearrange("(f p) n -> p f n", p=128),
            in_=h1c[bi][0][:, :, 0:n])], reads=[h1c[bi][1]], dma=True, tag="c2ast%d" % gi)

    barh2b = barrier("barC2b")
    for g in reversed(c2ag):
        g.__exit__(None, None, None)

    c2g = []

    def C2SB(name, shape, dt):
        g = nc.sbuf_tensor(name, shape, dt)
        t = g.__enter__()
        c2g.append(g)
        return t, Res()

    h1b = [C2SB("c2bh%d" % i, [128, FFT, 257], BF16) for i in range(2)]
    x1fc = [C2SB("c2x1f%d" % i, [128, CH, 257], F32) for i in range(2)]
    vc = C2SB("c2v", [128, CH, 257], F32)
    tc = C2SB("c2t", [128, CH, 257], F32)
    ac = tc
    vf = [C2SB("c2vf%d" % i, [128, 257], F32) for i in range(2)]
    vbc = C2SB("c2vb", [128, CH, 257], BF16)
    vsqc = C2SB("c2vsq", [128, CH, 257], BF16)
    rsb2 = C2SB("c2rsb", [128, 257], F32)
    msb2 = C2SB("c2msb", [128, 257], F32)

    def c2b_head(gi):
        w0, n = C2G[gi]
        bi = gi % 2
        st("SP", [
            lambda e, bi=bi, w0=w0, n=n: e.dma_start(
                out=h1b[bi][0][:, :, 0:n],
                in_=h1_d[:, w0:w0 + n].rearrange("(f p) n -> p f n", p=128)),
            lambda e, bi=bi, w0=w0, n=n: e.dma_start(
                out=x1fc[bi][0][:, :, 0:n],
                in_=x1f_d[:, w0:w0 + n].rearrange("(c p) n -> p c n", p=128)),
        ], writes=[h1b[bi][1], x1fc[bi][1]], dma=True, tag="c2bl%d" % gi)

    def c2b_main(gi):
        w0, n = C2G[gi]
        bi = gi % 2
        for dt in range(CH):
            bidx = 4 + dt % 2
            ems = [lambda e, f=f, dt=dt, bidx=bidx, bi=bi, n=n: e.matmul(
                PS[bidx][0][:, 0:n], w2_sb[0][:, f, dt * 128:(dt + 1) * 128],
                h1b[bi][0][:, f, 0:n], start=(f == 0), stop=(f == FFT - 1),
                skip_group_check=True) for f in range(FFT)]
            st("PE", ems, reads=[w2q[3], h1b[bi][1]], writes=[PS[bidx][1]],
               tag="c2w2_%d_%d" % (gi, dt))
            st("ACT", [lambda e, bidx=bidx, dt=dt, n=n: e.activation(
                vf[dt % 2][0][:, 0:n], PS[bidx][0][:, 0:n], AF.Identity,
                bias=vec["b2"][0][:, dt:dt + 1])],
                reads=[PS[bidx][1], vec["b2"][1]], writes=[vf[dt % 2][1]],
                tag="c2ve%d_%d" % (gi, dt))
            st("DVE", [lambda e, dt=dt, bi=bi, n=n: e.tensor_add(
                vc[0][:, dt, 0:n], vf[dt % 2][0][:, 0:n], x1fc[bi][0][:, dt, 0:n])],
                reads=[vf[dt % 2][1], x1fc[bi][1]], writes=[vc[1]],
                tag="c2v%d_%d" % (gi, dt))
        st("ACT", [lambda e, n=n: e.activation(vbc[0][:, :, 0:n], vc[0][:, :, 0:n],
                                               AF.Copy)],
           reads=[vc[1]], writes=[vbc[1]], tag="c2vb%d" % gi)
        st("ACT", [lambda e, n=n: e.activation(vsqc[0][:, :, 0:n], vbc[0][:, :, 0:n],
                                               AF.Square)],
           reads=[vbc[1]], writes=[vsqc[1]], tag="c2vs%d" % gi)
        stat_mms(2, 3, lambda c, n=n: vbc[0][:, c, 0:n], lambda c, n=n: vsqc[0][:, c, 0:n],
                 vbc[1], vsqc[1], n, tag="c2sm%d" % gi)

    def c2b_tail(gi):
        w0, n = C2G[gi]
        row_math(2, 3, n, rsb2, msb2, si=1, tag="c2rm%d" % gi)
        ems = []
        for c in range(CH):
            ems.append(lambda e, c=c, n=n: e.tensor_mul(
                tc[0][:, c, 0:n], vc[0][:, c, 0:n], rsb2[0][:, 0:n]))
            ems.append(lambda e, c=c, n=n: e.tensor_sub(
                tc[0][:, c, 0:n], tc[0][:, c, 0:n], msb2[0][:, 0:n]))
        st("DVE", ems, reads=[vc[1], rsb2[1], msb2[1]], writes=[tc[1]], tag="c2t%d" % gi)
        st("ACT", [lambda e, c=c, n=n: e.activation(
            ac[0][:, c, 0:n], tc[0][:, c, 0:n], AF.Identity,
            bias=vec["be2"][0][:, c:c + 1], scale=vec["g2"][0][:, c:c + 1])
            for c in range(CH)],
            reads=[tc[1], vec["be2"][1], vec["g2"][1]], writes=[ac[1]], tag="c2a%d" % gi)
        st("ACT", [lambda e, w0=w0, n=n: e.dma_start(
            out=a_d[:, w0:w0 + n].rearrange("(c p) n -> p c n", p=128),
            in_=ac[0][:, :, 0:n])], reads=[ac[1]], dma=True, tag="c2d%d" % gi)
        if taps >= 1:
            st("SP", [lambda e, w0=w0, n=n: e.dma_start(
                out=tap_ap["a"][:, w0:w0 + n].rearrange("(c p) n -> p c n", p=128),
                in_=ac[0][:, :, 0:n])], reads=[ac[1]], dma=True, tag="tapa%d" % gi)

    c2b_head(0)
    for gi in range(len(C2G)):
        c2b_main(gi)
        if gi + 1 < len(C2G):
            c2b_head(gi + 1)
        c2b_tail(gi)

    barh3 = barrier("barD")
    for g in reversed(c2g):
        g.__exit__(None, None, None)
    for g in reversed(c2wg):
        g.__exit__(None, None, None)

    dg = []

    def DSB(name, shape, dt):
        g = nc.sbuf_tensor(name, shape, dt)
        t = g.__enter__()
        dg.append(g)
        return t, Res()

    DG2 = [(1 + 256 * i, 256) for i in range(8)]
    NN = 257
    wr_sb = DSB("wr_sb", [128, CH, D], BF16)
    wk2_sb = DSB("wk2_sb", [128, CH, D], BF16)
    wv2_sb = DSB("wv2_sb", [128, CH, D], BF16)
    wrr = [Res() for _ in range(4)]
    wkr = [Res() for _ in range(4)]
    wvr = [Res() for _ in range(4)]
    agb = [DSB("d_a%d" % i, [128, CH, NN], F32) for i in range(2)]
    abf = [DSB("d_ab%d" % i, [128, CH, NN], BF16) for i in range(2)]
    xgb = DSB("d_xg", [128, CH, NN], F32)
    xmid = [DSB("d_xm%d" % i, [128, CH, NN], F32) for i in range(2)]
    xmidb = [Res() for _ in range(2)]
    ln2 = [DSB("d_ln2%d" % i, [128, CH, NN], F32) for i in range(2)]
    ln2b = [Res() for _ in range(2)]
    asq = DSB("d_asq", [128, CH, NN], BF16)
    bb1 = DSB("d_bb1", [128, CH, NN], BF16)
    bb2 = DSB("d_bb2", [128, CH, NN], BF16)
    xrb = [DSB("d_xr%d" % i, [128, CH, 256], BF16) for i in range(2)]
    xkb = [DSB("d_xk%d" % i, [128, CH, 256], BF16) for i in range(2)]
    scr2 = DSB("d_s2", [128, CH, 256], F32)
    rb_ = DSB("d_rb", [128, CH, 256], F32)
    ff_ = [DSB("d_ff%d" % i, [128, 256], F32) for i in range(2)]
    dif = abf
    kab = xrb
    rsbd = [DSB("d_rsb%d" % i, [128, NN], F32) for i in range(2)]
    msbd = [DSB("d_msb%d" % i, [128, NN], F32) for i in range(2)]
    rsbdF = DSB("d_rsbF", [128, 256], F32)
    msbdF = DSB("d_msbF", [128, 256], F32)
    bbf1 = xkb
    bbf2 = xrb

    for nm, tsb, rr, ds in (("wr", wr_sb, wrr, "qWr"), ("wk2", wk2_sb, wkr, "qWk2"),
                            ("wv2", wv2_sb, wvr, "qWv2")):
        for j in range(4):
            st("POOL", [lambda e, nm=nm, tsb=tsb, j=j: e.dma_start(
                out=tsb[0][:, :, j * 256:(j + 1) * 256],
                in_=wdr[nm][:, j * 256:(j + 1) * 256].rearrange(
                    "(c p) d -> p c d", p=128))],
                writes=[rr[j]], dma=True, dsem=ds, tag="dw%s%d" % (nm, j))

    def d_head(gi):
        w0, n = DG2[gi]
        cw0 = w0 - 1
        nn = n + 1
        bi = gi % 2
        st("SP", [
            lambda e, bi=bi, cw0=cw0, nn=nn: e.dma_start(
                out=agb[bi][0][:, :, 0:nn],
                in_=a_d[:, cw0:cw0 + nn].rearrange("(c p) n -> p c n", p=128)),
            lambda e, bi=bi, cw0=cw0, nn=nn: e.dma_start(
                out=xgb[0][:, :, 0:nn],
                in_=xTo[:, cw0:cw0 + nn].rearrange("(c p) n -> p c n", p=128)),
        ], writes=[agb[bi][1], xgb[1]], dma=True, tag="dl%d" % gi)
        st("POOL", [lambda e, bi=bi, cw0=cw0, nn=nn: e.dma_start(
            out=abf[bi][0][:, :, 0:nn],
            in_=a_d[:, cw0:cw0 + nn].rearrange("(c p) n -> p c n", p=128))],
            writes=[abf[bi][1]], dma=True, tag="dlb%d" % gi)
        st("ACT", [lambda e, bi=bi, nn=nn: e.activation(
            asq[0][:, :, 0:nn], abf[bi][0][:, :, 0:nn], AF.Square)],
            reads=[abf[bi][1]], writes=[asq[1]], tag="das%d" % gi)
        stat_mms(2, 3, lambda c, bi=bi, nn=nn: abf[bi][0][:, c, 0:nn],
                 lambda c, nn=nn: asq[0][:, c, 0:nn],
                 abf[bi][1], asq[1], nn, tag="dsm%d" % gi)

    def d_chain(gi):
        w0, n = DG2[gi]
        nn = n + 1
        bi = gi % 2
        row_math(2, 3, nn, rsbd[bi], msbd[bi], si=0, tag="drm%d" % gi)
        for eng, c0, c1, wrs in (("DVE", 0, 6, (xmid[bi][1],)),
                                 ("POOL", 6, CH, (xmidb[bi],))):
            ems = []
            for c in range(c0, c1):
                ems.append(lambda e, bi=bi, c=c, nn=nn: e.tensor_mul(
                    xmid[bi][0][:, c, 0:nn], agb[bi][0][:, c, 0:nn], rsbd[bi][0][:, 0:nn]))
                ems.append(lambda e, bi=bi, c=c, nn=nn: e.tensor_sub(
                    xmid[bi][0][:, c, 0:nn], xmid[bi][0][:, c, 0:nn], msbd[bi][0][:, 0:nn]))
                ems.append(lambda e, bi=bi, c=c, nn=nn: e.tensor_add(
                    xmid[bi][0][:, c, 0:nn], xmid[bi][0][:, c, 0:nn], xgb[0][:, c, 0:nn]))
            st(eng, ems, reads=[agb[bi][1], rsbd[bi][1], msbd[bi][1], xgb[1]],
               writes=list(wrs), tag="dxm%s%d" % (eng, gi))
        st("ACT", [lambda e, bi=bi, nn=nn: e.activation(
            bb1[0][:, :, 0:nn], xmid[bi][0][:, :, 0:nn], AF.Copy)],
            reads=[xmid[bi][1], xmidb[bi]], writes=[bb1[1]], tag="dxb%d" % gi)
        st("ACT", [lambda e, nn=nn: e.activation(bb2[0][:, :, 0:nn], bb1[0][:, :, 0:nn],
                                                 AF.Square)],
           reads=[bb1[1]], writes=[bb2[1]], tag="dxs%d" % gi)
        stat_mms(2, 3, lambda c, nn=nn: bb1[0][:, c, 0:nn],
                 lambda c, nn=nn: bb2[0][:, c, 0:nn],
                 bb1[1], bb2[1], nn, tag="dsm2%d" % gi)
        row_math(2, 3, nn, rsbd[bi], msbd[bi], si=0, tag="drm2%d" % gi)
        if gi == 0:
            st("DVE", [
                lambda e: e.tensor_mul(rsbd[0][0][:, 0:1], rsbd[0][0][:, 0:1],
                                       vec["cmask"][0][:, 0:1]),
                lambda e: e.tensor_mul(msbd[0][0][:, 0:1], msbd[0][0][:, 0:1],
                                       vec["cmask"][0][:, 0:1]),
            ], reads=[vec["cmask"][1]], writes=[rsbd[0][1], msbd[0][1]], tag="dmask")
        ems = []
        for c in range(CH):
            ems.append(lambda e, bi=bi, c=c, nn=nn: e.tensor_mul(
                ln2[bi][0][:, c, 0:nn], xmid[bi][0][:, c, 0:nn], rsbd[bi][0][:, 0:nn]))
            ems.append(lambda e, bi=bi, c=c, nn=nn: e.tensor_sub(
                ln2[bi][0][:, c, 0:nn], ln2[bi][0][:, c, 0:nn], msbd[bi][0][:, 0:nn]))
        st("DVE", ems, reads=[xmid[bi][1], xmidb[bi], rsbd[bi][1], msbd[bi][1]],
           writes=[ln2[bi][1], ln2b[bi]], tag="dln2%d" % gi)
        st("DVE", [lambda e, bi=bi, c=c, n=n: e.tensor_sub(
            dif[bi][0][:, c, 0:n], ln2[bi][0][:, c, 1:n + 1], ln2[bi][0][:, c, 0:n])
            for c in range(CH)],
            reads=[ln2[bi][1], ln2b[bi]], writes=[dif[bi][1]], tag="ddif%d" % gi)
        ems = []
        for c in range(CH):
            ems.append(lambda e, bi=bi, c=c, n=n: e.scalar_tensor_tensor(
                xrb[bi][0][:, c, 0:n], dif[bi][0][:, c, 0:n], vec["mur"][0][:, c:c + 1],
                ln2[bi][0][:, c, 0:n], ALU.mult, ALU.add))
            ems.append(lambda e, bi=bi, c=c, n=n: e.scalar_tensor_tensor(
                xkb[bi][0][:, c, 0:n], dif[bi][0][:, c, 0:n], vec["muk"][0][:, c:c + 1],
                ln2[bi][0][:, c, 0:n], ALU.mult, ALU.add))
        st("DVE", ems, reads=[ln2[bi][1], ln2b[bi], dif[bi][1],
                              vec["mur"][1], vec["muk"][1]],
           writes=[xrb[bi][1], xkb[bi][1]], tag="dxrk%d" % gi)

    def d_ffn_a(gi):
        w0, n = DG2[gi]
        bi = gi % 2
        for dt in range(CH):
            bidx = dt % 2
            ems = [lambda e, c=c, dt=dt, bidx=bidx, bi=bi, n=n: e.matmul(
                PS[bidx][0][:, 0:n], wr_sb[0][:, c, dt * 128:(dt + 1) * 128],
                xrb[bi][0][:, c, 0:n], start=(c == 0), stop=(c == CH - 1),
                skip_group_check=True) for c in range(CH)]
            st("PE", ems, reads=[wrr[3], xrb[bi][1]], writes=[PS[bidx][1]],
               tag="dwr%d_%d" % (gi, dt))
            st("ACT", [lambda e, bidx=bidx, dt=dt, n=n: e.activation(
                rb_[0][:, dt, 0:n], PS[bidx][0][:, 0:n], AF.Sigmoid)],
                reads=[PS[bidx][1]], writes=[rb_[1]], tag="drs%d_%d" % (gi, dt))
        for dt in range(CH):
            bidx = dt % 2
            ems = [lambda e, c=c, dt=dt, bidx=bidx, bi=bi, n=n: e.matmul(
                PS[bidx][0][:, 0:n], wk2_sb[0][:, c, dt * 128:(dt + 1) * 128],
                xkb[bi][0][:, c, 0:n], start=(c == 0), stop=(c == CH - 1),
                skip_group_check=True) for c in range(CH)]
            st("PE", ems, reads=[wkr[3], xkb[bi][1]], writes=[PS[bidx][1]],
               tag="dwk%d_%d" % (gi, dt))
            st("ACT", [lambda e, bidx=bidx, dt=dt, n=n: e.activation(
                scr2[0][:, dt, 0:n], PS[bidx][0][:, 0:n], AF.Relu)],
                reads=[PS[bidx][1]], writes=[scr2[1]], tag="dkr%d_%d" % (gi, dt))
        st("ACT", [lambda e, bi=bi, n=n: e.activation(
            kab[bi][0][:, :, 0:n], scr2[0][:, :, 0:n], AF.Square)],
            reads=[scr2[1]], writes=[kab[bi][1]], tag="dka%d" % gi)

    def d_ffn_b(gi):
        w0, n = DG2[gi]
        bi = gi % 2
        for dt in range(CH):
            bidx = dt % 2
            ems = [lambda e, c=c, dt=dt, bidx=bidx, bi=bi, n=n: e.matmul(
                PS[bidx][0][:, 0:n], wv2_sb[0][:, c, dt * 128:(dt + 1) * 128],
                kab[bi][0][:, c, 0:n], start=(c == 0), stop=(c == CH - 1),
                skip_group_check=True) for c in range(CH)]
            st("PE", ems, reads=[wvr[3], kab[bi][1]], writes=[PS[bidx][1]],
               tag="dwv%d_%d" % (gi, dt))
            st("ACT", [lambda e, bidx=bidx, dt=dt, n=n: e.activation(
                ff_[dt % 2][0][:, 0:n], PS[bidx][0][:, 0:n], AF.Copy)],
                reads=[PS[bidx][1]], writes=[ff_[dt % 2][1]], tag="dfe%d_%d" % (gi, dt))
            st("DVE", [lambda e, dt=dt, n=n: e.tensor_mul(
                scr2[0][:, dt, 0:n], ff_[dt % 2][0][:, 0:n], rb_[0][:, dt, 0:n])],
                reads=[ff_[dt % 2][1], rb_[1]], writes=[scr2[1]], tag="df%d_%d" % (gi, dt))
        st("ACT", [lambda e, bi=bi, n=n: e.activation(
            bbf1[bi][0][:, :, 0:n], scr2[0][:, :, 0:n], AF.Copy)],
            reads=[scr2[1]], writes=[bbf1[bi][1]], tag="dfb%d" % gi)
        st("ACT", [lambda e, bi=bi, n=n: e.activation(
            bbf2[bi][0][:, :, 0:n], bbf1[bi][0][:, :, 0:n], AF.Square)],
            reads=[bbf1[bi][1]], writes=[bbf2[bi][1]], tag="dfs%d" % gi)
        stat_mms(4, 5, lambda c, bi=bi, n=n: bbf1[bi][0][:, c, 0:n],
                 lambda c, bi=bi, n=n: bbf2[bi][0][:, c, 0:n],
                 bbf1[bi][1], bbf2[bi][1], n, tag="dsm3%d" % gi)
        row_math(4, 5, n, rsbdF, msbdF, si=2, tag="drm3%d" % gi)
        ems = []
        for c in range(CH):
            ems.append(lambda e, c=c, n=n: e.tensor_mul(
                scr2[0][:, c, 0:n], scr2[0][:, c, 0:n], rsbdF[0][:, 0:n]))
            ems.append(lambda e, c=c, n=n: e.tensor_sub(
                scr2[0][:, c, 0:n], scr2[0][:, c, 0:n], msbdF[0][:, 0:n]))
            ems.append(lambda e, bi=bi, c=c, n=n: e.tensor_add(
                scr2[0][:, c, 0:n], scr2[0][:, c, 0:n], xmid[bi][0][:, c, 1:1 + n]))
        st("DVE", ems, reads=[rsbdF[1], msbdF[1], xmid[bi][1], xmidb[bi]],
           writes=[scr2[1]], tag="dout%d" % gi)
        st("SP", [lambda e, w0=w0, n=n: e.dma_start(
            out=outT[:, w0 - 1:w0 - 1 + n].rearrange("(c p) n -> p c n", p=128),
            in_=scr2[0][:, :, 0:n])], reads=[scr2[1]], dma=True, tag="dst%d" % gi)

    d_head(0)
    d_chain(0)
    for gi in range(len(DG2)):
        if gi + 1 < len(DG2):
            d_head(gi + 1)
        d_ffn_a(gi)
        if gi + 1 < len(DG2):
            d_chain(gi + 1)
        d_ffn_b(gi)

    st("SP", [lambda e: e.nop()], extra=[("qSP", B.cnt["qSP"])], tag="fin")

    nc2 = _replay(nc, B)
    for g in reversed(semg_dummy) if False else []:
        pass
    for g in reversed(dg):
        g.__exit__(None, None, None)
    for g in reversed(guards):
        g.__exit__(None, None, None)
    return nc2


def _replay(nc, B):
    sems = {}
    semg = []
    with nc.Block() as block:
        for name in ("sPE", "sACT", "sDVE", "sPOOL", "sSP", "qSP", "qACT", "qGP", "qWA", "qWB", "qW1", "qW2", "qWr", "qWk2", "qWv2"):
            g = nc.semaphore(name)
            sems[name] = g.__enter__()
            semg.append(g)
        eng_map = {"PE": block.tensor, "ACT": block.scalar, "DVE": block.vector,
                   "POOL": block.gpsimd, "SP": block.sync}

        def mk(eng_key):
            def body(e):
                for (eng, waits, emits, sem, dma, tag) in B.ops:
                    if eng != eng_key:
                        continue
                    for sname, v in sorted(waits.items()):
                        e.wait_ge(sems[sname], v)
                    last = None
                    for fn in emits:
                        inst = fn(e)
                        if dma:
                            inst.then_inc(sems[sem], 16)
                        else:
                            last = inst
                    if not dma and last is not None:
                        last.then_inc(sems[sem], 1)
            return body

        for k, dec in eng_map.items():
            dec(mk(k))
    for g in reversed(semg):
        g.__exit__(None, None, None)
    return nc


def _stage_inputs(inputs):
    f32 = np.float32
    x = np.asarray(inputs["L0_attn_src"], f32)
    col = lambda v: np.ascontiguousarray(np.asarray(v, f32).reshape(-1, 128).T)
    shared = {
        "wq": np.ascontiguousarray(np.asarray(inputs["L0_attn_Wq"], f32)),
        "wk": np.ascontiguousarray(np.asarray(inputs["L0_attn_Wk"], f32)),
        "wv": np.ascontiguousarray(np.asarray(inputs["L0_attn_Wv"], f32)),
        "wo": np.ascontiguousarray(np.asarray(inputs["L0_attn_Wo"], f32)),
        "w1": np.ascontiguousarray(np.asarray(inputs["L0_attn_W1"], f32)),
        "w2": np.ascontiguousarray(np.asarray(inputs["L0_attn_W2"], f32)),
        "wr": np.ascontiguousarray(np.asarray(inputs["L0_ffn_W_r"], f32).T),
        "wk2": np.ascontiguousarray(np.asarray(inputs["L0_ffn_W_k"], f32).T),
        "wv2": np.ascontiguousarray(np.asarray(inputs["L0_ffn_W_v"], f32).T),
        "bq": col(inputs["L0_attn_bq"]),
        "bo": col(inputs["L0_attn_bo"]),
        "b2": col(inputs["L0_attn_b2"]),
        "g1": col(inputs["L0_attn_g1"]),
        "be1": col(inputs["L0_attn_be1"]),
        "g2": col(inputs["L0_attn_g2"]),
        "be2": col(inputs["L0_attn_be2"]),
        "mur": col(inputs["L0_ffn_mu_r"]),
        "muk": col(inputs["L0_ffn_mu_k"]),
        "b1c": col(inputs["L0_attn_b1"]),
        "bkrow": np.ascontiguousarray(np.asarray(inputs["L0_attn_bk"], f32)[None, :]),
        "bvrow": np.ascontiguousarray(np.asarray(inputs["L0_attn_bv"], f32)[None, :]),
        "ident": np.eye(128, dtype=f32),
    }
    zpat = np.zeros((32, PAIRS, 128), f32)
    for p in range(PAIRS):
        for hl in range(2):
            zpat[hl * 16 + 2 * p, p, 0:64] = 1.0
            zpat[hl * 16 + 2 * p + 1, p, 64:128] = 1.0
    shared["zpat"] = zpat
    masks = [np.zeros((128, CH), f32), np.ones((128, CH), f32)]
    xTb = [np.ascontiguousarray(
        np.concatenate([np.zeros((D, 1), f32), x[b].T], axis=1)) for b in range(4)]
    in_maps = []
    for c in range(8):
        b, h = c // 2, c % 2
        m = dict(shared)
        m["xT"] = xTb[b]
        m["xTo"] = np.ascontiguousarray(xTb[b][:, h * 2048:h * 2048 + OWN])
        m["cmask"] = masks[h]
        in_maps.append(m)
    return in_maps


def kernel(**inputs):
    in_maps = _stage_inputs(inputs)
    nc = build_program(taps=0)
    res = run_bass_kernel_spmd(nc, in_maps, core_ids=list(range(8)))
    out = np.empty((4, 4096, 1024), np.float32)
    for c in range(8):
        b, h = c // 2, c % 2
        out[b, h * 2048:(h + 1) * 2048, :] = res.results[c]["outT"].T
    return out
